# version 05
# baseline (speedup 1.0000x reference)
import contextlib
import numpy as np
import ml_dtypes
import concourse.bass as bass
import concourse.mybir as mybir
from concourse.bass_utils import run_bass_kernel_spmd

F32 = mybir.dt.float32
BF16 = mybir.dt.bfloat16
AF = mybir.ActivationFunctionType
ALU = mybir.AluOpType
AX = mybir.AxisListType

T = 2048
D = 1024
BFnp = ml_dtypes.bfloat16

NEG_BIG = -1.0e30


class Prog:

    def __init__(self):
        self.nc = bass.Bass("TRN2", target_bir_lowering=False)
        self.ctx = contextlib.ExitStack()
        self.sems = {}
        self.cnt = {}
        self._bar_n = 0
        self.block = None
        self.io = {}
        self.prefix = ""

    def phase(self, prefix, io):
        if self.block is not None:
            self.barrier()
        for k in [k for k in self.cnt if k.startswith("_")]:
            del self.cnt[k]
        self.prefix = prefix
        self.io = io or {}

    def sem(self, name):
        h = self.ctx.enter_context(self.nc.semaphore(name))
        self.sems[name] = h
        self.cnt[name] = 0
        return h

    def sbuf(self, name, shape, dtype):
        if name in self.io:
            return self.io[name]
        return self.ctx.enter_context(self.nc.sbuf_tensor(self.prefix + name, shape, dtype))

    def psum(self, name, shape, dtype):
        if name in self.io:
            return self.io[name]
        return self.ctx.enter_context(self.nc.psum_tensor(self.prefix + name, shape, dtype))

    def dram(self, name, shape, dtype, kind="Internal"):
        if name in self.io:
            return self.io[name]
        return self.nc.dram_tensor(self.prefix + name, shape, dtype, kind=kind).ap()

    def begin(self, consts=()):
        for s in ("LD", "GD", "SD", "ST", "PE", "DV", "AC", "PL", "BAR"):
            self.sem(s)
        self._constvals = list(consts)
        self._constmap = {v: i for i, v in enumerate(self._constvals)}
        if self._constvals:
            self.const_tile = self.sbuf("consts", [128, len(self._constvals)], F32)
        self.block = self.ctx.enter_context(self.nc.Block())
        if self._constvals:
            def _(v):
                ins = None
                for i, val in enumerate(self._constvals):
                    ins = v.memset(self.const_tile[:, i:i + 1], float(val))
                ins.then_inc(self.sems["DV"], 1)
            self.vector(_)
            self.inc("DV")
            self.barrier()

    def C(self, val):
        return self.const_tile[:, self._constmap[val]:self._constmap[val] + 1]

    def end(self):
        self.ctx.close()

    class _EngProxy:
        def __init__(self, eng, prog, queues):
            self._e = eng
            self._p = prog
            self._queues = queues
            self._flushed = False

        def dma_start(self, *a, **kw):
            if not self._flushed:
                for q in self._queues:
                    if self._p.cnt[q] > 0:
                        self._e.wait_ge(self._p.sems[q], self._p.cnt[q])
                self._flushed = True
            return self._e.dma_start(*a, **kw)

        def __getattr__(self, k):
            return getattr(self._e, k)

    def tensor(self, f):
        self.block.tensor(f)

    def vector(self, f):
        self.block.vector(f)

    def scalar(self, f):
        self.block.scalar(lambda e: f(Prog._EngProxy(e, self, ("SD",))))

    def gpsimd(self, f):
        self.block.gpsimd(lambda e: f(Prog._EngProxy(e, self, ("GD",))))

    def sync(self, f):
        self.block.sync(lambda e: f(Prog._EngProxy(e, self, ("LD", "ST"))))

    def inc(self, name, by=1):
        self.cnt[name] += by
        return self.cnt[name]

    def val(self, name):
        return self.cnt[name]

    def tick(self, ins, name, by=1):
        ins.then_inc(self.sems[name], by)
        return self.inc(name, by)

    def fence(self, eng, ins, name):
        t = self.tick(ins, name)
        eng.wait_ge(self.sems[name], t)
        return t

    def barrier(self):
        self._bar_n += 1
        tgt = 5 * self._bar_n

        def mk(own):
            def f(eng):
                for nm in own:
                    if self.cnt[nm] > 0:
                        eng.wait_ge(self.sems[nm], self.cnt[nm])
                eng.sem_inc(self.sems["BAR"], 1)
                eng.wait_ge(self.sems["BAR"], tgt)
            return f

        self.tensor(mk(("PE",)))
        self.vector(mk(("DV",)))
        self.scalar(mk(("AC", "SD")))
        self.gpsimd(mk(("PL", "GD")))
        self.sync(mk(("LD", "ST")))


def host_constants():
    c = {}
    c["ident"] = np.eye(128, dtype=np.float32)
    m = np.zeros((128, 128), np.float32)
    m[np.triu_indices(128, 1)] = NEG_BIG
    c["mask_diag"] = m
    s = np.arange(384)[None, :]
    r = np.arange(128)[:, None]
    band = (s >= r) & (s <= r + 256)
    gen = np.where(band, 0.0, NEG_BIG).astype(np.float32)
    c["mask_band"] = gen
    m0 = gen.copy()
    m0[:, :256] = NEG_BIG
    c["mask_band_t0"] = m0
    m1 = gen.copy()
    m1[:, :128] = NEG_BIG
    c["mask_band_t1"] = m1
    c["iota_t"] = np.arange(T, dtype=np.float32)[None, :]
    c["iota_i"] = np.arange(32, dtype=np.float32)[:, None]
    sel = np.zeros((32, 32 * 128), np.float32)
    for n in range(32):
        sel[n, n * 128:(n + 1) * 128] = 1.0
    c["sel"] = sel

    def expand(nb, w):
        e = np.zeros((nb, w), np.float32)
        for b in range(nb):
            e[b, b * 16:(b + 1) * 16] = 1.0
        return e
    c["E64"] = expand(64, 1024)
    c["E32"] = expand(32, 512)
    c["E44"] = expand(44, 704)
    bd = np.zeros((128, 2), np.float32)
    bd[0:64, 0] = 1.0
    bd[64:128, 1] = 1.0
    c["bdiag"] = bd
    c["bdiagT"] = np.ascontiguousarray(bd.T)
    c["ones1"] = np.ones((1, 128), np.float32)
    dup = np.zeros((32, 128), np.float32)
    for pp in range(128):
        dup[pp % 32, pp] = 1.0
    c["dup32"] = dup
    return c


def prep_E(inputs, c):
    qs = inputs["L1_ffn_q_scale"]
    ks = inputs["L1_ffn_k_scale"]
    lng = inputs["L1_ffn_ln_g"]
    lnb = inputs["L1_ffn_ln_b"]
    b1 = inputs["L1_ffn_b1"]
    b2 = inputs["L1_ffn_b2"]
    d = {
        "Wqkv": inputs["L1_ffn_Wqkv"],
        "Wo": inputs["L1_ffn_Wo"],
        "W1": inputs["L1_ffn_W1"],
        "W2": inputs["L1_ffn_W2"],
        "q_scale_r": np.ascontiguousarray(np.tile(qs, (1, 2)).T),
        "k_scale_r": np.ascontiguousarray(np.tile(ks, (1, 2)).T),
        "ln_g_r": np.ascontiguousarray(lng.reshape(6, 8, 128).transpose(2, 0, 1).reshape(128, 48)),
        "ln_b_r": np.ascontiguousarray(lnb.reshape(6, 8, 128).transpose(2, 0, 1).reshape(128, 48)),
        "b2_r": np.ascontiguousarray(b2.reshape(6, 8, 128).transpose(2, 0, 1).reshape(128, 48)),
        "ident": c["ident"], "mask_band": c["mask_band"],
        "mask_band_t0": c["mask_band_t0"], "mask_band_t1": c["mask_band_t1"],
        "iota_t": c["iota_t"], "iota_i": c["iota_i"], "dup32": c["dup32"],
        "bdiag": c["bdiag"], "bdiagT": c["bdiagT"], "ones1": c["ones1"],
    }
    b1r = np.zeros((128, 6 * 44), np.float32)
    for i in range(6):
        for j in range(22):
            w = min(128, 2730 - j * 128)
            b1r[0:w, i * 44 + j] = b1[i, j * 128:j * 128 + w]
            b1r[0:w, i * 44 + 22 + j] = b1[i, 2730 + j * 128:2730 + j * 128 + w]
    d["b1_r"] = b1r
    return d


def build_A(p=None, io=None):
    standalone = p is None
    if standalone:
        p = Prog()
    else:
        p.phase("A_", io)
    x_in = p.dram("x", [T, D], F32, kind="ExternalInput")
    ident_in = p.dram("ident", [128, 128], F32, kind="ExternalInput")
    x_fm_out = p.dram("x_fm", [D, T], F32, kind="ExternalOutput")
    xn_bf_out = p.dram("xn_bf", [D, T], F32, kind="ExternalOutput")
    xn_tm_out = p.dram("xn_tm", [T, D], F32, kind="ExternalOutput")

    NB = 3
    stg = [p.sbuf(f"stg{i}", [128, D], F32) for i in range(NB)]
    xn_tm = [p.sbuf(f"xntm{i}", [128, D], F32) for i in range(NB)]
    evac = [p.sbuf(f"evac{i}", [128, 1024], F32) for i in range(NB)]
    evb = [p.sbuf(f"evb{i}", [128, 1024], F32) for i in range(NB)]
    idf = p.sbuf("idf", [128, 128], F32)
    stats = [p.sbuf(f"stats{i}", [128, 6], F32) for i in range(NB)]

    pA = p.psum("pA", [128, 2048], F32)

    if standalone:
        p.begin(consts=[1e-5])

    def _(sy):
        sy.dma_start(out=idf[:], in_=ident_in).then_inc(p.sems["LD"], 16)
    p.sync(_)
    p.inc("LD", 16)

    NT = T // 128
    fin_pe = {}
    fin_xn = {}
    fin_ev = {}
    fin_st = {}

    for tt in range(NT):
        b = tt % NB
        prev = tt - NB

        def _(sy, tt=tt, b=b, prev=prev):
            if prev >= 0:
                sy.wait_ge(p.sems["PE"], fin_pe[prev])
                sy.wait_ge(p.sems["DV"], fin_xn[prev])
            sy.dma_start(out=stg[b][:], in_=x_in[tt * 128:(tt + 1) * 128, :]).then_inc(p.sems["LD"], 16)
        p.sync(_)
        ld_t = p.inc("LD", 16)

        def _(s, b=b, ld_t=ld_t, prev=prev):
            s.wait_ge(p.sems["LD"], ld_t)
            if prev >= 0:
                s.wait_ge(p.sems["ST"], fin_st[prev])
                s.wait_ge(p.sems["PE"], fin_pe[prev])
            s.activation(xn_tm[b][:], stg[b][:], AF.Square,
                         accum_out=stats[b][:, 0:1]).then_inc(p.sems["AC"], 1)
        p.scalar(_)
        ac_t = p.inc("AC")

        def _(v, b=b, ac_t=ac_t, ld_t=ld_t):
            st = stats[b]
            v.wait_ge(p.sems["LD"], ld_t)
            v.wait_ge(p.sems["AC"], ac_t)
            p.fence(v, v.reduce_sum(st[:, 1:2], stg[b][:], axis=AX.X), "DV")
            p.fence(v, v.tensor_scalar_mul(st[:, 1:2], st[:, 1:2], 1.0 / D), "DV")
            p.fence(v, v.tensor_tensor(st[:, 2:3], st[:, 1:2], st[:, 1:2], ALU.mult), "DV")
            p.tick(v.scalar_tensor_tensor(st[:, 3:4], st[:, 0:1], 1.0 / D, st[:, 2:3],
                                          ALU.mult, ALU.subtract), "DV")
        p.vector(_)
        dv_stats = p.val("DV")

        def _(s, b=b, dv_stats=dv_stats):
            st = stats[b]
            s.wait_ge(p.sems["DV"], dv_stats)
            p.fence(s, s.activation(st[:, 4:5], st[:, 3:4], AF.Ln, bias=p.C(1e-5)), "AC")
            p.tick(s.activation(st[:, 5:6], st[:, 4:5], AF.Exp, scale=-0.5), "AC")
        p.scalar(_)
        ac_t2 = p.val("AC")

        def _(v, b=b, ac_t2=ac_t2):
            st = stats[b]
            v.wait_ge(p.sems["AC"], ac_t2)
            v.tensor_scalar(xn_tm[b][:], stg[b][:], st[:, 1:2], st[:, 5:6],
                            ALU.subtract, ALU.mult).then_inc(p.sems["DV"], 1)
        p.vector(_)
        fin_xn[tt] = p.inc("DV")

        def _(t, b=b, tt=tt, ld_t=ld_t):
            t.wait_ge(p.sems["LD"], ld_t)
            if tt >= 1:
                t.wait_ge(p.sems["DV"], fin_ev[tt - 1])
            for c in range(8):
                t.transpose(pA[:, c * 128:(c + 1) * 128], stg[b][:, c * 128:(c + 1) * 128], idf[:])
            t.wait_ge(p.sems["DV"], fin_xn[tt])
            ins = None
            for c in range(8):
                ins = t.transpose(pA[:, 1024 + c * 128:1024 + (c + 1) * 128],
                                  xn_tm[b][:, c * 128:(c + 1) * 128], idf[:])
            ins.then_inc(p.sems["PE"], 1)
        p.tensor(_)
        fin_pe[tt] = p.inc("PE")

        def _(v, b=b, tt=tt, prev=prev):
            v.wait_ge(p.sems["PE"], fin_pe[tt])
            if prev >= 0:
                v.wait_ge(p.sems["ST"], fin_st[prev])
            v.tensor_copy(evac[b][:], pA[:, 0:1024])
            v.tensor_copy(evb[b][:], pA[:, 1024:2048]).then_inc(p.sems["DV"], 2)
        p.vector(_)
        fin_ev[tt] = p.inc("DV", 2)

        def _(sy, b=b, tt=tt):
            sy.wait_ge(p.sems["DV"], fin_xn[tt])
            sy.dma_start(out=xn_tm_out[tt * 128:(tt + 1) * 128, :],
                         in_=xn_tm[b][:]).then_inc(p.sems["ST"], 16)
            sy.wait_ge(p.sems["DV"], fin_ev[tt])
            sy.dma_start(out=x_fm_out.rearrange("(c p) t -> p c t", p=128)[:, :, tt * 128:(tt + 1) * 128],
                         in_=evac[b].rearrange("p (c q) -> p c q", q=128)).then_inc(p.sems["ST"], 16)
            sy.dma_start(out=xn_bf_out.rearrange("(c p) t -> p c t", p=128)[:, :, tt * 128:(tt + 1) * 128],
                         in_=evb[b].rearrange("p (c q) -> p c q", q=128)).then_inc(p.sems["ST"], 16)
        p.sync(_)
        fin_st[tt] = p.inc("ST", 48)

    def _(sy):
        sy.wait_ge(p.sems["ST"], p.val("ST"))
    p.sync(_)

    if standalone:
        p.end()
        return p.nc
    return None


def build_B(debug=False, p=None, io=None):
    standalone = p is None
    if standalone:
        p = Prog()
    else:
        p.phase("B_", io)
    xn_in = p.dram("xn_bf", [D, T], F32, kind="ExternalInput")
    ipw = p.dram("in_proj_w", [4096, 1024], F32, kind="ExternalInput")
    cw = p.dram("conv_w", [2048, 1, 4], F32, kind="ExternalInput")
    cb = p.dram("conv_b", [128, 16], F32, kind="ExternalInput")
    xpw = p.dram("x_proj_w", [96, 2048], F32, kind="ExternalInput")
    dtw = p.dram("dt_proj_w", [2048, 64], F32, kind="ExternalInput")
    dtb = p.dram("dt_proj_b", [128, 16], F32, kind="ExternalInput")
    alog = p.dram("A_log", [2048, 16], F32, kind="ExternalInput")
    dpar = p.dram("D_par", [128, 16], F32, kind="ExternalInput")
    opw = p.dram("out_proj_w", [1024, 2048], F32, kind="ExternalInput")
    ident_in = p.dram("ident", [128, 128], F32, kind="ExternalInput")
    sel_in = p.dram("sel", [32, 32 * 128], F32, kind="ExternalInput")
    mam_out = p.dram("mam_out", [D, T], F32, kind="ExternalOutput")

    TC = 512
    NTC = T // TC
    if debug:
        dbg_xp = p.dram("dbg_xp", [NTC, 128, 16 * 515], F32, kind="ExternalOutput")
        dbg_z = p.dram("dbg_z", [NTC, 128, 16 * TC], F32, kind="ExternalOutput")
        dbg_xa = p.dram("dbg_xa", [NTC, 128, 16 * TC], F32, kind="ExternalOutput")
        dbg_dl = p.dram("dbg_dl", [NTC, 64, TC], F32, kind="ExternalOutput")
        dbg_bc = p.dram("dbg_bc", [NTC, 32, TC], F32, kind="ExternalOutput")
        dbg_br = p.dram("dbg_br", [NTC, 128, 16 * TC], F32, kind="ExternalOutput")
        dbg_cr = p.dram("dbg_cr", [NTC, 128, 16 * TC], F32, kind="ExternalOutput")
        dbg_y = p.dram("dbg_y", [NTC, 128, 16 * TC], F32, kind="ExternalOutput")

    xnb = p.sbuf("xnb", [128, 8 * T], BF16)
    opT = p.sbuf("opT", [128, 16 * 1024], BF16)
    xpjT = p.sbuf("xpjT", [128, 16 * 96], BF16)
    dtT = p.sbuf("dtT", [64, 16 * 128], BF16)
    selb = p.sbuf("selb", [32, 32 * 128], BF16)
    id_b = p.sbuf("id_b", [128, 128], BF16)
    convw = p.sbuf("convw", [128, 16 * 4], F32)
    convb = p.sbuf("convb", [128, 16], F32)
    dtbc = p.sbuf("dtbc", [128, 16], F32)
    dcol = p.sbuf("dcol", [128, 16], F32)
    a_sb = p.sbuf("a_sb", [128, 256], F32)
    hstate = p.sbuf("hstate", [128, 256], F32)
    arena = p.sbuf("arena", [128, 24576], BF16)
    stage = [arena[:, 0:2048], arena[:, 2048:4096]]
    ybf = arena[:, 0:8192]
    xp = arena[:, 8192:16432]
    brep = arena[:, 8192:16384]
    crep = arena[:, 16384:24576]
    tails = p.sbuf("tails", [128, 48], BF16)
    ipT = [p.sbuf(f"ipT{i}", [128, 1024], BF16) for i in range(2)]
    z_bf = p.sbuf("z_bf", [128, 16 * TC], BF16)
    xa_bf = p.sbuf("xa_bf", [128, 16 * TC], BF16)
    xc = [p.sbuf(f"xc{i}", [128, TC], F32) for i in range(2)]
    dl_bf = p.sbuf("dl_bf", [64, TC], BF16)
    bc_bf = p.sbuf("bc_bf", [32, TC], BF16)
    v0 = p.sbuf("v0", [128, TC], F32)
    sp_ab = p.sbuf("sp_ab", [128, TC], F32)
    sp_l = p.sbuf("sp_l", [128, TC], F32)
    delta_dc = p.sbuf("delta_dc", [128, TC], F32)
    u_dc = p.sbuf("u_dc", [128, TC], BF16)
    dA = [p.sbuf(f"dA{i}", [128, TC], F32) for i in range(2)]
    dBu = [p.sbuf(f"dBu{i}", [128, TC], F32) for i in range(2)]
    hh = p.sbuf("hh", [128, TC], F32)
    ytmp = p.sbuf("ytmp", [128, TC], F32)
    yacc = p.sbuf("yacc", [128, TC], F32)
    sz = ytmp
    ostage = [p.sbuf("ostage0", [128, TC], F32)]

    pA = p.psum("pA", [128, 2048], F32)
    pB = p.psum("pB", [128, 1024], F32)
    pb0 = p.psum("pb0", [128, 1024], BF16)
    pb1 = p.psum("pb1", [128, 1024], BF16)

    if standalone:
        p.begin(consts=[1.0])

    LD, GD, ST, PE, DV, AC = (p.sems[k] for k in ("LD", "GD", "ST", "PE", "DV", "AC"))

    def _(g):
        g.dma_start(out=xnb.rearrange("p (c t) -> p c t", t=T),
                    in_=xn_in.rearrange("(c p) t -> p c t", p=128)).then_inc(GD, 16)
    p.gpsimd(_)
    p.inc("GD", 16)

    def _(sy):
        sy.dma_start(out=convw.rearrange("p (c j) -> p c j", j=4),
                     in_=cw.rearrange("(c p) one j -> p c (one j)", p=128)).then_inc(LD, 16)
        sy.dma_start(out=convb[:], in_=cb).then_inc(LD, 16)
        sy.dma_start(out=dtbc[:], in_=dtb).then_inc(LD, 16)
        sy.dma_start(out=dcol[:], in_=dpar).then_inc(LD, 16)
        sy.dma_start(out=a_sb.rearrange("p (c n) -> p c n", n=16),
                     in_=alog.rearrange("(c p) n -> p c n", p=128)).then_inc(LD, 16)
    p.sync(_)
    p.inc("LD", 16 * 5)

    def _(g):
        g.dma_start(out=id_b[:], in_=ident_in).then_inc(GD, 16)
        g.dma_start(out=selb[:], in_=sel_in).then_inc(GD, 16)
        g.dma_start(out=stage[0][0:96, :], in_=xpw).then_inc(GD, 16)
        g.dma_start(out=stage[1][:, 0:1024].rearrange("p (c r) -> p c r", r=64),
                    in_=dtw.rearrange("(c p) r -> p c r", p=128)).then_inc(GD, 16)
    p.gpsimd(_)
    p.inc("GD", 16 * 4)

    def _(s):
        s.wait_ge(LD, p.val("LD"))
        p.tick(s.activation(a_sb[:], a_sb[:], AF.Exp), "AC")
    p.scalar(_)

    def _(v):
        v.wait_ge(AC, p.val("AC"))
        p.tick(v.tensor_scalar_mul(a_sb[:], a_sb[:], -1.0), "DV")
        p.tick(v.memset(hstate[:], 0.0), "DV")
        p.tick(v.memset(tails[:], 0.0), "DV")
    p.vector(_)

    def _(t):
        t.wait_ge(GD, p.val("GD"))
        ins = None
        for kc in range(16):
            ins = t.transpose(pb0[:, kc * 96:(kc + 1) * 96] if kc < 10 else pb1[:, (kc - 10) * 96:(kc - 9) * 96],
                              stage[0][0:96, kc * 128:(kc + 1) * 128], id_b[0:96, 0:96])
        p.tick(ins, "PE")
    p.tensor(_)
    pe_xpj = p.val("PE")

    def _(v):
        v.wait_ge(PE, pe_xpj)
        for kc in range(16):
            src = pb0[:, kc * 96:(kc + 1) * 96] if kc < 10 else pb1[:, (kc - 10) * 96:(kc - 9) * 96]
            ins = v.tensor_copy(xpjT[:, kc * 96:(kc + 1) * 96], src)
        p.tick(ins, "DV")
    p.vector(_)
    dv_xpj = p.val("DV")

    def _(t):
        t.wait_ge(DV, dv_xpj)
        ins = None
        for c in range(16):
            ins = t.transpose(pb0[0:64, c * 128:(c + 1) * 128] if c < 8 else pb1[0:64, (c - 8) * 128:(c - 7) * 128],
                              stage[1][:, c * 64:(c + 1) * 64], id_b[:])
        p.tick(ins, "PE")
    p.tensor(_)
    pe_dtt = p.val("PE")

    def _(v):
        v.wait_ge(PE, pe_dtt)
        ins = None
        for c in range(16):
            src = pb0[0:64, c * 128:(c + 1) * 128] if c < 8 else pb1[0:64, (c - 8) * 128:(c - 7) * 128]
            ins = v.tensor_copy(dtT[:, c * 128:(c + 1) * 128], src)
        p.tick(ins, "DV")
    p.vector(_)
    p.cnt["_dttev"] = p.val("DV")

    pe_pre = p.val("PE")
    for nc8 in range(8):
        b = nc8 % 2
        def _(g, nc8=nc8, b=b):
            if nc8 >= 2:
                g.wait_ge(PE, p.cnt["_optr%d" % (nc8 - 2)])
            else:
                g.wait_ge(PE, pe_pre)
            g.dma_start(out=stage[b][:], in_=opw[nc8 * 128:(nc8 + 1) * 128, :]).then_inc(GD, 16)
        p.gpsimd(_)
        gd_t = p.inc("GD", 16)

        def _(t, nc8=nc8, b=b, gd_t=gd_t):
            t.wait_ge(GD, gd_t)
            if nc8 >= 1:
                t.wait_ge(DV, p.cnt["_opev%d" % (nc8 - 1)])
            else:
                t.wait_ge(DV, p.cnt["_dttev"])
            ins = None
            for kc in range(16):
                dst = pb0[:, (kc % 8) * 128:(kc % 8 + 1) * 128] if kc < 8 else pb1[:, (kc - 8) * 128:(kc - 7) * 128]
                ins = t.transpose(dst, stage[b][:, kc * 128:(kc + 1) * 128], id_b[:])
            p.tick(ins, "PE")
        p.tensor(_)
        p.cnt["_optr%d" % nc8] = p.val("PE")

        def _(v, nc8=nc8):
            v.wait_ge(PE, p.cnt["_optr%d" % nc8])
            ins = None
            for kc in range(16):
                src = pb0[:, (kc % 8) * 128:(kc % 8 + 1) * 128] if kc < 8 else pb1[:, (kc - 8) * 128:(kc - 7) * 128]
                ins = v.tensor_copy(opT[:, kc * 1024 + nc8 * 128:kc * 1024 + (nc8 + 1) * 128], src)
            p.tick(ins, "DV")
        p.vector(_)
        p.cnt["_opev%d" % nc8] = p.val("DV")

    p.barrier()

    for tc in range(NTC):
        t0 = tc * TC

        for nc in range(32):
            b = nc % 2
            def _(g, nc=nc, b=b, tc=tc):
                if nc >= 2:
                    g.wait_ge(PE, p.cnt["_iptr%d" % (nc - 2)])
                g.dma_start(out=stage[b][:, 0:1024], in_=ipw[nc * 128:(nc + 1) * 128, :]).then_inc(GD, 16)
            p.gpsimd(_)
            gd_t = p.inc("GD", 16)

            def _(t, nc=nc, b=b, gd_t=gd_t):
                t.wait_ge(GD, gd_t)
                if nc >= 1:
                    t.wait_ge(DV, p.cnt["_ipev%d" % (nc - 1)])
                ins = None
                for c in range(8):
                    ins = t.transpose(pb0[:, c * 128:(c + 1) * 128], stage[b][:, c * 128:(c + 1) * 128], id_b[:])
                p.tick(ins, "PE")
            p.tensor(_)
            p.cnt["_iptr%d" % nc] = p.val("PE")

            def _(v, nc=nc, b=b):
                v.wait_ge(PE, p.cnt["_iptr%d" % nc])
                p.tick(v.tensor_copy(ipT[b][:], pb0[:]), "DV")
            p.vector(_)
            p.cnt["_ipev%d" % nc] = p.val("DV")

            half = (nc % 2)
            def _(t, nc=nc, b=b, tc=tc, half=half):
                t.wait_ge(DV, p.cnt["_ipev%d" % nc])
                if nc >= 2:
                    t.wait_ge(DV, p.cnt["_xzev%d" % (nc - 2)])
                ins = None
                for c in range(8):
                    ins = t.matmul(pA[:, half * 512:half * 512 + 512],
                                   ipT[b][:, c * 128:(c + 1) * 128],
                                   xnb[:, c * T + t0:c * T + t0 + TC],
                                   start=(c == 0), stop=(c == 7))
                p.tick(ins, "PE")
            p.tensor(_)
            p.cnt["_ipmm%d" % nc] = p.val("PE")

            def _(v, nc=nc, half=half):
                v.wait_ge(PE, p.cnt["_ipmm%d" % nc])
                src = pA[:, half * 512:half * 512 + 512]
                if nc < 16:
                    v.tensor_copy(xp[:, nc * 515:nc * 515 + 3], tails[:, nc * 3:nc * 3 + 3])
                    ins = v.tensor_copy(xp[:, nc * 515 + 3:nc * 515 + 515], src)
                else:
                    ins = v.tensor_copy(z_bf[:, (nc - 16) * TC:(nc - 15) * TC], src)
                p.tick(ins, "DV")
            p.vector(_)
            p.cnt["_xzev%d" % nc] = p.val("DV")

        p.barrier()

        for dc in range(16):
            o = dc * 515
            def _(v, dc=dc, o=o):
                if dc >= 1:
                    v.wait_ge(AC, p.cnt["_silu%d" % (dc - 1)])
                w = lambda j: convw[:, dc * 4 + j:dc * 4 + j + 1]
                p.fence(v, v.tensor_scalar_mul(xc[0][:], xp[:, o + 0:o + TC], w(0)), "DV")
                p.fence(v, v.scalar_tensor_tensor(xc[1][:], xp[:, o + 1:o + 1 + TC], w(1), xc[0][:], ALU.mult, ALU.add), "DV")
                p.fence(v, v.scalar_tensor_tensor(xc[0][:], xp[:, o + 2:o + 2 + TC], w(2), xc[1][:], ALU.mult, ALU.add), "DV")
                p.fence(v, v.scalar_tensor_tensor(xc[1][:], xp[:, o + 3:o + 3 + TC], w(3), xc[0][:], ALU.mult, ALU.add), "DV")
                p.tick(v.tensor_copy(tails[:, dc * 3:dc * 3 + 3], xp[:, o + TC:o + TC + 3]), "DV")
            p.vector(_)
            dv_conv = p.val("DV")

            def _(s, dc=dc, dv_conv=dv_conv):
                s.wait_ge(DV, dv_conv)
                p.tick(s.activation(xa_bf[:, dc * TC:(dc + 1) * TC], xc[1][:], AF.Silu,
                                    bias=convb[:, dc:dc + 1]), "AC")
            p.scalar(_)
            p.cnt["_silu%d" % dc] = p.val("AC")

        p.barrier()
        if debug:
            def _(g, tc=tc):
                g.dma_start(out=dbg_xp[tc], in_=xp[:]).then_inc(GD, 16)
                g.dma_start(out=dbg_z[tc], in_=z_bf[:]).then_inc(GD, 16)
                g.dma_start(out=dbg_xa[tc], in_=xa_bf[:]).then_inc(GD, 16)
            p.gpsimd(_)
            p.inc("GD", 48)
            p.barrier()

        def _(t):
            ins = None
            for kc in range(16):
                ins = t.matmul(pB[0:96, 0:512], xpjT[:, kc * 96:(kc + 1) * 96],
                               xa_bf[:, kc * TC:(kc + 1) * TC],
                               start=(kc == 0), stop=(kc == 15))
            p.tick(ins, "PE")
        p.tensor(_)
        pe_dbc = p.val("PE")

        def _(v, pe_dbc=pe_dbc):
            v.wait_ge(PE, pe_dbc)
            p.tick(v.tensor_copy(dl_bf[:], pB[0:64, 0:512]), "DV")
            p.tick(v.tensor_copy(bc_bf[:], pB[64:96, 0:512]), "DV")
        p.vector(_)
        dv_bc = p.val("DV")

        for n in range(32):
            bank = n % 2
            def _(t, n=n, bank=bank, dv_bc=dv_bc):
                t.wait_ge(DV, dv_bc)
                if n >= 2:
                    t.wait_ge(DV, p.cnt["_brev%d" % (n - 2)])
                p.tick(t.matmul(pA[:, bank * 512:bank * 512 + 512],
                                selb[:, n * 128:(n + 1) * 128], bc_bf[:],
                                start=True, stop=True), "PE")
            p.tensor(_)
            p.cnt["_brmm%d" % n] = p.val("PE")

            def _(v, n=n, bank=bank):
                v.wait_ge(PE, p.cnt["_brmm%d" % n])
                dst = brep[:, n * TC:(n + 1) * TC] if n < 16 else crep[:, (n - 16) * TC:(n - 15) * TC]
                p.tick(v.tensor_copy(dst, pA[:, bank * 512:bank * 512 + 512]), "DV")
            p.vector(_)
            p.cnt["_brev%d" % n] = p.val("DV")

        p.barrier()
        if debug:
            def _(g, tc=tc):
                g.dma_start(out=dbg_dl[tc], in_=dl_bf[:]).then_inc(GD, 16)
                g.dma_start(out=dbg_bc[tc], in_=bc_bf[:]).then_inc(GD, 16)
                g.dma_start(out=dbg_br[tc], in_=brep[:, 0:16 * TC]).then_inc(GD, 16)
                g.dma_start(out=dbg_cr[tc], in_=crep[:, 0:16 * TC]).then_inc(GD, 16)
            p.gpsimd(_)
            p.inc("GD", 64)
            p.barrier()

        for dc in range(16):
            bank = dc % 2
            def _(t, dc=dc, bank=bank):
                if dc >= 2:
                    t.wait_ge(DV, p.cnt["_v0%d" % (dc - 2)])
                p.tick(t.matmul(pA[:, 1024 + bank * 512:1024 + bank * 512 + 512],
                                dtT[:, dc * 128:(dc + 1) * 128], dl_bf[:],
                                start=True, stop=True), "PE")
            p.tensor(_)
            pe_dp = p.val("PE")

            def _(v, dc=dc, bank=bank, pe_dp=pe_dp):
                v.wait_ge(PE, pe_dp)
                if dc >= 1:
                    v.wait_ge(AC, p.cnt["_acsp%d" % (dc - 1)])
                p.tick(v.tensor_scalar_add(v0[:], pA[:, 1024 + bank * 512:1024 + bank * 512 + 512],
                                           dtbc[:, dc:dc + 1]), "DV")
            p.vector(_)
            p.cnt["_v0%d" % dc] = p.val("DV")
            dv_v0 = p.val("DV")

            def _(s, dv_v0=dv_v0):
                s.wait_ge(DV, dv_v0)
                p.fence(s, s.activation(sp_ab[:], v0[:], AF.Abs), "AC")
                p.fence(s, s.activation(sp_ab[:], sp_ab[:], AF.Exp, scale=-1.0), "AC")
                p.tick(s.activation(sp_l[:], sp_ab[:], AF.Ln, bias=p.C(1.0)), "AC")
            p.scalar(_)
            ac_sp = p.val("AC")
            p.cnt["_acsp%d" % dc] = ac_sp

            def _(v, dc=dc, ac_sp=ac_sp):
                v.wait_ge(DV, p.cnt["_v0%d" % dc])
                p.fence(v, v.tensor_scalar_max(xc[0][:], v0[:], 0.0), "DV")
                v.wait_ge(AC, ac_sp)
                p.fence(v, v.tensor_add(delta_dc[:], xc[0][:], sp_l[:]), "DV")
                p.tick(v.tensor_mul(u_dc[:], delta_dc[:], xa_bf[:, dc * TC:(dc + 1) * TC]), "DV")
            p.vector(_)
            dv_u = p.val("DV")

            for n in range(16):
                nb = n % 2
                def _(s, dc=dc, n=n, nb=nb, dv_u=dv_u):
                    s.wait_ge(DV, dv_u)
                    if n >= 2:
                        s.wait_ge(DV, p.cnt["_scan%d_%d" % (dc, n - 2)])
                    p.tick(s.activation(dA[nb][:], delta_dc[:], AF.Exp,
                                        scale=a_sb[:, dc * 16 + n:dc * 16 + n + 1]), "AC")
                p.scalar(_)
                ac_da = p.val("AC")

                def _(v, dc=dc, n=n, nb=nb, ac_da=ac_da, dv_u=dv_u):
                    v.wait_ge(DV, dv_u)
                    if n >= 2:
                        v.wait_ge(DV, p.cnt["_scan%d_%d" % (dc, n - 2)])
                    p.fence(v, v.tensor_mul(dBu[nb][:], u_dc[:], brep[:, n * TC:(n + 1) * TC]), "DV")
                    v.wait_ge(AC, ac_da)
                    col = dc * 16 + n
                    p.fence(v, v.tensor_tensor_scan(hh[:], dA[nb][:], dBu[nb][:],
                                                    hstate[:, col:col + 1], ALU.mult, ALU.add), "DV")
                    p.cnt["_scan%d_%d" % (dc, n)] = p.val("DV")
                    v.tensor_copy(hstate[:, col:col + 1], hh[:, TC - 1:TC])
                    if n == 0:
                        p.fence(v, v.tensor_mul(yacc[:], hh[:], crep[:, n * TC:(n + 1) * TC]), "DV")
                    else:
                        p.fence(v, v.tensor_mul(ytmp[:], hh[:], crep[:, n * TC:(n + 1) * TC]), "DV")
                        p.fence(v, v.tensor_add(yacc[:], yacc[:], ytmp[:]), "DV")
                p.vector(_)

            def _(s, dc=dc):
                s.wait_ge(DV, p.val("DV"))
                p.tick(s.activation(sz[:], z_bf[:, dc * TC:(dc + 1) * TC], AF.Silu), "AC")
            p.scalar(_)
            ac_sz = p.val("AC")

            def _(v, dc=dc, ac_sz=ac_sz):
                p.fence(v, v.scalar_tensor_tensor(yacc[:], xa_bf[:, dc * TC:(dc + 1) * TC],
                                                  dcol[:, dc:dc + 1], yacc[:], ALU.mult, ALU.add), "DV")
                v.wait_ge(AC, ac_sz)
                p.tick(v.tensor_mul(ybf[:, dc * TC:(dc + 1) * TC], yacc[:], sz[:]), "DV")
            p.vector(_)

        p.barrier()
        if debug:
            def _(g, tc=tc):
                g.dma_start(out=dbg_y[tc], in_=ybf[:, 0:16 * TC]).then_inc(GD, 16)
            p.gpsimd(_)
            p.inc("GD", 16)
            p.barrier()

        for oc in range(8):
            bank = oc % 2
            def _(t, oc=oc, bank=bank):
                if oc >= 2:
                    t.wait_ge(DV, p.cnt["_oev%d" % (oc - 2)])
                ins = None
                for kc in range(16):
                    ins = t.matmul(pA[:, bank * 512:bank * 512 + 512],
                                   opT[:, kc * 1024 + oc * 128:kc * 1024 + (oc + 1) * 128],
                                   ybf[:, kc * TC:(kc + 1) * TC],
                                   start=(kc == 0), stop=(kc == 15))
                p.tick(ins, "PE")
            p.tensor(_)
            pe_o = p.val("PE")

            def _(v, oc=oc, bank=bank, pe_o=pe_o):
                v.wait_ge(PE, pe_o)
                if oc >= 1:
                    v.wait_ge(ST, p.cnt["_ost%d" % (oc - 1)])
                p.tick(v.tensor_copy(ostage[0][:], pA[:, bank * 512:bank * 512 + 512]), "DV")
            p.vector(_)
            p.cnt["_oev%d" % oc] = p.val("DV")

            def _(sy, oc=oc, bank=bank, tc=tc):
                sy.wait_ge(DV, p.cnt["_oev%d" % oc])
                sy.dma_start(out=mam_out[oc * 128:(oc + 1) * 128, tc * TC:(tc + 1) * TC],
                             in_=ostage[0][:]).then_inc(ST, 16)
            p.sync(_)
            p.cnt["_ost%d" % oc] = p.inc("ST", 16)

        p.barrier()

    def _(sy):
        sy.wait_ge(ST, p.val("ST"))
    p.sync(_)

    if standalone:
        p.end()
        return p.nc
    return None


def emit_quant(p, W, nb, src, dst, E_ap, sc, psA, psB, id_f):
    V, S, Tt = p.vector, p.scalar, p.tensor
    DV, AC, PE = p.sems["DV"], p.sems["AC"], p.sems["PE"]
    nw = (W + 511) // 512
    dv_entry = p.val("DV")

    def _(v):
        v.wait_ge(DV, dv_entry)
        p.fence(v, v.tensor_reduce(sc["bm"][:, 0:nb],
                                   src.rearrange("p (b j) -> p b j", j=16),
                                   axis=AX.X, op=ALU.max, apply_absolute_value=True), "DV")
        p.fence(v, v.tensor_scalar_max(sc["bm"][:, 0:nb], sc["bm"][:, 0:nb], 6e-12), "DV")
        p.fence(v, v.reciprocal(sc["rc"][:, 0:nb], sc["bm"][:, 0:nb]), "DV")
        p.tick(v.tensor_scalar_mul(sc["rc"][:, 0:nb], sc["rc"][:, 0:nb], 48.0), "DV")
    V(_)
    dv1 = p.val("DV")

    def _(t):
        t.wait_ge(DV, dv1)
        t.transpose(psB[:, 0:128], sc["bm"][:, 0:128], id_f)
        p.tick(t.transpose(psB[:, 128:256], sc["rc"][:, 0:128], id_f), "PE")
    Tt(_)
    pe1 = p.val("PE")

    def _(v):
        v.wait_ge(PE, pe1)
        v.tensor_copy(sc["bmT"][0:nb, :], psB[0:nb, 0:128])
        p.tick(v.tensor_copy(sc["rcT"][0:nb, :], psB[0:nb, 128:256]), "DV")
    V(_)
    dv2 = p.val("DV")

    def _(t):
        t.wait_ge(DV, dv2)
        ins = None
        for cw in range(nw):
            w0, w1 = cw * 512, min(W, (cw + 1) * 512)
            ins = t.matmul(psA[:, cw * 512:cw * 512 + (w1 - w0)], sc["bmT"][0:nb, :],
                           E_ap[0:nb, w0:w1], start=True, stop=True)
            ins = t.matmul(psA[:, nw * 512 + cw * 512:nw * 512 + cw * 512 + (w1 - w0)],
                           sc["rcT"][0:nb, :], E_ap[0:nb, w0:w1], start=True, stop=True)
        p.tick(ins, "PE")
    Tt(_)
    pe2 = p.val("PE")

    def _(s):
        s.wait_ge(DV, dv2)
        p.tick(s.activation(sc["ax"][:, 0:W], src, AF.Abs), "AC")
    S(_)
    ac1 = p.val("AC")

    def _(v):
        v.wait_ge(PE, pe2)
        v.wait_ge(AC, ac1)
        ins = None
        for cw in range(nw):
            w0, w1 = cw * 512, min(W, (cw + 1) * 512)
            ins = v.tensor_mul(sc["r"][:, w0:w1], sc["ax"][:, w0:w1],
                               psA[:, nw * 512 + cw * 512:nw * 512 + cw * 512 + (w1 - w0)])
        p.fence(v, ins, "DV")
        p.fence(v, v.tensor_single_scalar(sc["accA"][:, 0:W], sc["r"][:, 0:W], 2.0, ALU.is_gt), "DV")
        for thr in (6.0, 10.0, 14.0):
            p.fence(v, v.scalar_tensor_tensor(sc["accA"][:, 0:W], sc["r"][:, 0:W], thr,
                                              sc["accA"][:, 0:W], ALU.is_gt, ALU.add), "DV")
        p.fence(v, v.tensor_single_scalar(sc["accB"][:, 0:W], sc["r"][:, 0:W], 20.0, ALU.is_gt), "DV")
        p.fence(v, v.scalar_tensor_tensor(sc["accB"][:, 0:W], sc["r"][:, 0:W], 28.0,
                                          sc["accB"][:, 0:W], ALU.is_gt, ALU.add), "DV")
        p.fence(v, v.scalar_tensor_tensor(sc["accA"][:, 0:W], sc["accA"][:, 0:W], 0.5,
                                          sc["accB"][:, 0:W], ALU.mult, ALU.add), "DV")
        p.fence(v, v.tensor_scalar(sc["accB"][:, 0:W], sc["r"][:, 0:W], 40.0, 2.0,
                                   ALU.is_gt, ALU.mult), "DV")
        p.fence(v, v.tensor_add(sc["accA"][:, 0:W], sc["accA"][:, 0:W], sc["accB"][:, 0:W]), "DV")
        p.fence(v, v.tensor_scalar(sc["sg"][:, 0:W], src, 0.0, 1.0 / 3.0, ALU.is_gt, ALU.mult), "DV")
        p.fence(v, v.tensor_scalar_sub(sc["sg"][:, 0:W], sc["sg"][:, 0:W], 1.0 / 6.0), "DV")
        ins = None
        for cw in range(nw):
            w0, w1 = cw * 512, min(W, (cw + 1) * 512)
            ins = v.tensor_mul(sc["sg"][:, w0:w1], sc["sg"][:, w0:w1],
                               psA[:, cw * 512:cw * 512 + (w1 - w0)])
        p.fence(v, ins, "DV")
        p.tick(v.tensor_mul(dst, sc["accA"][:, 0:W], sc["sg"][:, 0:W]), "DV")
    V(_)
    return p.val("DV")


def build_C(p=None, io=None):
    standalone = p is None
    if standalone:
        p = Prog()
    else:
        p.phase("C_", io)
    xn_in = p.dram("xn_tm", [T, D], F32, kind="ExternalInput")
    xfm_in = p.dram("x_fm", [D, T], F32, kind="ExternalInput")
    mam_in = p.dram("mam_out", [D, T], F32, kind="ExternalInput")
    wg_in = p.dram("w_gate", [2816, 1024], F32, kind="ExternalInput")
    wu_in = p.dram("w_up", [2816, 1024], F32, kind="ExternalInput")
    wd_in = p.dram("w_down", [1024, 2816], F32, kind="ExternalInput")
    ident_in = p.dram("ident", [128, 128], F32, kind="ExternalInput")
    e64_in = p.dram("E64", [64, 1024], F32, kind="ExternalInput")
    e32_in = p.dram("E32", [32, 512], F32, kind="ExternalInput")
    e44_in = p.dram("E44", [44, 704], F32, kind="ExternalInput")
    x1_out = p.dram("x1_fm", [D, T], F32, kind="ExternalOutput")

    wgT_d = p.dram("wgT_d", [1024, 2816], BF16)
    wuT_d = p.dram("wuT_d", [1024, 2816], BF16)
    wdT_d = p.dram("wdT_d", [2816, 1024], BF16)

    xf = p.sbuf("xf", [128, 8 * T], F32)
    id_f = p.sbuf("id_f", [128, 128], F32)
    id_b = p.sbuf("id_b", [128, 128], BF16)
    e64 = p.sbuf("e64", [64, 1024], F32)
    e32 = p.sbuf("e32", [32, 512], F32)
    e44 = p.sbuf("e44", [44, 704], F32)
    sc = {
        "bm": p.sbuf("q_bm", [128, 128], F32),
        "rc": p.sbuf("q_rc", [128, 128], F32),
        "bmT": p.sbuf("q_bmT", [128, 128], F32),
        "rcT": p.sbuf("q_rcT", [128, 128], F32),
        "ax": p.sbuf("q_ax", [128, 1024], F32),
        "accA": p.sbuf("q_accA", [128, 1024], F32),
        "accB": p.sbuf("q_accB", [128, 1024], F32),
        "sg": p.sbuf("q_sg", [128, 1024], F32),
    }
    sc["r"] = sc["ax"]
    wstage = p.sbuf("wstage", [128, 2816], F32)
    wq_nat = p.sbuf("wq_nat", [128, 2816], BF16)
    xq_fm = p.sbuf("xq_fm", [128, 8 * 1024], BF16)
    hq_fm = p.sbuf("hq_fm", [128, 22 * 1024], BF16)
    wqg = p.sbuf("wqg", [128, 8 * 512], BF16)
    wqu = p.sbuf("wqu", [128, 8 * 512], BF16)
    wdT = p.sbuf("wdTs", [128, 22 * 128], BF16)
    xnst = p.sbuf("xnst", [128, 1024], F32)
    xqt = p.sbuf("xqt", [128, 1024], BF16)
    sgate = p.sbuf("sgate", [128, 512], F32)
    h2f = p.sbuf("h2f", [128, 512], F32)
    hqt = p.sbuf("hqt", [128, 512], BF16)

    pA = p.psum("pA", [128, 2048], F32)
    pB = p.psum("pB", [128, 1024], F32)
    pb0 = p.psum("pb0", [128, 1024], BF16)
    pb1 = p.psum("pb1", [128, 1024], BF16)

    if standalone:
        p.begin(consts=[1.0])
    LD, GD, ST, PE, DV, AC = (p.sems[k] for k in ("LD", "GD", "ST", "PE", "DV", "AC"))

    def _(sy):
        sy.dma_start(out=xf.rearrange("p (c t) -> p c t", t=T),
                     in_=xfm_in.rearrange("(c p) t -> p c t", p=128)).then_inc(LD, 16)
        sy.dma_start(out=id_f[:], in_=ident_in).then_inc(LD, 16)
        sy.dma_start(out=e64[:], in_=e64_in).then_inc(LD, 16)
        sy.dma_start(out=e32[:], in_=e32_in).then_inc(LD, 16)
        sy.dma_start(out=e44[:], in_=e44_in).then_inc(LD, 16)
    p.sync(_)
    p.inc("LD", 16 * 5)

    def _(g):
        g.dma_start(out=id_b[:], in_=ident_in).then_inc(GD, 16)
    p.gpsimd(_)
    p.inc("GD", 16)

    for c in range(8):
        def _(sy, c=c):
            sy.wait_ge(LD, p.val("LD"))
            if c >= 1:
                sy.wait_ge(DV, p.cnt["_madd%d" % (c - 1)])
            sy.dma_start(out=wstage[:, 0:2048], in_=mam_in[c * 128:(c + 1) * 128, :]).then_inc(LD, 16)
        p.sync(_)
        ld_t = p.inc("LD", 16)

        def _(v, c=c, ld_t=ld_t):
            v.wait_ge(LD, ld_t)
            p.tick(v.tensor_add(xf[:, c * T:(c + 1) * T], xf[:, c * T:(c + 1) * T],
                                wstage[:, 0:2048]), "DV")
        p.vector(_)
        p.cnt["_madd%d" % c] = p.val("DV")

    p.barrier()

    for nc8 in range(8):
        def _(sy, nc8=nc8):
            sy.dma_start(out=wstage[:, 0:2816], in_=wd_in[nc8 * 128:(nc8 + 1) * 128, :]).then_inc(LD, 16)
        p.sync(_)
        ld_t = p.inc("LD", 16)

        def _(v, ld_t=ld_t):
            v.wait_ge(LD, ld_t)
            v.engine_nop()
        p.vector(_)
        for hh in range(4):
            emit_quant(p, 704, 44, wstage[:, hh * 704:(hh + 1) * 704],
                       wq_nat[:, hh * 704:(hh + 1) * 704], e44[:], sc, pA, pB, id_f[:])
        dvq = p.val("DV")
        for grp, (k0, k1) in enumerate([(0, 11), (11, 22)]):
            def _(t, dvq=dvq, k0=k0, k1=k1, grp=grp):
                t.wait_ge(DV, dvq)
                if grp == 1:
                    t.wait_ge(DV, p.cnt["_wdev0"])
                ins = None
                for kc in range(k0, k1):
                    j = kc - k0
                    dst = pb0[:, j * 128:(j + 1) * 128] if j < 8 else pb1[:, (j - 8) * 128:(j - 7) * 128]
                    ins = t.transpose(dst, wq_nat[:, kc * 128:(kc + 1) * 128], id_b[:])
                p.tick(ins, "PE")
            p.tensor(_)
            pe_tr = p.val("PE")

            def _(v, pe_tr=pe_tr, k0=k0, k1=k1):
                v.wait_ge(PE, pe_tr)
                ins = None
                for kc in range(k0, k1):
                    j = kc - k0
                    src = pb0[:, j * 128:(j + 1) * 128] if j < 8 else pb1[:, (j - 8) * 128:(j - 7) * 128]
                    ins = v.tensor_copy(wdT[:, kc * 128:(kc + 1) * 128], src)
                p.tick(ins, "DV")
            p.vector(_)
            p.cnt["_wdev%d" % grp] = p.val("DV")
        dv_ev = p.val("DV")

        def _(sy, nc8=nc8, dv_ev=dv_ev):
            sy.wait_ge(DV, dv_ev)
            sy.dma_start(out=wdT_d.rearrange("(kc p) n -> p kc n", p=128)[:, :, nc8 * 128:(nc8 + 1) * 128],
                         in_=wdT.rearrange("p (kc n) -> p kc n", n=128)).then_inc(ST, 16)
        p.sync(_)
        p.inc("ST", 16)
        p.barrier()

    TH = 1024
    for th in range(2):
        th0 = th * TH
        for tt in range(8):
            ta = th0 + tt * 128
            def _(sy, ta=ta):
                sy.dma_start(out=xnst[:], in_=xn_in[ta:ta + 128, :]).then_inc(LD, 16)
            p.sync(_)
            ld_t = p.inc("LD", 16)

            def _(v, ld_t=ld_t):
                v.wait_ge(LD, ld_t)
                v.engine_nop()
            p.vector(_)
            emit_quant(p, 1024, 64, xnst[:], xqt[:], e64[:], sc, pA, pB, id_f[:])
            dvq = p.val("DV")

            def _(t, dvq=dvq, tt=tt):
                t.wait_ge(DV, dvq)
                ins = None
                for c in range(8):
                    ins = t.transpose(pb0[:, c * 128:(c + 1) * 128], xqt[:, c * 128:(c + 1) * 128], id_b[:])
                p.tick(ins, "PE")
            p.tensor(_)
            pe_t = p.val("PE")

            def _(v, pe_t=pe_t, tt=tt):
                v.wait_ge(PE, pe_t)
                ins = None
                for c in range(8):
                    ins = v.tensor_copy(xq_fm[:, c * 1024 + tt * 128:c * 1024 + (tt + 1) * 128],
                                        pb0[:, c * 128:(c + 1) * 128])
                p.tick(ins, "DV")
            p.vector(_)
            p.barrier()

        NS = [(0, 512), (512, 1024), (1024, 1536), (1536, 2048), (2048, 2560), (2560, 2816)]
        _emit_C2(p, th, NS, wg_in, wu_in, wgT_d, wuT_d, wqg, wqu, wstage, wq_nat,
                 xq_fm, hq_fm, sgate, h2f, hqt, e64, e32, sc, pA, pB, pb0, pb1, id_f, id_b)

        for oc in range(8):
            def _(sy, oc=oc):
                if oc >= 1:
                    sy.wait_ge(PE, p.cnt["_dnmm%d" % (oc - 1)])
                sy.dma_start(out=wdT.rearrange("p (kc n) -> p kc n", n=128),
                             in_=wdT_d.rearrange("(kc p) n -> p kc n", p=128)[:, :, oc * 128:(oc + 1) * 128]).then_inc(LD, 16)
            p.sync(_)
            ld_t = p.inc("LD", 16)

            for tch in range(2):
                bank = tch
                def _(t, oc=oc, tch=tch, bank=bank, ld_t=ld_t):
                    t.wait_ge(LD, ld_t)
                    if oc >= 1 or tch >= 1:
                        t.wait_ge(DV, p.cnt.get("_dnev%d_%d" % ((oc, tch - 1) if tch >= 1 else (oc - 1, 1)), 0))
                    ins = None
                    for kc in range(22):
                        ins = t.matmul(pA[:, bank * 512:bank * 512 + 512],
                                       wdT[:, kc * 128:(kc + 1) * 128],
                                       hq_fm[:, kc * 1024 + tch * 512:kc * 1024 + tch * 512 + 512],
                                       start=(kc == 0), stop=(kc == 21))
                    p.tick(ins, "PE")
                p.tensor(_)
                if tch == 1:
                    p.cnt["_dnmm%d" % oc] = p.val("PE")
                pe_mm = p.val("PE")

                def _(v, oc=oc, tch=tch, bank=bank, pe_mm=pe_mm, th=th):
                    v.wait_ge(PE, pe_mm)
                    dst = xf[:, oc * T + th * TH + tch * 512:oc * T + th * TH + tch * 512 + 512]
                    p.tick(v.tensor_add(dst, dst, pA[:, bank * 512:bank * 512 + 512]), "DV")
                p.vector(_)
                p.cnt["_dnev%d_%d" % (oc, tch)] = p.val("DV")
        p.barrier()

    def _(sy):
        sy.dma_start(out=x1_out.rearrange("(c p) t -> p c t", p=128),
                     in_=xf.rearrange("p (c t) -> p c t", t=T)).then_inc(ST, 16)
        sy.wait_ge(ST, p.val("ST") + 16)
    p.sync(_)
    p.inc("ST", 16)

    if standalone:
        p.end()
        return p.nc
    return None


def _emit_C2(p, th, NS, wg_in, wu_in, wgT_d, wuT_d, wqg, wqu, wstage, wq_nat,
             xq_fm, hq_fm, sgate, h2f, hqt, e64, e32, sc, pA, pB, pb0, pb1, id_f, id_b):
    LD, GD, ST, PE, DV, AC = (p.sems[k] for k in ("LD", "GD", "ST", "PE", "DV", "AC"))
    TH = 1024
    for si, (n0, n1) in enumerate(NS):
        nwid = n1 - n0
        nch = (nwid + 127) // 128
        if th == 0:
            for wi, (w_in_, wT_d, wq_sb) in enumerate([(wg_in, wgT_d, wqg), (wu_in, wuT_d, wqu)]):
                for rc in range(nch):
                    r0 = n0 + rc * 128
                    def _(sy, w_in_=w_in_, r0=r0):
                        sy.wait_ge(DV, p.val("DV"))
                        sy.dma_start(out=wstage[:, 0:1024], in_=w_in_[r0:r0 + 128, :]).then_inc(LD, 16)
                    p.sync(_)
                    ld_t = p.inc("LD", 16)

                    def _(v, ld_t=ld_t):
                        v.wait_ge(LD, ld_t)
                        v.engine_nop()
                    p.vector(_)
                    emit_quant(p, 1024, 64, wstage[:, 0:1024], wq_nat[:, 0:1024],
                               e64[:], sc, pA, pB, id_f[:])
                    dvq = p.val("DV")

                    def _(t, dvq=dvq, rc=rc):
                        t.wait_ge(DV, dvq)
                        ins = None
                        for c in range(8):
                            ins = t.transpose(pb0[:, c * 128:(c + 1) * 128],
                                              wq_nat[:, c * 128:(c + 1) * 128], id_b[:])
                        p.tick(ins, "PE")
                    p.tensor(_)
                    pe_t = p.val("PE")

                    def _(v, pe_t=pe_t, rc=rc, wq_sb=wq_sb):
                        v.wait_ge(PE, pe_t)
                        ins = None
                        for c in range(8):
                            ins = v.tensor_copy(wq_sb[:, c * 512 + rc * 128:c * 512 + (rc + 1) * 128],
                                                pb0[:, c * 128:(c + 1) * 128])
                        p.tick(ins, "DV")
                    p.vector(_)
                    p.barrier()

                dv_now = p.val("DV")
                def _(sy, wT_d=wT_d, wq_sb=wq_sb, n0=n0, nwid=nwid, dv_now=dv_now):
                    sy.wait_ge(DV, dv_now)
                    sy.dma_start(out=wT_d.rearrange("(kc p) n -> p kc n", p=128)[:, :, n0:n0 + nwid],
                                 in_=wq_sb.rearrange("p (kc n) -> p kc n", n=512)[:, :, 0:nwid]).then_inc(ST, 16)
                p.sync(_)
                p.inc("ST", 16)
        else:
            for wT_d, wq_sb in [(wgT_d, wqg), (wuT_d, wqu)]:
                def _(sy, wT_d=wT_d, wq_sb=wq_sb, n0=n0, nwid=nwid):
                    sy.wait_ge(PE, p.val("PE"))
                    sy.dma_start(out=wq_sb.rearrange("p (kc n) -> p kc n", n=512)[:, :, 0:nwid],
                                 in_=wT_d.rearrange("(kc p) n -> p kc n", p=128)[:, :, n0:n0 + nwid]).then_inc(LD, 16)
                p.sync(_)
                p.inc("LD", 16)
            p.barrier()

        for tt in range(8):
            def _(t, si=si, tt=tt, nwid=nwid):
                if tt >= 1:
                    t.wait_ge(DV, p.cnt.get("_hqev", 0))
                ins = None
                for c in range(8):
                    ins = t.matmul(pA[:, 0:nwid], xq_fm[:, c * 1024 + tt * 128:c * 1024 + (tt + 1) * 128],
                                   wqg[:, c * 512:c * 512 + nwid], start=(c == 0), stop=(c == 7))
                for c in range(8):
                    ins = t.matmul(pA[:, 512:512 + nwid], xq_fm[:, c * 1024 + tt * 128:c * 1024 + (tt + 1) * 128],
                                   wqu[:, c * 512:c * 512 + nwid], start=(c == 0), stop=(c == 7))
                p.tick(ins, "PE")
            p.tensor(_)
            pe_g = p.val("PE")

            def _(s, pe_g=pe_g, nwid=nwid):
                s.wait_ge(PE, pe_g)
                p.tick(s.activation(sgate[:, 0:nwid], pA[:, 0:nwid], AF.Silu), "AC")
            p.scalar(_)
            ac_g = p.val("AC")

            def _(v, ac_g=ac_g, nwid=nwid):
                v.wait_ge(AC, ac_g)
                p.tick(v.tensor_mul(h2f[:, 0:nwid], sgate[:, 0:nwid], pA[:, 512:512 + nwid]), "DV")
            p.vector(_)

            emit_quant(p, nwid, nwid // 16, h2f[:, 0:nwid], hqt[:, 0:nwid],
                       e32[:], sc, pA[:, 1024:2048], pB, id_f[:])
            dv_hq = p.val("DV")

            def _(t, dv_hq=dv_hq, nwid=nwid):
                t.wait_ge(DV, dv_hq)
                ins = None
                for c in range((nwid + 127) // 128):
                    ins = t.transpose(pb1[:, c * 128:(c + 1) * 128], hqt[:, c * 128:(c + 1) * 128], id_b[:])
                p.tick(ins, "PE")
            p.tensor(_)
            pe_h = p.val("PE")

            def _(v, pe_h=pe_h, si=si, tt=tt, nwid=nwid):
                v.wait_ge(PE, pe_h)
                ins = None
                for c in range((nwid + 127) // 128):
                    kc = si * 4 + c
                    ins = v.tensor_copy(hq_fm[:, kc * 1024 + tt * 128:kc * 1024 + (tt + 1) * 128],
                                        pb1[:, c * 128:(c + 1) * 128])
                p.tick(ins, "DV")
            p.vector(_)
            p.cnt["_hqev"] = p.val("DV")
        p.barrier()


def build_D(p=None, io=None):
    standalone = p is None
    if standalone:
        p = Prog()
    else:
        p.phase("D_", io)
    x1_in = p.dram("x1_fm", [D, T], F32, kind="ExternalInput")
    wat_in = p.dram("w_attn", [1024, 1152], F32, kind="ExternalInput")
    wpj_in = p.dram("w_proj", [1024, 1024], F32, kind="ExternalInput")
    ident_in = p.dram("ident", [128, 128], F32, kind="ExternalInput")
    maskd_in = p.dram("mask_diag", [128, 128], F32, kind="ExternalInput")
    xn2_out = p.dram("xn2_bf", [D, T], F32, kind="ExternalOutput")
    x2p_out = p.dram("x2p_fm", [D, T], F32, kind="ExternalOutput")

    xf = p.sbuf("xf", [128, 8 * T], F32)
    xn2 = p.sbuf("xn2", [128, 8 * T], BF16)
    oat = p.sbuf("oat", [128, 8 * T], BF16)
    wslq = p.sbuf("wslq", [128, 8 * 64], BF16)
    wslk = p.sbuf("wslk", [128, 8 * 64], BF16)
    wslv = p.sbuf("wslv", [128, 8 * 64], BF16)
    wpj = p.sbuf("wpj", [128, 8 * 1024], BF16)
    id_f = p.sbuf("id_f", [128, 128], F32)
    id_b = p.sbuf("id_b", [128, 128], BF16)
    maskd = p.sbuf("maskd", [128, 128], F32)
    xtm = p.sbuf("xtm", [128, 1024], F32)
    xntm = p.sbuf("xntm", [128, 1024], F32)
    xn2f = p.sbuf("xn2f", [128, 1024], F32)
    stats = p.sbuf("stats", [128, 8], F32)
    kfm = p.sbuf("kfm", [64, T], BF16)
    vtm = p.sbuf("vtm", [128, 16 * 64], BF16)
    qbf = p.sbuf("qbf", [64, T], BF16)
    psb = p.sbuf("psb", [128, T], BF16)
    pts = p.sbuf("pts", [128, 16 * 128], BF16)
    rs = p.sbuf("rs", [128, 8], F32)
    mx = p.sbuf("mx", [128, 4], F32)
    ostage = p.sbuf("ostage", [128, 512], F32)

    pA = p.psum("pA", [128, 2048], F32)
    pB = p.psum("pB", [128, 1024], F32)
    pb0 = p.psum("pb0", [128, 1024], BF16)
    pb1 = p.psum("pb1", [128, 1024], BF16)

    if standalone:
        p.begin(consts=[1e-5])
    LD, GD, ST, PE, DV, AC = (p.sems[k] for k in ("LD", "GD", "ST", "PE", "DV", "AC"))

    def _(sy):
        sy.dma_start(out=xf.rearrange("p (c t) -> p c t", t=T),
                     in_=x1_in.rearrange("(c p) t -> p c t", p=128)).then_inc(LD, 16)
        sy.dma_start(out=id_f[:], in_=ident_in).then_inc(LD, 16)
        sy.dma_start(out=maskd[:], in_=maskd_in).then_inc(LD, 16)
    p.sync(_)
    p.inc("LD", 48)

    def _(g):
        g.dma_start(out=id_b[:], in_=ident_in).then_inc(GD, 16)
        g.dma_start(out=wslk.rearrange("p (c n) -> p c n", n=64),
                    in_=wat_in.rearrange("(c p) n -> p c n", p=128)[:, :, 1024:1088]).then_inc(GD, 16)
        g.dma_start(out=wslv.rearrange("p (c n) -> p c n", n=64),
                    in_=wat_in.rearrange("(c p) n -> p c n", p=128)[:, :, 1088:1152]).then_inc(GD, 16)
        g.dma_start(out=wpj.rearrange("p (c n) -> p c n", n=1024),
                    in_=wpj_in.rearrange("(c p) n -> p c n", p=128)).then_inc(GD, 16)
    p.gpsimd(_)
    p.inc("GD", 64)
    p.barrier()

    for tt in range(16):
        def _(t, tt=tt):
            if tt >= 1:
                t.wait_ge(DV, p.cnt["_lnev%d" % (tt - 1)])
            ins = None
            for c in range(8):
                ins = t.transpose(pB[:, (c % 4) * 128 + (c // 4) * 512:(c % 4) * 128 + (c // 4) * 512 + 128],
                                  xf[:, c * T + tt * 128:c * T + (tt + 1) * 128], id_f[:])
            p.tick(ins, "PE")
        p.tensor(_)
        pe_t = p.val("PE")

        def _(v, pe_t=pe_t, tt=tt):
            v.wait_ge(PE, pe_t)
            v.tensor_copy(xtm[:, 0:512], pB[:, 0:512])
            p.tick(v.tensor_copy(xtm[:, 512:1024], pB[:, 512:1024]), "DV")
        p.vector(_)
        dv_tm = p.val("DV")
        p.cnt["_lnev%d" % tt] = dv_tm

        def _(s, dv_tm=dv_tm):
            s.wait_ge(DV, dv_tm)
            p.tick(s.activation(xntm[:], xtm[:], AF.Square, accum_out=stats[:, 0:1]), "AC")
        p.scalar(_)
        ac1 = p.val("AC")

        def _(v, ac1=ac1):
            v.wait_ge(AC, ac1)
            p.fence(v, v.reduce_sum(stats[:, 1:2], xtm[:], axis=AX.X), "DV")
            p.fence(v, v.tensor_scalar_mul(stats[:, 1:2], stats[:, 1:2], 1.0 / D), "DV")
            p.fence(v, v.tensor_tensor(stats[:, 2:3], stats[:, 1:2], stats[:, 1:2], ALU.mult), "DV")
            p.tick(v.scalar_tensor_tensor(stats[:, 3:4], stats[:, 0:1], 1.0 / D, stats[:, 2:3],
                                          ALU.mult, ALU.subtract), "DV")
        p.vector(_)
        dv_st = p.val("DV")

        def _(s, dv_st=dv_st):
            s.wait_ge(DV, dv_st)
            p.fence(s, s.activation(stats[:, 4:5], stats[:, 3:4], AF.Ln, bias=p.C(1e-5)), "AC")
            p.tick(s.activation(stats[:, 5:6], stats[:, 4:5], AF.Exp, scale=-0.5), "AC")
        p.scalar(_)
        ac2 = p.val("AC")

        def _(v, ac2=ac2):
            v.wait_ge(AC, ac2)
            p.tick(v.tensor_scalar(xntm[:], xtm[:], stats[:, 1:2], stats[:, 5:6],
                                   ALU.subtract, ALU.mult), "DV")
        p.vector(_)
        dv_xn = p.val("DV")

        def _(t, tt=tt, dv_xn=dv_xn):
            t.wait_ge(DV, dv_xn)
            ins = None
            for c in range(8):
                ins = t.transpose(pB[:, (c % 4) * 128 + (c // 4) * 512:(c % 4) * 128 + (c // 4) * 512 + 128],
                                  xntm[:, c * 128:(c + 1) * 128], id_f[:])
            p.tick(ins, "PE")
        p.tensor(_)
        pe_t2 = p.val("PE")

        def _(v, pe_t2=pe_t2, tt=tt):
            v.wait_ge(PE, pe_t2)
            if tt >= 1:
                v.wait_ge(ST, p.cnt["_xnst%d" % (tt - 1)])
            ins = None
            for c in range(8):
                src = pB[:, (c % 4) * 128 + (c // 4) * 512:(c % 4) * 128 + (c // 4) * 512 + 128]
                v.tensor_copy(xn2f[:, c * 128:(c + 1) * 128], src)
                ins = v.tensor_copy(xn2[:, c * T + tt * 128:c * T + (tt + 1) * 128], src)
            p.tick(ins, "DV")
        p.vector(_)
        p.cnt["_lnev%d" % tt] = p.val("DV")

        def _(sy, tt=tt):
            sy.wait_ge(DV, p.cnt["_lnev%d" % tt])
            sy.dma_start(out=xn2_out.rearrange("(c p) t -> p c t", p=128)[:, :, tt * 128:(tt + 1) * 128],
                         in_=xn2f.rearrange("p (c q) -> p c q", q=128)).then_inc(ST, 16)
        p.sync(_)
        p.cnt["_xnst%d" % tt] = p.inc("ST", 16)

    p.barrier()

    for tc in range(4):
        def _(t, tc=tc):
            if tc >= 2:
                t.wait_ge(DV, p.cnt["_kev%d" % (tc - 2)])
            ins = None
            for c in range(8):
                ins = t.matmul(pB[0:64, (tc % 2) * 512:(tc % 2) * 512 + 512],
                               wslk[:, c * 64:(c + 1) * 64],
                               xn2[:, c * T + tc * 512:c * T + (tc + 1) * 512],
                               start=(c == 0), stop=(c == 7))
            p.tick(ins, "PE")
        p.tensor(_)
        pe_k = p.val("PE")

        def _(v, tc=tc, pe_k=pe_k):
            v.wait_ge(PE, pe_k)
            p.tick(v.tensor_copy(kfm[:, tc * 512:(tc + 1) * 512],
                                 pB[0:64, (tc % 2) * 512:(tc % 2) * 512 + 512]), "DV")
        p.vector(_)
        p.cnt["_kev%d" % tc] = p.val("DV")

    for tt in range(16):
        def _(t, tt=tt):
            if tt >= 2:
                t.wait_ge(DV, p.cnt["_vev%d" % (tt - 2)])
            ins = None
            for c in range(8):
                ins = t.matmul(pA[:, (tt % 2) * 512:(tt % 2) * 512 + 64],
                               xn2[:, c * T + tt * 128:c * T + (tt + 1) * 128],
                               wslv[:, c * 64:(c + 1) * 64],
                               start=(c == 0), stop=(c == 7))
            p.tick(ins, "PE")
        p.tensor(_)
        pe_v = p.val("PE")

        def _(v, tt=tt, pe_v=pe_v):
            v.wait_ge(PE, pe_v)
            p.tick(v.tensor_copy(vtm[:, tt * 64:(tt + 1) * 64],
                                 pA[:, (tt % 2) * 512:(tt % 2) * 512 + 64]), "DV")
        p.vector(_)
        p.cnt["_vev%d" % tt] = p.val("DV")

    p.barrier()

    for h in range(16):
        def _(g, h=h):
            if h >= 1:
                g.wait_ge(PE, p.cnt["_qslmm"])
            g.dma_start(out=wslq.rearrange("p (c n) -> p c n", n=64),
                        in_=wat_in.rearrange("(c p) n -> p c n", p=128)[:, :, h * 64:(h + 1) * 64]
                        ).then_inc(GD, 16)
        p.gpsimd(_)
        gd_q = p.inc("GD", 16)

        for tc in range(4):
            def _(t, h=h, tc=tc, gd_q=gd_q):
                t.wait_ge(GD, gd_q)
                if tc >= 2:
                    t.wait_ge(DV, p.cnt["_qev%d" % (tc - 2)])
                ins = None
                for c in range(8):
                    ins = t.matmul(pB[0:64, (tc % 2) * 512:(tc % 2) * 512 + 512],
                                   wslq[:, c * 64:(c + 1) * 64],
                                   xn2[:, c * T + tc * 512:c * T + (tc + 1) * 512],
                                   start=(c == 0), stop=(c == 7))
                p.tick(ins, "PE")
            p.tensor(_)
            pe_q = p.val("PE")
            if tc == 3:
                p.cnt["_qslmm"] = pe_q

            def _(v, tc=tc, pe_q=pe_q):
                v.wait_ge(PE, pe_q)
                p.tick(v.tensor_copy(qbf[:, tc * 512:(tc + 1) * 512],
                                     pB[0:64, (tc % 2) * 512:(tc % 2) * 512 + 512]), "DV")
            p.vector(_)
            p.cnt["_qev%d" % tc] = p.val("DV")

        dv_q = p.val("DV")
        for qi in range(16):
            nk = qi * 128 + 128
            nkc = (nk + 511) // 512

            def _(t, h=h, qi=qi, nk=nk, nkc=nkc, dv_q=dv_q):
                t.wait_ge(DV, dv_q)
                if qi >= 1:
                    t.wait_ge(AC, p.cnt["_pexp%d" % (qi - 1)])
                ins = None
                for kc in range(nkc):
                    w0 = kc * 512
                    w1 = min(nk, w0 + 512)
                    ins = t.matmul(pA[:, w0:w1], qbf[:, qi * 128:(qi + 1) * 128],
                                   kfm[:, w0:w1], start=True, stop=True)
                p.tick(ins, "PE")
            p.tensor(_)
            pe_s = p.val("PE")

            def _(v, qi=qi, nk=nk, pe_s=pe_s):
                v.wait_ge(PE, pe_s)
                if qi >= 1:
                    v.wait_ge(AC, p.cnt["_pexp%d" % (qi - 1)])
                p.fence(v, v.tensor_add(pB[:, 0:128], pA[:, nk - 128:nk], maskd[:]), "DV")
                if nk > 128:
                    p.fence(v, v.reduce_max(mx[:, 0:1], pA[:, 0:nk - 128], axis=AX.X), "DV")
                    p.fence(v, v.reduce_max(mx[:, 1:2], pB[:, 0:128], axis=AX.X), "DV")
                    p.fence(v, v.tensor_max(mx[:, 0:1], mx[:, 0:1], mx[:, 1:2]), "DV")
                else:
                    p.fence(v, v.reduce_max(mx[:, 0:1], pB[:, 0:128], axis=AX.X), "DV")
                p.tick(v.tensor_scalar_mul(mx[:, 1:2], mx[:, 0:1], -0.125), "DV")
            p.vector(_)
            dv_mx = p.val("DV")

            def _(s, qi=qi, nk=nk, dv_mx=dv_mx):
                s.wait_ge(DV, dv_mx)
                if nk > 128:
                    s.activation(psb[:, 0:nk - 128], pA[:, 0:nk - 128], AF.Exp, scale=0.125,
                                 bias=mx[:, 1:2], accum_out=rs[:, 0:1])
                ins = s.activation(psb[:, nk - 128:nk], pB[:, 0:128], AF.Exp, scale=0.125,
                                   bias=mx[:, 1:2], accum_out=rs[:, 1:2])
                p.tick(ins, "AC")
            p.scalar(_)
            p.cnt["_pexp%d" % qi] = p.val("AC")
            ac_e = p.val("AC")

            def _(v, qi=qi, nk=nk, ac_e=ac_e):
                v.wait_ge(AC, ac_e)
                if nk > 128:
                    p.fence(v, v.tensor_add(rs[:, 6:7], rs[:, 0:1], rs[:, 1:2]), "DV")
                else:
                    p.fence(v, v.tensor_copy(rs[:, 6:7], rs[:, 1:2]), "DV")
                p.fence(v, v.reciprocal(rs[:, 7:8], rs[:, 6:7]), "DV")
                p.tick(v.tensor_scalar_mul(psb[:, 0:nk], psb[:, 0:nk], rs[:, 7:8]), "DV")
            p.vector(_)
            dv_pn = p.val("DV")

            nkt = nk // 128
            for g0 in range(0, nkt, 8):
                g1 = min(nkt, g0 + 8)
                def _(t, g0=g0, g1=g1, dv_pn=dv_pn, qi=qi):
                    t.wait_ge(DV, dv_pn)
                    if g0 > 0:
                        t.wait_ge(DV, p.cnt["_ptev"])
                    elif qi >= 1:
                        t.wait_ge(DV, p.cnt.get("_ptev", 0))
                    ins = None
                    for kt in range(g0, g1):
                        ins = t.transpose(pb0[:, (kt - g0) * 128:(kt - g0 + 1) * 128],
                                          psb[:, kt * 128:(kt + 1) * 128], id_b[:])
                    p.tick(ins, "PE")
                p.tensor(_)
                pe_pt = p.val("PE")

                def _(v, g0=g0, g1=g1, pe_pt=pe_pt):
                    v.wait_ge(PE, pe_pt)
                    ins = None
                    for kt in range(g0, g1):
                        ins = v.tensor_copy(pts[:, kt * 128:(kt + 1) * 128],
                                            pb0[:, (kt - g0) * 128:(kt - g0 + 1) * 128])
                    p.tick(ins, "DV")
                p.vector(_)
                p.cnt["_ptev"] = p.val("DV")
            dv_pt = p.val("DV")

            def _(t, qi=qi, nkt=nkt, dv_pt=dv_pt):
                t.wait_ge(DV, dv_pt)
                if qi >= 1:
                    t.wait_ge(DV, p.cnt["_oev"])
                ins = None
                for kt in range(nkt):
                    ins = t.matmul(pB[0:64, 512:512 + 128], vtm[:, kt * 64:(kt + 1) * 64],
                                   pts[:, kt * 128:(kt + 1) * 128],
                                   start=(kt == 0), stop=(kt == nkt - 1))
                p.tick(ins, "PE")
            p.tensor(_)
            pe_o = p.val("PE")

            def _(v, h=h, qi=qi, pe_o=pe_o):
                v.wait_ge(PE, pe_o)
                c = h // 2
                r0 = (h % 2) * 64
                p.tick(v.tensor_copy(oat[r0:r0 + 64, c * T + qi * 128:c * T + (qi + 1) * 128],
                                     pB[0:64, 512:512 + 128]), "DV")
            p.vector(_)
            p.cnt["_oev"] = p.val("DV")
        p.barrier()

    for oc in range(8):
        for tc in range(4):
            bank = (oc * 4 + tc) % 2
            def _(t, oc=oc, tc=tc, bank=bank):
                key = "_prev%d" % (1 - bank)
                if (oc * 4 + tc) >= 2:
                    t.wait_ge(DV, p.cnt["_pradd%d" % bank])
                ins = None
                for c in range(8):
                    ins = t.matmul(pA[:, bank * 512:bank * 512 + 512],
                                   wpj[:, c * 1024 + oc * 128:c * 1024 + (oc + 1) * 128],
                                   oat[:, c * T + tc * 512:c * T + (tc + 1) * 512],
                                   start=(c == 0), stop=(c == 7))
                p.tick(ins, "PE")
            p.tensor(_)
            pe_p = p.val("PE")

            def _(v, oc=oc, tc=tc, bank=bank, pe_p=pe_p):
                v.wait_ge(PE, pe_p)
                dst = xf[:, oc * T + tc * 512:oc * T + (tc + 1) * 512]
                p.tick(v.tensor_add(dst, dst, pA[:, bank * 512:bank * 512 + 512]), "DV")
            p.vector(_)
            p.cnt["_pradd%d" % bank] = p.val("DV")

    p.barrier()
    def _(sy):
        sy.dma_start(out=x2p_out.rearrange("(c p) t -> p c t", p=128),
                     in_=xf.rearrange("p (c t) -> p c t", t=T)).then_inc(ST, 16)
        sy.wait_ge(ST, p.val("ST") + 16)
    p.sync(_)
    p.inc("ST", 16)

    if standalone:
        p.end()
        return p.nc
    return None


def build_E(p=None, io=None, dbg=False):
    standalone = p is None
    if standalone:
        p = Prog()
    else:
        p.phase("E_", io)
    if dbg:
        dbg_s = p.dram("dbg_s", [8, 128, 8 * 2049], F32, kind="ExternalOutput")
        dbg_t = p.dram("dbg_t", [4, 128, T], F32, kind="ExternalOutput")
        dbg_rot = p.dram("dbg_rot", [4, 32, T], F32, kind="ExternalOutput")
        dbg_icol = p.dram("dbg_icol", [32, 4], F32, kind="ExternalOutput")
    xn2_in = p.dram("xn2_bf", [D, T], F32, kind="ExternalInput")
    x2p_in = p.dram("x2p_fm", [D, T], F32, kind="ExternalInput")
    wqkv_in = p.dram("Wqkv", [6, 1024, 3072], F32, kind="ExternalInput")
    wo_in = p.dram("Wo", [6, 1024, 1024], F32, kind="ExternalInput")
    w1_in = p.dram("W1", [6, 1024, 5460], F32, kind="ExternalInput")
    w2_in = p.dram("W2", [6, 2730, 1024], F32, kind="ExternalInput")
    qs_in = p.dram("q_scale_r", [128, 6], F32, kind="ExternalInput")
    ks_in = p.dram("k_scale_r", [128, 6], F32, kind="ExternalInput")
    lng_in = p.dram("ln_g_r", [128, 48], F32, kind="ExternalInput")
    lnb_in = p.dram("ln_b_r", [128, 48], F32, kind="ExternalInput")
    b1_in = p.dram("b1_r", [128, 6 * 44], F32, kind="ExternalInput")
    b2_in = p.dram("b2_r", [128, 48], F32, kind="ExternalInput")
    ident_in = p.dram("ident", [128, 128], F32, kind="ExternalInput")
    mb_in = p.dram("mask_band", [128, 384], F32, kind="ExternalInput")
    mb0_in = p.dram("mask_band_t0", [128, 384], F32, kind="ExternalInput")
    mb1_in = p.dram("mask_band_t1", [128, 384], F32, kind="ExternalInput")
    iota_t_in = p.dram("iota_t", [1, T], F32, kind="ExternalInput")
    iota_i_in = p.dram("iota_i", [32, 1], F32, kind="ExternalInput")
    dup32_in = p.dram("dup32", [32, 128], F32, kind="ExternalInput")
    bdiag_in = p.dram("bdiag", [128, 2], F32, kind="ExternalInput")
    bdiagT_in = p.dram("bdiagT", [2, 128], F32, kind="ExternalInput")
    ones1_in = p.dram("ones1", [1, 128], F32, kind="ExternalInput")
    out_tm = p.dram("out_tm", [T, D], F32, kind="ExternalOutput")

    TG = T + 2049 - T
    SW = 2049
    s_bf = p.sbuf("s_bf", [128, 8 * SW], BF16)
    tbl_cs = p.sbuf("tbl_cs", [128, T], BF16)
    tbl_ss = p.sbuf("tbl_ss", [128, T], BF16)
    tbl_ci = p.sbuf("tbl_ci", [128, T], BF16)
    tbl_si = p.sbuf("tbl_si", [128, T], BF16)
    xn_i = p.sbuf("xn_i", [128, 8 * T], BF16)
    maskb = p.sbuf("maskb", [128, 3 * 384], F32)
    id_f = p.sbuf("id_f", [128, 128], F32)
    id_b = p.sbuf("id_b", [128, 128], BF16)
    qs_sb = p.sbuf("qs_sb", [128, 6], F32)
    ks_sb = p.sbuf("ks_sb", [128, 6], F32)
    lng_sb = p.sbuf("lng_sb", [128, 48], F32)
    lnb_sb = p.sbuf("lnb_sb", [128, 48], F32)
    b1_sb = p.sbuf("b1_sb", [128, 6 * 44], F32)
    b2_sb = p.sbuf("b2_sb", [128, 48], F32)
    bdiag = p.sbuf("bdiag_s", [128, 2], F32)
    onec_b = p.sbuf("onec_b", [128, 1], BF16)
    onec_f = p.sbuf("onec_f", [128, 1], F32)
    bdiagT = p.sbuf("bdiagT_s", [2, 128], F32)
    ones1 = p.sbuf("ones1_s", [1, 128], F32)
    dup32 = p.sbuf("dup32_s", [32, 128], BF16)

    AR = 24576 + 4096
    arena = p.sbuf("arena_e", [128, AR], BF16)
    oat = arena[:, 0:16384]
    qh = arena[:, 16384:18432]
    kh = arena[:, 18432:20736]
    vtm = arena[:, 20736:22784]
    h2 = arena[:, 0:22528]
    af32 = p.sbuf("af32", [128, 8192], F32)
    sq_sb = af32[:, 0:512]
    nb_sb = af32[:, 512:1024]
    qn_sb = af32[:, 1024:1536]
    rr_sb = af32[:, 1536:2048]
    t1_sb = af32[:, 2048:2560]
    gel = af32[:, 2560:3072]
    wsl = [p.sbuf(f"wsl{i}", [128, 1024], BF16) for i in range(3)]
    w2sl = p.sbuf("w2sl", [128, 22 * 128], BF16)
    psb = p.sbuf("psb_e", [128, 384], BF16)
    pts = p.sbuf("pts_e", [128, 384], BF16)
    rs = p.sbuf("rs_e", [128, 8], F32)
    mxx = p.sbuf("mx_e", [128, 4], F32)
    rowAB = af32[0:2, 4096:4608]
    rowA = af32[0:1, 5632:6144]
    rowB = af32[0:1, 4608:5120]
    rowC = af32[0:1, 5120:5632]
    x2st = af32[:, 0:2048]
    fst = af32[:, 2048:4096]
    rot_a = af32[0:32, 0:2048]
    rot_b = af32[0:32, 2048:4096]
    rot_c = af32[0:32, 4096:6144]
    rot_d = af32[0:32, 6144:8192]
    icol = p.sbuf("icol", [32, 4], F32)
    rint = p.sbuf("rint", [32, 512], mybir.dt.int32)

    pA = p.psum("pA", [128, 2048], F32)
    pB = p.psum("pB", [128, 1024], F32)
    pb0 = p.psum("pb0", [128, 1024], BF16)
    pb1 = p.psum("pb1", [128, 1024], BF16)

    if standalone:
        p.begin(consts=[1e-5, 1.0])
    LD, GD, ST, PE, DV, AC = (p.sems[k] for k in ("LD", "GD", "ST", "PE", "DV", "AC"))

    HI2PI = 6.28125
    LO2PI = 0.0021853071795864769
    INV2PI = 1.0 / (2.0 * np.pi)

    def _(sy):
        for dst, src in [(id_f[:], ident_in), (maskb[:, 0:384], mb_in),
                         (maskb[:, 384:768], mb0_in), (maskb[:, 768:1152], mb1_in),
                         (qs_sb[:], qs_in), (ks_sb[:], ks_in), (lng_sb[:], lng_in),
                         (lnb_sb[:], lnb_in), (b1_sb[:], b1_in), (b2_sb[:], b2_in),
                         (bdiag[:], bdiag_in), (bdiagT[:], bdiagT_in),
                         (ones1[:], ones1_in),
                         (rot_a[:, 0:1], iota_i_in), (rot_b[0:1, :], iota_t_in)]:
            sy.dma_start(out=dst, in_=src).then_inc(LD, 16)
            p.inc("LD", 16)
    p.sync(_)

    def _(g):
        g.dma_start(out=id_b[:], in_=ident_in).then_inc(GD, 16)
        g.dma_start(out=dup32[:], in_=dup32_in).then_inc(GD, 16)
        g.dma_start(out=s_bf.rearrange("p (c t) -> p c t", t=SW)[:, :, 1:SW],
                    in_=xn2_in.rearrange("(c p) t -> p c t", p=128)).then_inc(GD, 16)
    p.gpsimd(_)
    p.inc("GD", 48)

    def _(v):
        ins = None
        for c in range(8):
            ins = v.memset(s_bf[:, c * SW:c * SW + 1], 0.0)
        p.tick(ins, "DV")
        p.tick(v.memset(kh[:, 0:256], 0.0), "DV")
        p.tick(v.memset(onec_b[:], 1.0), "DV")
        p.tick(v.memset(onec_f[:], 1.0), "DV")
    p.vector(_)
    p.barrier()

    def _(t):
        ins = None
        for tc in range(2):
            ins = t.matmul(pA[:, (tc % 2) * 512:(tc % 2) * 512 + 512][0:32, :],
                           ones1[0:1, 0:32], rot_b[0:1, tc * 512:(tc + 1) * 512],
                           start=True, stop=True)
        p.tick(ins, "PE")
    p.tensor(_)
    pe_t = p.val("PE")

    def _(v, pe_t=pe_t):
        v.wait_ge(PE, pe_t)
        v.tensor_copy(rot_c[:, 0:512], pA[0:32, 0:512])
        v.tensor_copy(rot_c[:, 512:1024], pA[0:32, 512:1024])
        p.tick(v.engine_nop(), "DV")
    p.vector(_)
    def _(t):
        t.wait_ge(DV, p.val("DV"))
        ins = None
        for tc in range(2, 4):
            ins = t.matmul(pA[:, (tc % 2) * 512:(tc % 2) * 512 + 512][0:32, :],
                           ones1[0:1, 0:32], rot_b[0:1, tc * 512:(tc + 1) * 512],
                           start=True, stop=True)
        p.tick(ins, "PE")
    p.tensor(_)
    pe_t2 = p.val("PE")

    def _(v, pe_t2=pe_t2):
        v.wait_ge(PE, pe_t2)
        v.tensor_copy(rot_c[:, 1024:1536], pA[0:32, 0:512])
        p.fence(v, v.tensor_copy(rot_c[:, 1536:2048], pA[0:32, 512:1024]), "DV")
        p.fence(v, v.tensor_scalar_mul(icol[:, 0:1], rot_a[:, 0:1], 1.0 / 32.0), "DV")
        p.fence(v, v.tensor_scalar_mul(icol[:, 3:4], icol[:, 0:1], -0.00013529385118360437), "DV")
        p.fence(v, v.scalar_tensor_tensor(icol[:, 0:1], icol[:, 0:1], -9.210205078125,
                                          icol[:, 3:4], ALU.mult, ALU.add), "DV")
        p.fence(v, v.tensor_scalar(icol[:, 1:2], rot_a[:, 0:1], 5.0, 64.0,
                                   ALU.mult, ALU.add), "DV")
        p.fence(v, v.tensor_scalar_mul(icol[:, 3:4], icol[:, 1:2], -1.9956912337970589e-10), "DV")
        p.fence(v, v.scalar_tensor_tensor(icol[:, 1:2], icol[:, 1:2], 0.0044642859138548374,
                                          icol[:, 3:4], ALU.mult, ALU.add), "DV")
    p.vector(_)
    dv_r1 = p.val("DV")

    def _(s, dv_r1=dv_r1):
        s.wait_ge(DV, dv_r1)
        p.fence(s, s.activation(icol[:, 0:1], icol[:, 0:1], AF.Exp), "AC")
        p.tick(s.activation(icol[:, 2:3], icol[:, 1:2], AF.Ln), "AC")
    p.scalar(_)
    ac_r1 = p.val("AC")

    def _(v, ac_r1=ac_r1):
        v.wait_ge(AC, ac_r1)
        p.fence(v, v.tensor_scalar_mul(icol[:, 3:4], icol[:, 2:3], -1.0), "DV")
        p.fence(v, v.tensor_scalar_mul(rot_a[:], rot_c[:], icol[:, 0:1]), "DV")
        p.fence(v, v.tensor_scalar_mul(rot_b[:], rot_a[:], INV2PI), "DV")
        for qq in range(4):
            p.fence(v, v.tensor_copy(rint[:], rot_b[:, qq * 512:(qq + 1) * 512]), "DV")
            p.fence(v, v.tensor_copy(rot_b[:, qq * 512:(qq + 1) * 512], rint[:]), "DV")
        p.fence(v, v.scalar_tensor_tensor(rot_a[:], rot_b[:], -HI2PI, rot_a[:], ALU.mult, ALU.add), "DV")
        p.fence(v, v.scalar_tensor_tensor(rot_a[:], rot_b[:], -LO2PI, rot_a[:], ALU.mult, ALU.add), "DV")
        p.fence(v, v.tensor_single_scalar(rot_d[:], rot_a[:], float(np.pi), ALU.is_gt), "DV")
        p.fence(v, v.scalar_tensor_tensor(rot_a[:], rot_d[:], -2.0 * float(np.pi), rot_a[:], ALU.mult, ALU.add), "DV")
        p.tick(v.tensor_scalar(rot_c[:], rot_c[:], 1.0 / 128.0, -8.0, ALU.mult, ALU.add), "DV")
    p.vector(_)
    dv_r2 = p.val("DV")

    def _(s, dv_r2=dv_r2):
        s.wait_ge(DV, dv_r2)
        p.fence(s, s.activation(rot_b[:], rot_a[:], AF.Sin), "AC")
        p.fence(s, s.activation(rot_d[:], rot_a[:], AF.Sin, scale=0.5), "AC")
        p.fence(s, s.activation(rot_d[:], rot_d[:], AF.Square), "AC")
        p.fence(s, s.activation(rot_a[:], rot_c[:], AF.Exp, scale=icol[:, 2:3]), "AC")
        p.tick(s.activation(rot_c[:], rot_c[:], AF.Exp, scale=icol[:, 3:4]), "AC")
    p.scalar(_)
    ac_r2 = p.val("AC")

    def _(v, ac_r2=ac_r2):
        v.wait_ge(AC, ac_r2)
        p.fence(v, v.tensor_scalar(rot_d[:], rot_d[:], -2.0, 1.0, ALU.mult, ALU.add), "DV")
        p.fence(v, v.tensor_mul(tbl_cs[0:32, :], rot_d[:], rot_a[:]), "DV")
        p.fence(v, v.tensor_mul(tbl_ss[0:32, :], rot_b[:], rot_a[:]), "DV")
        p.fence(v, v.tensor_mul(tbl_ci[0:32, :], rot_d[:], rot_c[:]), "DV")
        p.tick(v.tensor_mul(tbl_si[0:32, :], rot_b[:], rot_c[:]), "DV")
    p.vector(_)
    dv_r3 = p.val("DV")

    if dbg:
        dv_now = p.val("DV")
        def _(g, dv_now=dv_now):
            g.wait_ge(p.sems["DV"], dv_now)
            g.dma_start(out=dbg_rot[0], in_=rot_a[:]).then_inc(p.sems["GD"], 16)
            g.dma_start(out=dbg_rot[1], in_=rot_b[:]).then_inc(p.sems["GD"], 16)
            g.dma_start(out=dbg_rot[2], in_=rot_c[:]).then_inc(p.sems["GD"], 16)
            g.dma_start(out=dbg_rot[3], in_=rot_d[:]).then_inc(p.sems["GD"], 16)
            g.dma_start(out=dbg_icol, in_=icol[:]).then_inc(p.sems["GD"], 16)
        p.gpsimd(_)
        p.inc("GD", 80)
        p.barrier()

    for ti, tb in enumerate([tbl_cs, tbl_ss, tbl_ci, tbl_si]):
        for tc in range(4):
            def _(t, tb=tb, tc=tc, ti=ti, dv_r3=dv_r3):
                t.wait_ge(DV, dv_r3)
                if ti > 0 or tc >= 2:
                    t.wait_ge(DV, p.cnt.get("_rotev", 0))
                p.tick(t.matmul(pA[:, (tc % 2) * 512:(tc % 2) * 512 + 512],
                                dup32[:], tb[0:32, tc * 512:(tc + 1) * 512],
                                start=True, stop=True), "PE")
            p.tensor(_)
            pe_d = p.val("PE")

            def _(v, tb=tb, tc=tc, pe_d=pe_d):
                v.wait_ge(PE, pe_d)
                p.tick(v.tensor_copy(tb[:, tc * 512:(tc + 1) * 512],
                                     pA[:, (tc % 2) * 512:(tc % 2) * 512 + 512]), "DV")
            p.vector(_)
            p.cnt["_rotev"] = p.val("DV")
    p.barrier()
    if dbg:
        def _(g):
            for ti, tb in enumerate([tbl_cs, tbl_ss, tbl_ci, tbl_si]):
                g.dma_start(out=dbg_t[ti], in_=tb[:]).then_inc(p.sems["GD"], 16)
            g.dma_start(out=dbg_s[0], in_=s_bf[:]).then_inc(p.sems["GD"], 16)
        p.gpsimd(_)
        p.inc("GD", 80)
        p.barrier()

    def shifted(kc, a, b):
        off = 0 if kc >= 4 else 1
        return s_bf[:, kc * SW + off + a:kc * SW + off + b]

    for li in range(6):
        for hp in range(8):
            for wi, col0 in enumerate([hp * 128, 1024 + hp * 128, 2048 + hp * 128]):
                def _(g, wi=wi, col0=col0, li=li, hp=hp):
                    if li > 0 or hp > 0:
                        g.wait_ge(PE, p.cnt.get("_wslmm%d" % wi, 0))
                    g.dma_start(out=wsl[wi].rearrange("p (kc n) -> p kc n", n=128),
                                in_=wqkv_in[li].rearrange("(kc p) n -> p kc n", p=128)[:, :, col0:col0 + 128]
                                ).then_inc(GD, 16)
                p.gpsimd(_)
            gd_w = p.inc("GD", 48)

            for qk, (wi, dstbuf, tbl_c, tbl_s, sc_col) in enumerate(
                    [(0, qh, tbl_cs, tbl_ss, qs_sb[:, li:li + 1]),
                     (1, kh, tbl_ci, tbl_si, ks_sb[:, li:li + 1])]):
                for tc in range(4):
                    bank = (tc % 2)
                    def _(t, wi=wi, tc=tc, bank=bank, gd_w=gd_w, qk=qk):
                        t.wait_ge(GD, gd_w)
                        t.wait_ge(DV, p.cnt.get("_qnev%d" % bank, 0))
                        ins = None
                        for kc in range(8):
                            ins = t.matmul(pA[:, bank * 512:bank * 512 + 512],
                                           wsl[wi][:, kc * 128:(kc + 1) * 128],
                                           shifted(kc, tc * 512, (tc + 1) * 512),
                                           start=(kc == 0), stop=(kc == 7))
                        p.tick(ins, "PE")
                    p.tensor(_)
                    pe_q = p.val("PE")

                    def _(s, pe_q=pe_q, bank=bank):
                        s.wait_ge(PE, pe_q)
                        p.tick(s.activation(sq_sb[:], pA[:, bank * 512:bank * 512 + 512],
                                            AF.Square), "AC")
                    p.scalar(_)
                    ac_sq = p.val("AC")

                    def _(t, ac_sq=ac_sq, bank=bank):
                        t.wait_ge(AC, ac_sq)
                        t.wait_ge(DV, p.cnt.get("_nbev", 0))
                        p.tick(t.matmul(pB[0:2, 0:512], bdiag[:], sq_sb[:],
                                        start=True, stop=True), "PE")
                    p.tensor(_)
                    pe_ss = p.val("PE")

                    def _(v, pe_ss=pe_ss):
                        v.wait_ge(PE, pe_ss)
                        p.tick(v.tensor_scalar_max(rowAB[0:2, :], pB[0:2, 0:512], 1e-24), "DV")
                    p.vector(_)
                    dv_ss = p.val("DV")

                    def _(s, dv_ss=dv_ss):
                        s.wait_ge(DV, dv_ss)
                        p.fence(s, s.activation(rowAB[0:2, :], rowAB[0:2, :], AF.Ln), "AC")
                        p.tick(s.activation(rowAB[0:2, :], rowAB[0:2, :], AF.Exp, scale=-0.5), "AC")
                    p.scalar(_)
                    ac_rf = p.val("AC")

                    def _(t, ac_rf=ac_rf):
                        t.wait_ge(AC, ac_rf)
                        p.tick(t.matmul(pB[:, 512:1024], bdiagT[:], rowAB[0:2, :],
                                        start=True, stop=True), "PE")
                    p.tensor(_)
                    pe_nb = p.val("PE")

                    def _(v, pe_nb=pe_nb, bank=bank, sc_col=sc_col, tc=tc,
                          dstbuf=dstbuf, tbl_c=tbl_c, tbl_s=tbl_s, qk=qk):
                        v.wait_ge(PE, pe_nb)
                        p.fence(v, v.tensor_copy(nb_sb[:], pB[:, 512:1024]), "DV")
                        p.cnt["_nbev"] = p.val("DV")
                        p.fence(v, v.scalar_tensor_tensor(qn_sb[:], pA[:, bank * 512:bank * 512 + 512],
                                                          sc_col, nb_sb[:], ALU.mult, ALU.mult), "DV")
                        p.cnt["_qnev%d" % bank] = p.val("DV")
                        p.fence(v, v.tensor_mul(t1_sb[:], qn_sb[:], tbl_c[:, tc * 512:(tc + 1) * 512]), "DV")
                        ins = None
                        for qrt in range(4):
                            r0 = qrt * 32
                            src_r0 = (qrt + 1) * 32 if qrt % 2 == 0 else (qrt - 1) * 32
                            sgn = -1.0 if qrt % 2 == 0 else 1.0
                            ins = v.tensor_scalar_mul(rr_sb[r0:r0 + 32, :],
                                                      qn_sb[src_r0:src_r0 + 32, :], sgn)
                        p.fence(v, ins, "DV")
                        p.fence(v, v.tensor_mul(rr_sb[:], rr_sb[:],
                                                tbl_s[:, tc * 512:(tc + 1) * 512]), "DV")
                        off = 256 if (qk == 1) else 0
                        dst = dstbuf[:, off + tc * 512:off + (tc + 1) * 512]
                        p.tick(v.tensor_add(dst, t1_sb[:], rr_sb[:]), "DV")
                    p.vector(_)

            for tt in range(16):
                bank = tt % 2
                def _(t, tt=tt, bank=bank, gd_w=gd_w):
                    t.wait_ge(GD, gd_w)
                    if tt >= 2:
                        t.wait_ge(DV, p.cnt["_vev%d" % (tt - 2)])
                    ins = None
                    for kc in range(8):
                        ins = t.matmul(pA[:, 1024 + bank * 512:1024 + bank * 512 + 128],
                                       shifted(kc, tt * 128, (tt + 1) * 128),
                                       wsl[2][:, kc * 128:(kc + 1) * 128],
                                       start=(kc == 0), stop=(kc == 7))
                    p.tick(ins, "PE")
                p.tensor(_)
                pe_v = p.val("PE")

                def _(v, tt=tt, bank=bank, pe_v=pe_v):
                    v.wait_ge(PE, pe_v)
                    p.tick(v.tensor_copy(vtm[:, tt * 128:(tt + 1) * 128],
                                         pA[:, 1024 + bank * 512:1024 + bank * 512 + 128]), "DV")
                p.vector(_)
                p.cnt["_vev%d" % tt] = p.val("DV")
            p.cnt["_wslmm0"] = p.val("PE")
            p.cnt["_wslmm1"] = p.val("PE")
            p.cnt["_wslmm2"] = p.val("PE")
            dv_vq = p.val("DV")

            for hh in range(2):
                h = hp * 2 + hh
                r0 = hh * 64
                for qi in range(16):
                    def _(t, qi=qi, r0=r0, dv_vq=dv_vq):
                        t.wait_ge(DV, dv_vq)
                        if True:
                            t.wait_ge(AC, p.cnt.get("_eexp", 0))
                        p.tick(t.matmul(pA[:, 0:384], qh[r0:r0 + 64, qi * 128:(qi + 1) * 128],
                                        kh[r0:r0 + 64, qi * 128:qi * 128 + 384],
                                        start=True, stop=True), "PE")
                    p.tensor(_)
                    pe_s = p.val("PE")

                    def _(v, qi=qi, pe_s=pe_s):
                        v.wait_ge(PE, pe_s)
                        v.wait_ge(AC, p.cnt.get("_eexp", 0))
                        mo = 384 if qi == 0 else (768 if qi == 1 else 0)
                        p.fence(v, v.tensor_add(pB[:, 0:384], pA[:, 0:384],
                                                maskb[:, mo:mo + 384]), "DV")
                        p.fence(v, v.reduce_max(mxx[:, 0:1], pB[:, 0:384], axis=AX.X), "DV")
                        p.tick(v.tensor_scalar_mul(mxx[:, 1:2], mxx[:, 0:1], -8.0), "DV")
                    p.vector(_)
                    dv_mx = p.val("DV")

                    def _(s, dv_mx=dv_mx):
                        s.wait_ge(DV, dv_mx)
                        p.tick(s.activation(psb[:], pB[:, 0:384], AF.Exp, scale=8.0,
                                            bias=mxx[:, 1:2], accum_out=rs[:, 0:1]), "AC")
                    p.scalar(_)
                    p.cnt["_eexp"] = p.val("AC")
                    ac_e = p.val("AC")

                    def _(v, ac_e=ac_e):
                        v.wait_ge(AC, ac_e)
                        p.fence(v, v.reciprocal(rs[:, 1:2], rs[:, 0:1]), "DV")
                        p.tick(v.tensor_scalar_mul(psb[:], psb[:], rs[:, 1:2]), "DV")
                    p.vector(_)
                    dv_pn = p.val("DV")

                    def _(t, dv_pn=dv_pn):
                        t.wait_ge(DV, dv_pn)
                        t.wait_ge(DV, p.cnt.get("_ptev_e", 0))
                        ins = None
                        for kt in range(3):
                            ins = t.transpose(pb0[:, kt * 128:(kt + 1) * 128],
                                              psb[:, kt * 128:(kt + 1) * 128], id_b[:])
                        p.tick(ins, "PE")
                    p.tensor(_)
                    pe_pt = p.val("PE")

                    def _(v, pe_pt=pe_pt):
                        v.wait_ge(PE, pe_pt)
                        p.tick(v.tensor_copy(pts[:], pb0[:, 0:384]), "DV")
                    p.vector(_)
                    p.cnt["_ptev_e"] = p.val("DV")
                    dv_pt = p.val("DV")

                    def _(t, qi=qi, hh=hh, dv_pt=dv_pt):
                        t.wait_ge(DV, dv_pt)
                        t.wait_ge(DV, p.cnt.get("_oev_e", 0))
                        first = True
                        ins = None
                        for kt in range(3):
                            vi = qi - 2 + kt
                            if vi < 0:
                                continue
                            ins = t.matmul(pB[0:64, 512:640],
                                           vtm[:, vi * 128 + hh * 64:vi * 128 + hh * 64 + 64],
                                           pts[:, kt * 128:(kt + 1) * 128],
                                           start=first, stop=(kt == 2))
                            first = False
                        p.tick(ins, "PE")
                    p.tensor(_)
                    pe_o = p.val("PE")

                    def _(v, h=h, qi=qi, pe_o=pe_o):
                        v.wait_ge(PE, pe_o)
                        c = h // 2
                        rr = (h % 2) * 64
                        p.tick(v.tensor_copy(oat[rr:rr + 64, c * T + qi * 128:c * T + (qi + 1) * 128],
                                             pB[0:64, 512:640]), "DV")
                    p.vector(_)
                    p.cnt["_oev_e"] = p.val("DV")
            p.barrier()

        for oc in range(8):
            def _(g, oc=oc, li=li):
                if oc >= 1:
                    g.wait_ge(PE, p.cnt["_womm%d" % (oc - 1)])
                g.dma_start(out=wsl[0].rearrange("p (kc n) -> p kc n", n=128),
                            in_=wo_in[li].rearrange("(kc p) n -> p kc n", p=128)[:, :, oc * 128:(oc + 1) * 128]
                            ).then_inc(GD, 16)
            p.gpsimd(_)
            gd_wo = p.inc("GD", 16)

            for tc in range(4):
                bank = tc % 2
                def _(t, oc=oc, tc=tc, bank=bank, gd_wo=gd_wo):
                    t.wait_ge(GD, gd_wo)
                    t.wait_ge(DV, p.cnt.get("_woadd%d" % bank, 0))
                    ins = None
                    for kc in range(8):
                        ins = t.matmul(pA[:, bank * 512:bank * 512 + 512],
                                       wsl[0][:, kc * 128:(kc + 1) * 128],
                                       oat[:, kc * T + tc * 512:kc * T + (tc + 1) * 512],
                                       start=(kc == 0), stop=(kc == 7))
                    p.tick(ins, "PE")
                p.tensor(_)
                if tc == 3:
                    p.cnt["_womm%d" % oc] = p.val("PE")
                pe_wo = p.val("PE")

                def _(v, oc=oc, tc=tc, bank=bank, pe_wo=pe_wo):
                    v.wait_ge(PE, pe_wo)
                    dst = s_bf[:, oc * SW + 1 + tc * 512:oc * SW + 1 + (tc + 1) * 512]
                    p.tick(v.tensor_add(dst, dst, pA[:, bank * 512:bank * 512 + 512]), "DV")
                p.vector(_)
                p.cnt["_woadd%d" % bank] = p.val("DV")
        p.barrier()

        onesb_col = id_b
        for tc in range(4):
            def _(t, tc=tc, li=li):
                t.wait_ge(DV, p.cnt.get("_xnev%d" % ((tc - 1) % 4), 0) if tc >= 1 else 0)
                ins = None
                for kc in range(8):
                    ins = t.matmul(pB[0:1, 0:512], onec_b[:, 0:1],
                                   shifted(kc, tc * 512, (tc + 1) * 512),
                                   start=(kc == 0), stop=(kc == 7))
                p.tick(ins, "PE")
            p.tensor(_)
            pe_sm = p.val("PE")

            for kc in range(8):
                def _(s, kc=kc, tc=tc, pe_sm=pe_sm):
                    if kc == 0:
                        s.wait_ge(PE, pe_sm)
                    else:
                        s.wait_ge(PE, p.val("PE"))
                    p.tick(s.activation(sq_sb[:], shifted(kc, tc * 512, (tc + 1) * 512),
                                        AF.Square), "AC")
                p.scalar(_)
                ac_q = p.val("AC")

                def _(t, kc=kc, ac_q=ac_q):
                    t.wait_ge(AC, ac_q)
                    p.tick(t.matmul(pB[0:1, 512:1024], onec_f[:, 0:1], sq_sb[:],
                                    start=(kc == 0), stop=(kc == 7)), "PE")
                p.tensor(_)
            pe_sq = p.val("PE")

            def _(v, pe_sq=pe_sq):
                v.wait_ge(PE, pe_sq)
                p.fence(v, v.tensor_scalar_mul(rowA[0:1, :], pB[0:1, 0:512], 1.0 / D), "DV")
                p.fence(v, v.tensor_tensor(rowC[0:1, :], rowA[0:1, :], rowA[0:1, :], ALU.mult), "DV")
                p.tick(v.scalar_tensor_tensor(rowB[0:1, :], pB[0:1, 512:1024], 1.0 / D,
                                              rowC[0:1, :], ALU.mult, ALU.subtract), "DV")
            p.vector(_)
            dv_st = p.val("DV")

            def _(s, dv_st=dv_st):
                s.wait_ge(DV, dv_st)
                p.fence(s, s.activation(rowB[0:1, :], rowB[0:1, :], AF.Ln, bias=p.C(1e-5)[0:1, 0:1]), "AC")
                p.tick(s.activation(rowB[0:1, :], rowB[0:1, :], AF.Exp, scale=-0.5), "AC")
            p.scalar(_)
            ac_rs = p.val("AC")

            def _(v, ac_rs=ac_rs):
                v.wait_ge(AC, ac_rs)
                p.fence(v, v.scalar_tensor_tensor(rowC[0:1, :], rowA[0:1, :], -1.0,
                                                  rowB[0:1, :], ALU.mult, ALU.mult), "DV")
                p.tick(v.engine_nop(), "DV")
            p.vector(_)
            dv_nm = p.val("DV")

            def _(t, dv_nm=dv_nm):
                t.wait_ge(DV, dv_nm)
                t.matmul(pB[:, 0:512], ones1[0:1, :], rowB[0:1, :], start=True, stop=True)
                p.tick(t.matmul(pB[:, 512:1024], ones1[0:1, :], rowC[0:1, :],
                                start=True, stop=True), "PE")
            p.tensor(_)
            pe_bc = p.val("PE")

            for kc in range(8):
                def _(v, kc=kc, tc=tc, li=li, pe_bc=pe_bc):
                    v.wait_ge(PE, pe_bc)
                    p.fence(v, v.tensor_mul(t1_sb[:], shifted(kc, tc * 512, (tc + 1) * 512),
                                            pB[:, 0:512]), "DV")
                    p.fence(v, v.tensor_add(t1_sb[:], t1_sb[:], pB[:, 512:1024]), "DV")
                    p.fence(v, v.tensor_scalar(xn_i[:, kc * T + tc * 512:kc * T + (tc + 1) * 512],
                                               t1_sb[:], lng_sb[:, li * 8 + kc:li * 8 + kc + 1],
                                               lnb_sb[:, li * 8 + kc:li * 8 + kc + 1],
                                               ALU.mult, ALU.add), "DV")
                p.vector(_)
            p.cnt["_xnev%d" % tc] = p.val("DV")
        p.barrier()

        for th in range(2):
            for j in range(22):
                wj = 128 if j < 21 else 42
                def _(g, j=j, li=li, wj=wj):
                    g.wait_ge(PE, p.cnt.get("_w1mm", 0))
                    g.dma_start(out=wsl[0].rearrange("p (kc n) -> p kc n", n=128)[:, :, 0:wj],
                                in_=w1_in[li].rearrange("(kc p) n -> p kc n", p=128)[:, :, j * 128:j * 128 + wj]
                                ).then_inc(GD, 16)
                    g.dma_start(out=wsl[1].rearrange("p (kc n) -> p kc n", n=128)[:, :, 0:wj],
                                in_=w1_in[li].rearrange("(kc p) n -> p kc n", p=128)[:, :, 2730 + j * 128:2730 + j * 128 + wj]
                                ).then_inc(GD, 16)
                p.gpsimd(_)
                gd_w1 = p.inc("GD", 32)

                for tc in range(2):
                    ta = th * 1024 + tc * 512
                    def _(t, j=j, tc=tc, ta=ta, gd_w1=gd_w1, wj=wj):
                        t.wait_ge(GD, gd_w1)
                        t.wait_ge(DV, p.cnt.get("_h2ev", 0))
                        ins = None
                        for kc in range(8):
                            ins = t.matmul(pA[0:wj, 0:512], wsl[0][:, kc * 128:kc * 128 + wj],
                                           xn_i[:, kc * T + ta:kc * T + ta + 512],
                                           start=(kc == 0), stop=(kc == 7))
                        for kc in range(8):
                            ins = t.matmul(pA[0:wj, 512:1024], wsl[1][:, kc * 128:kc * 128 + wj],
                                           xn_i[:, kc * T + ta:kc * T + ta + 512],
                                           start=(kc == 0), stop=(kc == 7))
                        p.tick(ins, "PE")
                    p.tensor(_)
                    p.cnt["_w1mm"] = p.val("PE")
                    pe_w1 = p.val("PE")

                    def _(s, j=j, li=li, wj=wj, pe_w1=pe_w1):
                        s.wait_ge(PE, pe_w1)
                        p.tick(s.activation(gel[0:wj, :], pA[0:wj, 512:1024], AF.Gelu,
                                            bias=b1_sb[:, li * 44 + 22 + j:li * 44 + 22 + j + 1][0:wj, :]), "AC")
                    p.scalar(_)
                    ac_g = p.val("AC")

                    def _(v, j=j, li=li, tc=tc, wj=wj, ac_g=ac_g):
                        v.wait_ge(AC, ac_g)
                        p.tick(v.scalar_tensor_tensor(
                            h2[0:wj, j * 1024 + tc * 512:j * 1024 + tc * 512 + 512],
                            pA[0:wj, 0:512], b1_sb[:, li * 44 + j:li * 44 + j + 1][0:wj, :],
                            gel[0:wj, :], ALU.add, ALU.mult), "DV")
                    p.vector(_)
                    p.cnt["_h2ev"] = p.val("DV")
            dv_h2 = p.val("DV")

            for oc in range(8):
                def _(g, oc=oc, li=li):
                    g.wait_ge(PE, p.cnt.get("_w2mm", 0))
                    g.dma_start(out=w2sl.rearrange("p (kc n) -> p kc n", n=128)[:, 0:21, :],
                                in_=w2_in[li][0:2688].rearrange("(kc p) n -> p kc n", p=128)[:, :, oc * 128:(oc + 1) * 128]
                                ).then_inc(GD, 16)
                    g.dma_start(out=w2sl[0:42, 21 * 128:22 * 128],
                                in_=w2_in[li][2688:2730, oc * 128:(oc + 1) * 128]).then_inc(GD, 16)
                p.gpsimd(_)
                gd_w2 = p.inc("GD", 32)

                for tc in range(2):
                    bank = tc
                    ta = th * 1024 + tc * 512
                    def _(t, oc=oc, tc=tc, bank=bank, ta=ta, gd_w2=gd_w2, dv_h2=dv_h2):
                        t.wait_ge(GD, gd_w2)
                        t.wait_ge(DV, dv_h2)
                        t.wait_ge(DV, p.cnt.get("_w2add%d" % bank, 0))
                        ins = None
                        for j in range(22):
                            wj = 128 if j < 21 else 42
                            ins = t.matmul(pA[:, 1024 + bank * 512:1024 + bank * 512 + 512],
                                           w2sl[0:wj, j * 128:(j + 1) * 128],
                                           h2[0:wj, j * 1024 + tc * 512:j * 1024 + tc * 512 + 512],
                                           start=(j == 0), stop=(j == 21))
                        p.tick(ins, "PE")
                    p.tensor(_)
                    p.cnt["_w2mm"] = p.val("PE")
                    pe_w2 = p.val("PE")

                    def _(v, oc=oc, li=li, ta=ta, bank=bank, pe_w2=pe_w2):
                        v.wait_ge(PE, pe_w2)
                        dst = s_bf[:, oc * SW + 1 + ta:oc * SW + 1 + ta + 512]
                        p.tick(v.scalar_tensor_tensor(dst, pA[:, 1024 + bank * 512:1024 + bank * 512 + 512],
                                                      b2_sb[:, li * 8 + oc:li * 8 + oc + 1],
                                                      dst, ALU.add, ALU.add), "DV")
                    p.vector(_)
                    p.cnt["_w2add%d" % bank] = p.val("DV")
            p.barrier()
        if dbg:
            def _(g, li=li):
                g.dma_start(out=dbg_s[li + 1], in_=s_bf[:]).then_inc(p.sems["GD"], 16)
            p.gpsimd(_)
            p.inc("GD", 16)
            p.barrier()

    for c in range(8):
        def _(sy, c=c):
            if c >= 1:
                sy.wait_ge(PE, p.cnt.get("_eptr%d" % (c - 1), 0))
            sy.dma_start(out=x2st[:], in_=x2p_in[c * 128:(c + 1) * 128, :]).then_inc(LD, 16)
        p.sync(_)
        ld_x = p.inc("LD", 16)

        def _(v, c=c, ld_x=ld_x):
            v.wait_ge(LD, ld_x)
            if c >= 1:
                v.wait_ge(PE, p.cnt.get("_eptr%d" % (c - 1), 0))
            p.tick(v.tensor_add(fst[:], x2st[:], s_bf[:, c * SW + 1:c * SW + 1 + T]), "DV")
        p.vector(_)
        dv_f = p.val("DV")

        for tg in range(2):
            def _(t, c=c, tg=tg, dv_f=dv_f):
                t.wait_ge(DV, dv_f)
                if c > 0 or tg > 0:
                    t.wait_ge(DV, p.cnt.get("_epev", 0))
                ins = None
                for tt in range(8):
                    ti = tg * 8 + tt
                    ins = t.transpose(pA[:, tt * 128:(tt + 1) * 128] if tt < 4 else pA[:, 512 + (tt - 4) * 128:512 + (tt - 3) * 128],
                                      fst[:, ti * 128:(ti + 1) * 128], id_f[:])
                p.tick(ins, "PE")
            p.tensor(_)
            p.cnt["_eptr%d" % c] = p.val("PE")
            pe_ep = p.val("PE")

            def _(v, pe_ep=pe_ep):
                v.wait_ge(PE, pe_ep)
                v.wait_ge(ST, p.cnt.get("_epst", 0))
                v.tensor_copy(x2st[:, 0:512], pA[:, 0:512])
                p.tick(v.tensor_copy(x2st[:, 512:1024], pA[:, 512:1024]), "DV")
            p.vector(_)
            p.cnt["_epev"] = p.val("DV")
            dv_ep = p.val("DV")

            def _(sy, c=c, tg=tg, dv_ep=dv_ep):
                sy.wait_ge(DV, dv_ep)
                sy.dma_start(out=out_tm.rearrange("(g q) d -> q g d", q=128)[:, tg * 8:(tg + 1) * 8, c * 128:(c + 1) * 128] if False else
                             out_tm[tg * 1024:(tg + 1) * 1024, c * 128:(c + 1) * 128].rearrange("(w q) d -> q w d", q=128),
                             in_=x2st[:, 0:1024].rearrange("p (w d) -> p w d", d=128)).then_inc(ST, 16)
            p.sync(_)
            p.cnt["_epst"] = p.inc("ST", 16)

    def _(sy):
        sy.wait_ge(ST, p.val("ST"))
    p.sync(_)

    if standalone:
        p.end()
        return p.nc
    return None


def build_fused(dbg=False):
    p = Prog()
    I32 = mybir.dt.int32
    ikind = "ExternalOutput" if dbg else "Internal"

    ext = {}
    for name, shape in [
        ("x", [T, D]), ("ident", [128, 128]), ("sel", [32, 32 * 128]),
        ("in_proj_w", [4096, 1024]), ("conv_w", [2048, 1, 4]), ("conv_b", [128, 16]),
        ("x_proj_w", [96, 2048]), ("dt_proj_w", [2048, 64]), ("dt_proj_b", [128, 16]),
        ("A_log", [2048, 16]), ("D_par", [128, 16]), ("out_proj_w", [1024, 2048]),
        ("w_gate", [2816, 1024]), ("w_up", [2816, 1024]), ("w_down", [1024, 2816]),
        ("E64", [64, 1024]), ("E32", [32, 512]), ("E44", [44, 704]),
        ("w_attn", [1024, 1152]), ("w_proj", [1024, 1024]), ("mask_diag", [128, 128]),
        ("Wqkv", [6, 1024, 3072]), ("Wo", [6, 1024, 1024]),
        ("W1", [6, 1024, 5460]), ("W2", [6, 2730, 1024]),
        ("q_scale_r", [128, 6]), ("k_scale_r", [128, 6]),
        ("ln_g_r", [128, 48]), ("ln_b_r", [128, 48]), ("b1_r", [128, 6 * 44]),
        ("b2_r", [128, 48]), ("mask_band", [128, 384]), ("mask_band_t0", [128, 384]),
        ("mask_band_t1", [128, 384]), ("iota_t", [1, T]), ("iota_i", [32, 1]),
        ("dup32", [32, 128]), ("bdiag", [128, 2]), ("bdiagT", [2, 128]),
        ("ones1", [1, 128]),
    ]:
        ext[name] = p.dram(name, shape, F32, kind="ExternalInput")
    out_tm = p.dram("out_tm", [T, D], F32, kind="ExternalOutput")

    xfm_d = p.dram("xfm_d", [D, T], F32, kind=ikind)
    xnbf_d = p.dram("xnbf_d", [D, T], F32, kind=ikind)
    xntm_d = p.dram("xntm_d", [T, D], F32, kind=ikind)
    mam_d = p.dram("mam_d", [D, T], F32, kind=ikind)
    x1_d = p.dram("x1_d", [D, T], F32, kind=ikind)
    xn2_d = p.dram("xn2_d", [D, T], F32, kind=ikind)
    x2p_d = p.dram("x2p_d", [D, T], F32, kind=ikind)

    pA = p.psum("pA", [128, 2048], F32)
    pB = p.psum("pB", [128, 1024], F32)
    pb0 = p.psum("pb0", [128, 1024], BF16)
    pb1 = p.psum("pb1", [128, 1024], BF16)
    ps = {"pA": pA, "pB": pB, "pb0": pb0, "pb1": pb1}

    G1f = p.sbuf("G1", [128, 16388], F32)
    G1b = G1f[:].bitcast(BF16)
    G2b = p.sbuf("G2", [128, 16384], BF16)
    G2f = G2b[:].bitcast(F32)
    G2i = G2b[:].bitcast(I32)
    G3b = p.sbuf("G3", [128, 30720], BF16)
    G4f = p.sbuf("G4", [128, 8448], F32)
    G4b = G4f[:].bitcast(BF16)
    G5b = p.sbuf("G5", [128, 8192], BF16)
    G5f = G5b[:].bitcast(F32)
    id_f = p.sbuf("gid_f", [128, 128], F32)
    id_b = p.sbuf("gid_b", [128, 128], BF16)

    p.begin(consts=[1e-5, 1.0])

    def _(sy):
        sy.dma_start(out=id_f[:], in_=ext["ident"]).then_inc(p.sems["LD"], 16)
    p.sync(_)
    p.inc("LD", 16)

    def _(g):
        g.dma_start(out=id_b[:], in_=ext["ident"]).then_inc(p.sems["GD"], 16)
    p.gpsimd(_)
    p.inc("GD", 16)

    ioA = {"pA": pA, "idf": id_f,
           "x": ext["x"], "ident": ext["ident"],
           "x_fm": xfm_d, "xn_bf": xnbf_d, "xn_tm": xntm_d}
    for i in range(3):
        ioA[f"stg{i}"] = G1f[:, i * 1024:(i + 1) * 1024]
        ioA[f"xntm{i}"] = G1f[:, 3072 + i * 1024:3072 + (i + 1) * 1024]
        ioA[f"evac{i}"] = G1f[:, 6144 + i * 1024:6144 + (i + 1) * 1024]
        ioA[f"evb{i}"] = G1f[:, 9216 + i * 1024:9216 + (i + 1) * 1024]
        ioA[f"stats{i}"] = G1f[:, 12288 + i * 8:12288 + (i + 1) * 8]

    build_A(p=p, io=ioA)

    ioB = dict(ps)
    ioB.update({"id_b": id_b, "xn_bf": xnbf_d, "mam_out": mam_d,
                "in_proj_w": ext["in_proj_w"], "conv_w": ext["conv_w"],
                "conv_b": ext["conv_b"], "x_proj_w": ext["x_proj_w"],
                "dt_proj_w": ext["dt_proj_w"], "dt_proj_b": ext["dt_proj_b"],
                "A_log": ext["A_log"], "D_par": ext["D_par"],
                "out_proj_w": ext["out_proj_w"], "ident": ext["ident"],
                "sel": ext["sel"],
                "xnb": G2b[:, 0:16384],
                "opT": G1b[:, 0:16384],
                "z_bf": G1b[:, 16384:24576],
                "xa_bf": G1b[:, 24576:32768],
                "arena": G3b[:, 0:24576],
                "selb": G3b[0:32, 24576:28672],
                "ipT0": G3b[:, 28672:29696], "ipT1": G3b[:, 29696:30720],
                "dl_bf": G5b[0:64, 0:512], "bc_bf": G5b[0:32, 512:1024],
                "tails": G5b[:, 1024:1072], "u_dc": G5b[:, 1072:1584],
                "dtT": G5b[0:64, 1584:3632], "xpjT": G5b[:, 3632:5168],
                "v0": G4f[:, 0:512], "sp_ab": G4f[:, 512:1024],
                "sp_l": G4f[:, 1024:1536], "delta_dc": G4f[:, 1536:2048],
                "xc0": G4f[:, 2048:2560], "xc1": G4f[:, 2560:3072],
                "ytmp": G4f[:, 3072:3584], "yacc": G4f[:, 3584:4096],
                "ostage0": G4f[:, 4096:4608],
                "dA0": G4f[:, 4608:5120], "dA1": G4f[:, 5120:5632],
                "dBu0": G4f[:, 5632:6144], "dBu1": G4f[:, 6144:6656],
                "hh": G4f[:, 6656:7168],
                "convw": G4f[:, 7168:7232], "convb": G4f[:, 7232:7248],
                "dtbc": G4f[:, 7248:7264], "dcol": G4f[:, 7264:7280],
                "a_sb": G4f[:, 7280:7536], "hstate": G4f[:, 7536:7792]})
    build_B(p=p, io=ioB)

    ioC = dict(ps)
    ioC.update({"id_f": id_f, "id_b": id_b,
                "xn_tm": xntm_d, "x_fm": xfm_d, "mam_out": mam_d, "x1_fm": x1_d,
                "w_gate": ext["w_gate"], "w_up": ext["w_up"], "w_down": ext["w_down"],
                "ident": ext["ident"], "E64": ext["E64"], "E32": ext["E32"],
                "E44": ext["E44"],
                "xf": G1f[:, 0:16384],
                "e64": G5f[0:64, 0:1024], "e32": G5f[0:32, 1024:1536],
                "e44": G5f[0:44, 1536:2240],
                "q_bm": G4f[:, 0:128], "q_rc": G4f[:, 128:256],
                "q_bmT": G4f[:, 256:384], "q_rcT": G4f[:, 384:512],
                "q_ax": G4f[:, 512:1536], "q_accA": G4f[:, 1536:2560],
                "q_accB": G4f[:, 2560:3584], "q_sg": G4f[:, 3584:4608],
                "wstage": G4f[:, 4608:7424], "xnst": G4f[:, 4608:5632],
                "sgate": G4f[:, 7424:7936], "h2f": G4f[:, 7936:8448],
                "wq_nat": G2b[:, 0:2816], "xqt": G2b[:, 2816:3840],
                "hqt": G2b[:, 3840:4352], "wqg": G2b[:, 4352:8448],
                "wqu": G2b[:, 8448:12544], "wdTs": G2b[:, 12544:15360],
                "xq_fm": G3b[:, 0:8192], "hq_fm": G3b[:, 8192:30720]})
    build_C(p=p, io=ioC)

    ioD = dict(ps)
    ioD.update({"id_f": id_f, "id_b": id_b,
                "x1_fm": x1_d, "xn2_bf": xn2_d, "x2p_fm": x2p_d,
                "w_attn": ext["w_attn"], "w_proj": ext["w_proj"],
                "ident": ext["ident"], "mask_diag": ext["mask_diag"],
                "xf": G1f[:, 0:16384],
                "xn2": G2b[:, 0:16384],
                "oat": G3b[:, 0:16384], "psb": G3b[:, 16384:18432],
                "pts": G3b[:, 18432:20480], "kfm": G3b[0:64, 20480:22528],
                "vtm": G3b[:, 22528:23552], "qbf": G3b[0:64, 23552:25600],
                "wslq": G3b[:, 25600:26112], "wslk": G3b[:, 26112:26624],
                "wslv": G3b[:, 26624:27136],
                "xtm": G4f[:, 0:1024], "xntm": G4f[:, 1024:2048],
                "xn2f": G4f[:, 2048:3072], "stats": G4f[:, 3072:3080],
                "rs": G4f[:, 3080:3088], "mx": G4f[:, 3088:3092],
                "ostage": G4f[:, 3092:3604], "maskd": G4f[:, 3604:3732],
                "wpj": G4b[:, 7464:15656]})
    build_D(p=p, io=ioD)

    ioE = dict(ps)
    ioE.update({"id_f": id_f, "id_b": id_b,
                "xn2_bf": xn2_d, "x2p_fm": x2p_d, "out_tm": out_tm,
                "Wqkv": ext["Wqkv"], "Wo": ext["Wo"], "W1": ext["W1"], "W2": ext["W2"],
                "q_scale_r": ext["q_scale_r"], "k_scale_r": ext["k_scale_r"],
                "ln_g_r": ext["ln_g_r"], "ln_b_r": ext["ln_b_r"],
                "b1_r": ext["b1_r"], "b2_r": ext["b2_r"], "ident": ext["ident"],
                "mask_band": ext["mask_band"], "mask_band_t0": ext["mask_band_t0"],
                "mask_band_t1": ext["mask_band_t1"], "iota_t": ext["iota_t"],
                "iota_i": ext["iota_i"], "dup32": ext["dup32"],
                "bdiag": ext["bdiag"], "bdiagT": ext["bdiagT"], "ones1": ext["ones1"],
                "s_bf": G1b[:, 0:16392],
                "xn_i": G1b[:, 16392:32776],
                "tbl_cs": G5b[:, 0:2048], "tbl_ss": G5b[:, 2048:4096],
                "tbl_ci": G5b[:, 4096:6144], "tbl_si": G5b[:, 6144:8192],
                "af32": G4f[:, 0:8192], "maskb": G2f[:, 4400:5552],
                "arena_e": G3b[:, 0:28672],
                "psb_e": G3b[:, 28672:29056], "pts_e": G3b[:, 29056:29440],
                "w2sl": G2b[:, 0:2816], "wsl0": G2b[:, 2816:3840],
                "wsl1": G2b[:, 3840:4864], "wsl2": G2b[:, 4864:5888],
                "b1_sb": G2f[:, 2944:3208],
                "qs_sb": G2f[:, 3208:3214], "ks_sb": G2f[:, 3214:3220],
                "lng_sb": G2f[:, 3220:3268], "lnb_sb": G2f[:, 3268:3316],
                "b2_sb": G2f[:, 3316:3364], "rs_e": G2f[:, 3364:3372],
                "mx_e": G2f[:, 3372:3376], "bdiag_s": G2f[:, 3376:3378],
                "onec_f": G2f[:, 3378:3379],
                "bdiagT_s": G2f[0:2, 3379:3507], "ones1_s": G2f[0:1, 3507:3635],
                "dup32_s": G2b[0:32, 8576:8704], "icol": G2f[0:32, 3763:3767],
                "rint": G2i[0:32, 3767:4279],
                "onec_b": G2b[:, 8704:8705]})
    build_E(p=p, io=ioE, dbg=dbg)

    p.end()
    return p.nc


def prep_shared(inputs, c):
    m = dict(prep_E(inputs, c))
    m.update({
        "sel": c["sel"],
        "in_proj_w": inputs["L0_attn_in_proj_w"],
        "conv_w": inputs["L0_attn_conv_w"],
        "conv_b": np.ascontiguousarray(np.asarray(inputs["L0_attn_conv_b"]).reshape(16, 128).T),
        "x_proj_w": inputs["L0_attn_x_proj_w"],
        "dt_proj_w": inputs["L0_attn_dt_proj_w"],
        "dt_proj_b": np.ascontiguousarray(np.asarray(inputs["L0_attn_dt_proj_b"]).reshape(16, 128).T),
        "A_log": inputs["L0_attn_A_log"],
        "D_par": np.ascontiguousarray(np.asarray(inputs["L0_attn_D"]).reshape(16, 128).T),
        "out_proj_w": inputs["L0_attn_out_proj_w"],
        "w_gate": inputs["L0_ffn_w_gate"], "w_up": inputs["L0_ffn_w_up"],
        "w_down": inputs["L0_ffn_w_down"],
        "E64": c["E64"], "E32": c["E32"], "E44": c["E44"],
        "w_attn": inputs["L1_attn_w_attn"], "w_proj": inputs["L1_attn_w_proj"],
        "mask_diag": c["mask_diag"],
    })
    return m


def kernel(**inputs):
    NCORES = 8
    c = host_constants()
    x_full = np.ascontiguousarray(np.asarray(inputs["L0_attn_x"]))
    cores = list(range(NCORES))
    shared = prep_shared(inputs, c)
    nc = build_fused()
    maps = []
    for i in range(NCORES):
        m = dict(shared)
        m["x"] = np.ascontiguousarray(x_full[i])
        maps.append(m)
    res = run_bass_kernel_spmd(nc, maps, core_ids=cores).results
    out = np.stack([res[i]["out_tm"] for i in range(NCORES)], axis=0)
    return out.astype(np.float32)


def kernel_multilaunch(**inputs):
    NCORES = 8
    c = host_constants()
    x_full = np.ascontiguousarray(np.asarray(inputs["L0_attn_x"]))
    cores = list(range(NCORES))

    ncA = build_A()
    mapsA = [{"x": np.ascontiguousarray(x_full[i]), "ident": c["ident"]} for i in range(NCORES)]
    resA = run_bass_kernel_spmd(ncA, mapsA, core_ids=cores).results

    ncB = build_B()
    mapsB = []
    for i in range(NCORES):
        mapsB.append({
            "xn_bf": resA[i]["xn_bf"],
            "in_proj_w": inputs["L0_attn_in_proj_w"],
            "conv_w": inputs["L0_attn_conv_w"],
            "conv_b": np.ascontiguousarray(np.asarray(inputs["L0_attn_conv_b"]).reshape(16, 128).T),
            "x_proj_w": inputs["L0_attn_x_proj_w"],
            "dt_proj_w": inputs["L0_attn_dt_proj_w"],
            "dt_proj_b": np.ascontiguousarray(np.asarray(inputs["L0_attn_dt_proj_b"]).reshape(16, 128).T),
            "A_log": inputs["L0_attn_A_log"],
            "D_par": np.ascontiguousarray(np.asarray(inputs["L0_attn_D"]).reshape(16, 128).T),
            "out_proj_w": inputs["L0_attn_out_proj_w"],
            "ident": c["ident"], "sel": c["sel"],
        })
    resB = run_bass_kernel_spmd(ncB, mapsB, core_ids=cores).results

    ncC = build_C()
    mapsC = []
    for i in range(NCORES):
        mapsC.append({
            "xn_tm": resA[i]["xn_tm"],
            "x_fm": resA[i]["x_fm"],
            "mam_out": resB[i]["mam_out"],
            "w_gate": inputs["L0_ffn_w_gate"], "w_up": inputs["L0_ffn_w_up"],
            "w_down": inputs["L0_ffn_w_down"],
            "ident": c["ident"], "E64": c["E64"], "E32": c["E32"], "E44": c["E44"],
        })
    resC = run_bass_kernel_spmd(ncC, mapsC, core_ids=cores).results

    ncD = build_D()
    mapsD = []
    for i in range(NCORES):
        mapsD.append({
            "x1_fm": resC[i]["x1_fm"],
            "w_attn": inputs["L1_attn_w_attn"], "w_proj": inputs["L1_attn_w_proj"],
            "ident": c["ident"], "mask_diag": c["mask_diag"],
        })
    resD = run_bass_kernel_spmd(ncD, mapsD, core_ids=cores).results

    ncE = build_E()
    base = prep_E(inputs, c)
    mapsE = []
    for i in range(NCORES):
        m = dict(base)
        m["xn2_bf"] = resD[i]["xn2_bf"]
        m["x2p_fm"] = resD[i]["x2p_fm"]
        mapsE.append(m)
    resE = run_bass_kernel_spmd(ncE, mapsE, core_ids=cores).results

    out = np.stack([resE[i]["out_tm"] for i in range(NCORES)], axis=0)
    return out.astype(np.float32)


# version 06 (final)
# speedup vs baseline: 1.1285x; 1.1285x over previous
import contextlib
import numpy as np
import ml_dtypes
import concourse.bass as bass
import concourse.mybir as mybir
from concourse.bass_utils import run_bass_kernel_spmd

F32 = mybir.dt.float32
BF16 = mybir.dt.bfloat16
AF = mybir.ActivationFunctionType
ALU = mybir.AluOpType
AX = mybir.AxisListType

T = 2048
D = 1024
BFnp = ml_dtypes.bfloat16

NEG_BIG = -1.0e30


class Prog:

    def __init__(self):
        self.nc = bass.Bass("TRN2", target_bir_lowering=False)
        self.ctx = contextlib.ExitStack()
        self.sems = {}
        self.cnt = {}
        self._bar_n = 0
        self.block = None
        self.io = {}
        self.prefix = ""

    def phase(self, prefix, io):
        if self.block is not None:
            self.barrier()
        for k in [k for k in self.cnt if k.startswith("_")]:
            del self.cnt[k]
        self.prefix = prefix
        self.io = io or {}

    def sem(self, name):
        h = self.ctx.enter_context(self.nc.semaphore(name))
        self.sems[name] = h
        self.cnt[name] = 0
        return h

    def sbuf(self, name, shape, dtype):
        if name in self.io:
            return self.io[name]
        return self.ctx.enter_context(self.nc.sbuf_tensor(self.prefix + name, shape, dtype))

    def psum(self, name, shape, dtype):
        if name in self.io:
            return self.io[name]
        return self.ctx.enter_context(self.nc.psum_tensor(self.prefix + name, shape, dtype))

    def dram(self, name, shape, dtype, kind="Internal"):
        if name in self.io:
            return self.io[name]
        return self.nc.dram_tensor(self.prefix + name, shape, dtype, kind=kind).ap()

    def begin(self, consts=()):
        for s in ("LD", "GD", "SD", "ST", "PE", "DV", "AC", "PL", "BAR"):
            self.sem(s)
        self._constvals = list(consts)
        self._constmap = {v: i for i, v in enumerate(self._constvals)}
        if self._constvals:
            self.const_tile = self.sbuf("consts", [128, len(self._constvals)], F32)
        self.block = self.ctx.enter_context(self.nc.Block())
        if self._constvals:
            def _(v):
                ins = None
                for i, val in enumerate(self._constvals):
                    ins = v.memset(self.const_tile[:, i:i + 1], float(val))
                ins.then_inc(self.sems["DV"], 1)
            self.vector(_)
            self.inc("DV")
            self.barrier()

    def C(self, val):
        return self.const_tile[:, self._constmap[val]:self._constmap[val] + 1]

    def end(self):
        self.ctx.close()

    class _EngProxy:
        def __init__(self, eng, prog, queues):
            self._e = eng
            self._p = prog
            self._queues = queues
            self._flushed = False

        def dma_start(self, *a, **kw):
            if not self._flushed:
                for q in self._queues:
                    if self._p.cnt[q] > 0:
                        self._e.wait_ge(self._p.sems[q], self._p.cnt[q])
                self._flushed = True
            return self._e.dma_start(*a, **kw)

        def __getattr__(self, k):
            return getattr(self._e, k)

    def tensor(self, f):
        self.block.tensor(f)

    def vector(self, f):
        self.block.vector(f)

    def scalar(self, f):
        self.block.scalar(lambda e: f(Prog._EngProxy(e, self, ("SD",))))

    def gpsimd(self, f):
        self.block.gpsimd(lambda e: f(Prog._EngProxy(e, self, ("GD",))))

    def sync(self, f):
        self.block.sync(lambda e: f(Prog._EngProxy(e, self, ("LD", "ST"))))

    def inc(self, name, by=1):
        self.cnt[name] += by
        return self.cnt[name]

    def val(self, name):
        return self.cnt[name]

    def tick(self, ins, name, by=1):
        ins.then_inc(self.sems[name], by)
        return self.inc(name, by)

    def fence(self, eng, ins, name):
        t = self.tick(ins, name)
        eng.wait_ge(self.sems[name], t)
        return t

    def barrier(self):
        self._bar_n += 1
        tgt = 5 * self._bar_n

        def mk(own):
            def f(eng):
                for nm in own:
                    if self.cnt[nm] > 0:
                        eng.wait_ge(self.sems[nm], self.cnt[nm])
                eng.sem_inc(self.sems["BAR"], 1)
                eng.wait_ge(self.sems["BAR"], tgt)
            return f

        self.tensor(mk(("PE",)))
        self.vector(mk(("DV",)))
        self.scalar(mk(("AC", "SD")))
        self.gpsimd(mk(("PL", "GD")))
        self.sync(mk(("LD", "ST")))


def host_constants():
    c = {}
    c["ident"] = np.eye(128, dtype=np.float32)
    m = np.zeros((128, 128), np.float32)
    m[np.triu_indices(128, 1)] = NEG_BIG
    c["mask_diag"] = m
    s = np.arange(384)[None, :]
    r = np.arange(128)[:, None]
    band = (s >= r) & (s <= r + 256)
    gen = np.where(band, 0.0, NEG_BIG).astype(np.float32)
    c["mask_band"] = gen
    m0 = gen.copy()
    m0[:, :256] = NEG_BIG
    c["mask_band_t0"] = m0
    m1 = gen.copy()
    m1[:, :128] = NEG_BIG
    c["mask_band_t1"] = m1
    c["iota_t"] = np.arange(T, dtype=np.float32)[None, :]
    c["iota_i"] = np.arange(32, dtype=np.float32)[:, None]
    sel = np.zeros((32, 32 * 128), np.float32)
    for n in range(32):
        sel[n, n * 128:(n + 1) * 128] = 1.0
    c["sel"] = sel

    def expand(nb, w):
        e = np.zeros((nb, w), np.float32)
        for b in range(nb):
            e[b, b * 16:(b + 1) * 16] = 1.0
        return e
    c["E64"] = expand(64, 1024)
    c["E32"] = expand(32, 512)
    c["E44"] = expand(44, 704)
    bd = np.zeros((128, 2), np.float32)
    bd[0:64, 0] = 1.0
    bd[64:128, 1] = 1.0
    c["bdiag"] = bd
    c["bdiagT"] = np.ascontiguousarray(bd.T)
    c["ones1"] = np.ones((1, 128), np.float32)
    dup = np.zeros((32, 128), np.float32)
    for pp in range(128):
        dup[pp % 32, pp] = 1.0
    c["dup32"] = dup
    return c


def prep_E(inputs, c):
    qs = inputs["L1_ffn_q_scale"]
    ks = inputs["L1_ffn_k_scale"]
    lng = inputs["L1_ffn_ln_g"]
    lnb = inputs["L1_ffn_ln_b"]
    b1 = inputs["L1_ffn_b1"]
    b2 = inputs["L1_ffn_b2"]
    d = {
        "Wqkv": inputs["L1_ffn_Wqkv"],
        "Wo": inputs["L1_ffn_Wo"],
        "W1": inputs["L1_ffn_W1"],
        "W2": inputs["L1_ffn_W2"],
        "q_scale_r": np.ascontiguousarray(np.tile(qs, (1, 2)).T),
        "k_scale_r": np.ascontiguousarray(np.tile(ks, (1, 2)).T),
        "ln_g_r": np.ascontiguousarray(lng.reshape(6, 8, 128).transpose(2, 0, 1).reshape(128, 48)),
        "ln_b_r": np.ascontiguousarray(lnb.reshape(6, 8, 128).transpose(2, 0, 1).reshape(128, 48)),
        "b2_r": np.ascontiguousarray(b2.reshape(6, 8, 128).transpose(2, 0, 1).reshape(128, 48)),
        "ident": c["ident"], "mask_band": c["mask_band"],
        "mask_band_t0": c["mask_band_t0"], "mask_band_t1": c["mask_band_t1"],
        "iota_t": c["iota_t"], "iota_i": c["iota_i"], "dup32": c["dup32"],
        "bdiag": c["bdiag"], "bdiagT": c["bdiagT"], "ones1": c["ones1"],
    }
    b1r = np.zeros((128, 6 * 44), np.float32)
    for i in range(6):
        for j in range(22):
            w = min(128, 2730 - j * 128)
            b1r[0:w, i * 44 + j] = b1[i, j * 128:j * 128 + w]
            b1r[0:w, i * 44 + 22 + j] = b1[i, 2730 + j * 128:2730 + j * 128 + w]
    d["b1_r"] = b1r
    return d


def build_A(p=None, io=None):
    standalone = p is None
    if standalone:
        p = Prog()
    else:
        p.phase("A_", io)
    x_in = p.dram("x", [T, D], F32, kind="ExternalInput")
    ident_in = p.dram("ident", [128, 128], F32, kind="ExternalInput")
    x_fm_out = p.dram("x_fm", [D, T], F32, kind="ExternalOutput")
    xn_bf_out = p.dram("xn_bf", [D, T], F32, kind="ExternalOutput")
    xn_tm_out = p.dram("xn_tm", [T, D], F32, kind="ExternalOutput")

    NB = 3
    stg = [p.sbuf(f"stg{i}", [128, D], F32) for i in range(NB)]
    xn_tm = [p.sbuf(f"xntm{i}", [128, D], F32) for i in range(NB)]
    evac = [p.sbuf(f"evac{i}", [128, 1024], F32) for i in range(NB)]
    evb = [p.sbuf(f"evb{i}", [128, 1024], F32) for i in range(NB)]
    idf = p.sbuf("idf", [128, 128], F32)
    stats = [p.sbuf(f"stats{i}", [128, 6], F32) for i in range(NB)]

    pA = p.psum("pA", [128, 2048], F32)

    if standalone:
        p.begin(consts=[1e-5])

    def _(sy):
        sy.dma_start(out=idf[:], in_=ident_in).then_inc(p.sems["LD"], 16)
    p.sync(_)
    p.inc("LD", 16)

    NT = T // 128
    fin_pe = {}
    fin_xn = {}
    fin_ev = {}
    fin_st = {}

    for tt in range(NT):
        b = tt % NB
        prev = tt - NB

        def _(sy, tt=tt, b=b, prev=prev):
            if prev >= 0:
                sy.wait_ge(p.sems["PE"], fin_pe[prev])
                sy.wait_ge(p.sems["DV"], fin_xn[prev])
            sy.dma_start(out=stg[b][:], in_=x_in[tt * 128:(tt + 1) * 128, :]).then_inc(p.sems["LD"], 16)
        p.sync(_)
        ld_t = p.inc("LD", 16)

        def _(s, b=b, ld_t=ld_t, prev=prev):
            s.wait_ge(p.sems["LD"], ld_t)
            if prev >= 0:
                s.wait_ge(p.sems["ST"], fin_st[prev])
                s.wait_ge(p.sems["PE"], fin_pe[prev])
            s.activation(xn_tm[b][:], stg[b][:], AF.Square,
                         accum_out=stats[b][:, 0:1]).then_inc(p.sems["AC"], 1)
        p.scalar(_)
        ac_t = p.inc("AC")

        def _(v, b=b, ac_t=ac_t, ld_t=ld_t):
            st = stats[b]
            v.wait_ge(p.sems["LD"], ld_t)
            v.wait_ge(p.sems["AC"], ac_t)
            p.fence(v, v.reduce_sum(st[:, 1:2], stg[b][:], axis=AX.X), "DV")
            p.fence(v, v.tensor_scalar_mul(st[:, 1:2], st[:, 1:2], 1.0 / D), "DV")
            p.fence(v, v.tensor_tensor(st[:, 2:3], st[:, 1:2], st[:, 1:2], ALU.mult), "DV")
            p.tick(v.scalar_tensor_tensor(st[:, 3:4], st[:, 0:1], 1.0 / D, st[:, 2:3],
                                          ALU.mult, ALU.subtract), "DV")
        p.vector(_)
        dv_stats = p.val("DV")

        def _(s, b=b, dv_stats=dv_stats):
            st = stats[b]
            s.wait_ge(p.sems["DV"], dv_stats)
            p.fence(s, s.activation(st[:, 4:5], st[:, 3:4], AF.Ln, bias=p.C(1e-5)), "AC")
            p.tick(s.activation(st[:, 5:6], st[:, 4:5], AF.Exp, scale=-0.5), "AC")
        p.scalar(_)
        ac_t2 = p.val("AC")

        def _(v, b=b, ac_t2=ac_t2):
            st = stats[b]
            v.wait_ge(p.sems["AC"], ac_t2)
            v.tensor_scalar(xn_tm[b][:], stg[b][:], st[:, 1:2], st[:, 5:6],
                            ALU.subtract, ALU.mult).then_inc(p.sems["DV"], 1)
        p.vector(_)
        fin_xn[tt] = p.inc("DV")

        def _(t, b=b, tt=tt, ld_t=ld_t):
            t.wait_ge(p.sems["LD"], ld_t)
            if tt >= 1:
                t.wait_ge(p.sems["DV"], fin_ev[tt - 1])
            for c in range(8):
                t.transpose(pA[:, c * 128:(c + 1) * 128], stg[b][:, c * 128:(c + 1) * 128], idf[:])
            t.wait_ge(p.sems["DV"], fin_xn[tt])
            ins = None
            for c in range(8):
                ins = t.transpose(pA[:, 1024 + c * 128:1024 + (c + 1) * 128],
                                  xn_tm[b][:, c * 128:(c + 1) * 128], idf[:])
            ins.then_inc(p.sems["PE"], 1)
        p.tensor(_)
        fin_pe[tt] = p.inc("PE")

        def _(v, b=b, tt=tt, prev=prev):
            v.wait_ge(p.sems["PE"], fin_pe[tt])
            if prev >= 0:
                v.wait_ge(p.sems["ST"], fin_st[prev])
            v.tensor_copy(evac[b][:], pA[:, 0:1024])
            v.tensor_copy(evb[b][:], pA[:, 1024:2048]).then_inc(p.sems["DV"], 2)
        p.vector(_)
        fin_ev[tt] = p.inc("DV", 2)

        def _(sy, b=b, tt=tt):
            sy.wait_ge(p.sems["DV"], fin_xn[tt])
            sy.dma_start(out=xn_tm_out[tt * 128:(tt + 1) * 128, :],
                         in_=xn_tm[b][:]).then_inc(p.sems["ST"], 16)
            sy.wait_ge(p.sems["DV"], fin_ev[tt])
            sy.dma_start(out=x_fm_out.rearrange("(c p) t -> p c t", p=128)[:, :, tt * 128:(tt + 1) * 128],
                         in_=evac[b].rearrange("p (c q) -> p c q", q=128)).then_inc(p.sems["ST"], 16)
            sy.dma_start(out=xn_bf_out.rearrange("(c p) t -> p c t", p=128)[:, :, tt * 128:(tt + 1) * 128],
                         in_=evb[b].rearrange("p (c q) -> p c q", q=128)).then_inc(p.sems["ST"], 16)
        p.sync(_)
        fin_st[tt] = p.inc("ST", 48)

    def _(sy):
        sy.wait_ge(p.sems["ST"], p.val("ST"))
    p.sync(_)

    if standalone:
        p.end()
        return p.nc
    return None


def build_B(debug=False, p=None, io=None):
    standalone = p is None
    if standalone:
        p = Prog()
    else:
        p.phase("B_", io)
    xn_in = p.dram("xn_bf", [D, T], F32, kind="ExternalInput")
    ipw = p.dram("in_proj_w", [4096, 1024], F32, kind="ExternalInput")
    cw = p.dram("conv_w", [2048, 1, 4], F32, kind="ExternalInput")
    cb = p.dram("conv_b", [128, 16], F32, kind="ExternalInput")
    xpw = p.dram("x_proj_w", [96, 2048], F32, kind="ExternalInput")
    dtw = p.dram("dt_proj_w", [2048, 64], F32, kind="ExternalInput")
    dtb = p.dram("dt_proj_b", [128, 16], F32, kind="ExternalInput")
    alog = p.dram("A_log", [2048, 16], F32, kind="ExternalInput")
    dpar = p.dram("D_par", [128, 16], F32, kind="ExternalInput")
    opw = p.dram("out_proj_w", [1024, 2048], F32, kind="ExternalInput")
    ident_in = p.dram("ident", [128, 128], F32, kind="ExternalInput")
    sel_in = p.dram("sel", [32, 32 * 128], F32, kind="ExternalInput")
    mam_out = p.dram("mam_out", [D, T], F32, kind="ExternalOutput")

    TC = 512
    NTC = T // TC
    if debug:
        dbg_xp = p.dram("dbg_xp", [NTC, 128, 16 * 515], F32, kind="ExternalOutput")
        dbg_z = p.dram("dbg_z", [NTC, 128, 16 * TC], F32, kind="ExternalOutput")
        dbg_xa = p.dram("dbg_xa", [NTC, 128, 16 * TC], F32, kind="ExternalOutput")
        dbg_dl = p.dram("dbg_dl", [NTC, 64, TC], F32, kind="ExternalOutput")
        dbg_bc = p.dram("dbg_bc", [NTC, 32, TC], F32, kind="ExternalOutput")
        dbg_br = p.dram("dbg_br", [NTC, 128, 16 * TC], F32, kind="ExternalOutput")
        dbg_cr = p.dram("dbg_cr", [NTC, 128, 16 * TC], F32, kind="ExternalOutput")
        dbg_y = p.dram("dbg_y", [NTC, 128, 16 * TC], F32, kind="ExternalOutput")

    xnb = p.sbuf("xnb", [128, 8 * T], BF16)
    opT = p.sbuf("opT", [128, 16 * 1024], BF16)
    xpjT = p.sbuf("xpjT", [128, 16 * 96], BF16)
    dtT = p.sbuf("dtT", [64, 16 * 128], BF16)
    selb = p.sbuf("selb", [32, 32 * 128], BF16)
    id_b = p.sbuf("id_b", [128, 128], BF16)
    convw = p.sbuf("convw", [128, 16 * 4], F32)
    convb = p.sbuf("convb", [128, 16], F32)
    dtbc = p.sbuf("dtbc", [128, 16], F32)
    dcol = p.sbuf("dcol", [128, 16], F32)
    a_sb = p.sbuf("a_sb", [128, 256], F32)
    hstate = p.sbuf("hstate", [128, 256], F32)
    arena = p.sbuf("arena", [128, 24576], BF16)
    stage = [arena[:, 0:2048], arena[:, 2048:4096]]
    ybf = arena[:, 0:8192]
    xp = arena[:, 8192:16432]
    brep = arena[:, 8192:16384]
    crep = arena[:, 16384:24576]
    tails = p.sbuf("tails", [128, 48], BF16)
    ipT = [p.sbuf(f"ipT{i}", [128, 1024], BF16) for i in range(2)]
    z_bf = p.sbuf("z_bf", [128, 16 * TC], BF16)
    xa_bf = p.sbuf("xa_bf", [128, 16 * TC], BF16)
    xc = [p.sbuf(f"xc{i}", [128, TC], F32) for i in range(2)]
    dl_bf = p.sbuf("dl_bf", [64, TC], BF16)
    bc_bf = p.sbuf("bc_bf", [32, TC], BF16)
    v0 = p.sbuf("v0", [128, TC], F32)
    sp_ab = p.sbuf("sp_ab", [128, TC], F32)
    sp_l = p.sbuf("sp_l", [128, TC], F32)
    delta_dc = p.sbuf("delta_dc", [128, TC], F32)
    u_dc = p.sbuf("u_dc", [128, TC], BF16)
    dA = [p.sbuf(f"dA{i}", [128, TC], F32) for i in range(2)]
    dBu = [p.sbuf(f"dBu{i}", [128, TC], F32) for i in range(2)]
    hh = p.sbuf("hh", [128, TC], F32)
    ytmp = p.sbuf("ytmp", [128, TC], F32)
    yacc = p.sbuf("yacc", [128, TC], F32)
    sz = ytmp
    ostage = [p.sbuf("ostage0", [128, TC], F32)]

    pA = p.psum("pA", [128, 2048], F32)
    pB = p.psum("pB", [128, 1024], F32)
    pb0 = p.psum("pb0", [128, 1024], BF16)
    pb1 = p.psum("pb1", [128, 1024], BF16)

    if standalone:
        p.begin(consts=[1.0])

    LD, GD, ST, PE, DV, AC = (p.sems[k] for k in ("LD", "GD", "ST", "PE", "DV", "AC"))

    def _(g):
        g.dma_start(out=xnb.rearrange("p (c t) -> p c t", t=T),
                    in_=xn_in.rearrange("(c p) t -> p c t", p=128)).then_inc(GD, 16)
    p.gpsimd(_)
    p.inc("GD", 16)

    def _(sy):
        sy.dma_start(out=convw.rearrange("p (c j) -> p c j", j=4),
                     in_=cw.rearrange("(c p) one j -> p c (one j)", p=128)).then_inc(LD, 16)
        sy.dma_start(out=convb[:], in_=cb).then_inc(LD, 16)
        sy.dma_start(out=dtbc[:], in_=dtb).then_inc(LD, 16)
        sy.dma_start(out=dcol[:], in_=dpar).then_inc(LD, 16)
        sy.dma_start(out=a_sb.rearrange("p (c n) -> p c n", n=16),
                     in_=alog.rearrange("(c p) n -> p c n", p=128)).then_inc(LD, 16)
    p.sync(_)
    p.inc("LD", 16 * 5)

    def _(g):
        g.dma_start(out=id_b[:], in_=ident_in).then_inc(GD, 16)
        g.dma_start(out=selb[:], in_=sel_in).then_inc(GD, 16)
        g.dma_start(out=stage[0][0:96, :], in_=xpw).then_inc(GD, 16)
        g.dma_start(out=stage[1][:, 0:1024].rearrange("p (c r) -> p c r", r=64),
                    in_=dtw.rearrange("(c p) r -> p c r", p=128)).then_inc(GD, 16)
    p.gpsimd(_)
    p.inc("GD", 16 * 4)

    def _(s):
        s.wait_ge(LD, p.val("LD"))
        p.tick(s.activation(a_sb[:], a_sb[:], AF.Exp), "AC")
    p.scalar(_)

    def _(v):
        v.wait_ge(AC, p.val("AC"))
        p.tick(v.tensor_scalar_mul(a_sb[:], a_sb[:], -1.0), "DV")
        p.tick(v.memset(hstate[:], 0.0), "DV")
        p.tick(v.memset(tails[:], 0.0), "DV")
    p.vector(_)

    def _(t):
        t.wait_ge(GD, p.val("GD"))
        ins = None
        for kc in range(16):
            ins = t.transpose(pb0[:, kc * 96:(kc + 1) * 96] if kc < 10 else pb1[:, (kc - 10) * 96:(kc - 9) * 96],
                              stage[0][0:96, kc * 128:(kc + 1) * 128], id_b[0:96, 0:96])
        p.tick(ins, "PE")
    p.tensor(_)
    pe_xpj = p.val("PE")

    def _(v):
        v.wait_ge(PE, pe_xpj)
        for kc in range(16):
            src = pb0[:, kc * 96:(kc + 1) * 96] if kc < 10 else pb1[:, (kc - 10) * 96:(kc - 9) * 96]
            ins = v.tensor_copy(xpjT[:, kc * 96:(kc + 1) * 96], src)
        p.tick(ins, "DV")
    p.vector(_)
    dv_xpj = p.val("DV")

    def _(t):
        t.wait_ge(DV, dv_xpj)
        ins = None
        for c in range(16):
            ins = t.transpose(pb0[0:64, c * 128:(c + 1) * 128] if c < 8 else pb1[0:64, (c - 8) * 128:(c - 7) * 128],
                              stage[1][:, c * 64:(c + 1) * 64], id_b[:])
        p.tick(ins, "PE")
    p.tensor(_)
    pe_dtt = p.val("PE")

    def _(v):
        v.wait_ge(PE, pe_dtt)
        ins = None
        for c in range(16):
            src = pb0[0:64, c * 128:(c + 1) * 128] if c < 8 else pb1[0:64, (c - 8) * 128:(c - 7) * 128]
            ins = v.tensor_copy(dtT[:, c * 128:(c + 1) * 128], src)
        p.tick(ins, "DV")
    p.vector(_)
    p.cnt["_dttev"] = p.val("DV")

    pe_pre = p.val("PE")
    for nc8 in range(8):
        b = nc8 % 2
        def _(g, nc8=nc8, b=b):
            if nc8 >= 2:
                g.wait_ge(PE, p.cnt["_optr%d" % (nc8 - 2)])
            else:
                g.wait_ge(PE, pe_pre)
            g.dma_start(out=stage[b][:], in_=opw[nc8 * 128:(nc8 + 1) * 128, :]).then_inc(GD, 16)
        p.gpsimd(_)
        gd_t = p.inc("GD", 16)

        def _(t, nc8=nc8, b=b, gd_t=gd_t):
            t.wait_ge(GD, gd_t)
            if nc8 >= 1:
                t.wait_ge(DV, p.cnt["_opev%d" % (nc8 - 1)])
            else:
                t.wait_ge(DV, p.cnt["_dttev"])
            ins = None
            for kc in range(16):
                dst = pb0[:, (kc % 8) * 128:(kc % 8 + 1) * 128] if kc < 8 else pb1[:, (kc - 8) * 128:(kc - 7) * 128]
                ins = t.transpose(dst, stage[b][:, kc * 128:(kc + 1) * 128], id_b[:])
            p.tick(ins, "PE")
        p.tensor(_)
        p.cnt["_optr%d" % nc8] = p.val("PE")

        def _(v, nc8=nc8):
            v.wait_ge(PE, p.cnt["_optr%d" % nc8])
            ins = None
            for kc in range(16):
                src = pb0[:, (kc % 8) * 128:(kc % 8 + 1) * 128] if kc < 8 else pb1[:, (kc - 8) * 128:(kc - 7) * 128]
                ins = v.tensor_copy(opT[:, kc * 1024 + nc8 * 128:kc * 1024 + (nc8 + 1) * 128], src)
            p.tick(ins, "DV")
        p.vector(_)
        p.cnt["_opev%d" % nc8] = p.val("DV")

    p.barrier()

    for tc in range(NTC):
        t0 = tc * TC

        for nc in range(32):
            b = nc % 2
            def _(g, nc=nc, b=b, tc=tc):
                if nc >= 2:
                    g.wait_ge(PE, p.cnt["_iptr%d" % (nc - 2)])
                g.dma_start(out=stage[b][:, 0:1024], in_=ipw[nc * 128:(nc + 1) * 128, :]).then_inc(GD, 16)
            p.gpsimd(_)
            gd_t = p.inc("GD", 16)

            def _(t, nc=nc, b=b, gd_t=gd_t):
                t.wait_ge(GD, gd_t)
                if nc >= 1:
                    t.wait_ge(DV, p.cnt["_ipev%d" % (nc - 1)])
                ins = None
                for c in range(8):
                    ins = t.transpose(pb0[:, c * 128:(c + 1) * 128], stage[b][:, c * 128:(c + 1) * 128], id_b[:])
                p.tick(ins, "PE")
            p.tensor(_)
            p.cnt["_iptr%d" % nc] = p.val("PE")

            def _(v, nc=nc, b=b):
                v.wait_ge(PE, p.cnt["_iptr%d" % nc])
                p.tick(v.tensor_copy(ipT[b][:], pb0[:]), "DV")
            p.vector(_)
            p.cnt["_ipev%d" % nc] = p.val("DV")

            half = (nc % 2)
            def _(t, nc=nc, b=b, tc=tc, half=half):
                t.wait_ge(DV, p.cnt["_ipev%d" % nc])
                if nc >= 2:
                    t.wait_ge(DV, p.cnt["_xzev%d" % (nc - 2)])
                ins = None
                for c in range(8):
                    ins = t.matmul(pA[:, half * 512:half * 512 + 512],
                                   ipT[b][:, c * 128:(c + 1) * 128],
                                   xnb[:, c * T + t0:c * T + t0 + TC],
                                   start=(c == 0), stop=(c == 7))
                p.tick(ins, "PE")
            p.tensor(_)
            p.cnt["_ipmm%d" % nc] = p.val("PE")

            def _(v, nc=nc, half=half):
                v.wait_ge(PE, p.cnt["_ipmm%d" % nc])
                src = pA[:, half * 512:half * 512 + 512]
                if nc < 16:
                    v.tensor_copy(xp[:, nc * 515:nc * 515 + 3], tails[:, nc * 3:nc * 3 + 3])
                    ins = v.tensor_copy(xp[:, nc * 515 + 3:nc * 515 + 515], src)
                else:
                    ins = v.tensor_copy(z_bf[:, (nc - 16) * TC:(nc - 15) * TC], src)
                p.tick(ins, "DV")
            p.vector(_)
            p.cnt["_xzev%d" % nc] = p.val("DV")

        p.barrier()

        for dc in range(16):
            o = dc * 515
            def _(v, dc=dc, o=o):
                if dc >= 1:
                    v.wait_ge(AC, p.cnt["_silu%d" % (dc - 1)])
                w = lambda j: convw[:, dc * 4 + j:dc * 4 + j + 1]
                p.fence(v, v.tensor_scalar_mul(xc[0][:], xp[:, o + 0:o + TC], w(0)), "DV")
                p.fence(v, v.scalar_tensor_tensor(xc[1][:], xp[:, o + 1:o + 1 + TC], w(1), xc[0][:], ALU.mult, ALU.add), "DV")
                p.fence(v, v.scalar_tensor_tensor(xc[0][:], xp[:, o + 2:o + 2 + TC], w(2), xc[1][:], ALU.mult, ALU.add), "DV")
                p.fence(v, v.scalar_tensor_tensor(xc[1][:], xp[:, o + 3:o + 3 + TC], w(3), xc[0][:], ALU.mult, ALU.add), "DV")
                p.tick(v.tensor_copy(tails[:, dc * 3:dc * 3 + 3], xp[:, o + TC:o + TC + 3]), "DV")
            p.vector(_)
            dv_conv = p.val("DV")

            def _(s, dc=dc, dv_conv=dv_conv):
                s.wait_ge(DV, dv_conv)
                p.tick(s.activation(xa_bf[:, dc * TC:(dc + 1) * TC], xc[1][:], AF.Silu,
                                    bias=convb[:, dc:dc + 1]), "AC")
            p.scalar(_)
            p.cnt["_silu%d" % dc] = p.val("AC")

        p.barrier()
        if debug:
            def _(g, tc=tc):
                g.dma_start(out=dbg_xp[tc], in_=xp[:]).then_inc(GD, 16)
                g.dma_start(out=dbg_z[tc], in_=z_bf[:]).then_inc(GD, 16)
                g.dma_start(out=dbg_xa[tc], in_=xa_bf[:]).then_inc(GD, 16)
            p.gpsimd(_)
            p.inc("GD", 48)
            p.barrier()

        def _(t):
            ins = None
            for kc in range(16):
                ins = t.matmul(pB[0:96, 0:512], xpjT[:, kc * 96:(kc + 1) * 96],
                               xa_bf[:, kc * TC:(kc + 1) * TC],
                               start=(kc == 0), stop=(kc == 15))
            p.tick(ins, "PE")
        p.tensor(_)
        pe_dbc = p.val("PE")

        def _(v, pe_dbc=pe_dbc):
            v.wait_ge(PE, pe_dbc)
            p.tick(v.tensor_copy(dl_bf[:], pB[0:64, 0:512]), "DV")
            p.tick(v.tensor_copy(bc_bf[:], pB[64:96, 0:512]), "DV")
        p.vector(_)
        dv_bc = p.val("DV")

        for n in range(32):
            bank = n % 2
            def _(t, n=n, bank=bank, dv_bc=dv_bc):
                t.wait_ge(DV, dv_bc)
                if n >= 2:
                    t.wait_ge(DV, p.cnt["_brev%d" % (n - 2)])
                p.tick(t.matmul(pA[:, bank * 512:bank * 512 + 512],
                                selb[:, n * 128:(n + 1) * 128], bc_bf[:],
                                start=True, stop=True), "PE")
            p.tensor(_)
            p.cnt["_brmm%d" % n] = p.val("PE")

            def _(v, n=n, bank=bank):
                v.wait_ge(PE, p.cnt["_brmm%d" % n])
                dst = brep[:, n * TC:(n + 1) * TC] if n < 16 else crep[:, (n - 16) * TC:(n - 15) * TC]
                p.tick(v.tensor_copy(dst, pA[:, bank * 512:bank * 512 + 512]), "DV")
            p.vector(_)
            p.cnt["_brev%d" % n] = p.val("DV")

        p.barrier()
        if debug:
            def _(g, tc=tc):
                g.dma_start(out=dbg_dl[tc], in_=dl_bf[:]).then_inc(GD, 16)
                g.dma_start(out=dbg_bc[tc], in_=bc_bf[:]).then_inc(GD, 16)
                g.dma_start(out=dbg_br[tc], in_=brep[:, 0:16 * TC]).then_inc(GD, 16)
                g.dma_start(out=dbg_cr[tc], in_=crep[:, 0:16 * TC]).then_inc(GD, 16)
            p.gpsimd(_)
            p.inc("GD", 64)
            p.barrier()

        for dc in range(16):
            bank = dc % 2
            def _(t, dc=dc, bank=bank):
                if dc >= 2:
                    t.wait_ge(DV, p.cnt["_v0%d" % (dc - 2)])
                p.tick(t.matmul(pA[:, 1024 + bank * 512:1024 + bank * 512 + 512],
                                dtT[:, dc * 128:(dc + 1) * 128], dl_bf[:],
                                start=True, stop=True), "PE")
            p.tensor(_)
            pe_dp = p.val("PE")

            def _(v, dc=dc, bank=bank, pe_dp=pe_dp):
                v.wait_ge(PE, pe_dp)
                if dc >= 1:
                    v.wait_ge(AC, p.cnt["_acsp%d" % (dc - 1)])
                p.tick(v.tensor_scalar_add(v0[:], pA[:, 1024 + bank * 512:1024 + bank * 512 + 512],
                                           dtbc[:, dc:dc + 1]), "DV")
            p.vector(_)
            p.cnt["_v0%d" % dc] = p.val("DV")
            dv_v0 = p.val("DV")

            def _(s, dv_v0=dv_v0):
                s.wait_ge(DV, dv_v0)
                p.fence(s, s.activation(sp_ab[:], v0[:], AF.Abs), "AC")
                p.fence(s, s.activation(sp_ab[:], sp_ab[:], AF.Exp, scale=-1.0), "AC")
                p.tick(s.activation(sp_l[:], sp_ab[:], AF.Ln, bias=p.C(1.0)), "AC")
            p.scalar(_)
            ac_sp = p.val("AC")
            p.cnt["_acsp%d" % dc] = ac_sp

            def _(v, dc=dc, ac_sp=ac_sp):
                v.wait_ge(DV, p.cnt["_v0%d" % dc])
                p.fence(v, v.tensor_scalar_max(xc[0][:], v0[:], 0.0), "DV")
                v.wait_ge(AC, ac_sp)
                p.fence(v, v.tensor_add(delta_dc[:], xc[0][:], sp_l[:]), "DV")
                p.tick(v.tensor_mul(u_dc[:], delta_dc[:], xa_bf[:, dc * TC:(dc + 1) * TC]), "DV")
            p.vector(_)
            dv_u = p.val("DV")

            for n in range(16):
                nb = n % 2
                def _(s, dc=dc, n=n, nb=nb, dv_u=dv_u):
                    s.wait_ge(DV, dv_u)
                    if n >= 2:
                        s.wait_ge(DV, p.cnt["_scan%d_%d" % (dc, n - 2)])
                    p.tick(s.activation(dA[nb][:], delta_dc[:], AF.Exp,
                                        scale=a_sb[:, dc * 16 + n:dc * 16 + n + 1]), "AC")
                p.scalar(_)
                ac_da = p.val("AC")

                def _(v, dc=dc, n=n, nb=nb, ac_da=ac_da, dv_u=dv_u):
                    v.wait_ge(DV, dv_u)
                    if n >= 2:
                        v.wait_ge(DV, p.cnt["_scan%d_%d" % (dc, n - 2)])
                    p.fence(v, v.tensor_mul(dBu[nb][:], u_dc[:], brep[:, n * TC:(n + 1) * TC]), "DV")
                    v.wait_ge(AC, ac_da)
                    col = dc * 16 + n
                    p.fence(v, v.tensor_tensor_scan(hh[:], dA[nb][:], dBu[nb][:],
                                                    hstate[:, col:col + 1], ALU.mult, ALU.add), "DV")
                    p.cnt["_scan%d_%d" % (dc, n)] = p.val("DV")
                    v.tensor_copy(hstate[:, col:col + 1], hh[:, TC - 1:TC])
                    if n == 0:
                        p.fence(v, v.tensor_mul(yacc[:], hh[:], crep[:, n * TC:(n + 1) * TC]), "DV")
                    else:
                        p.fence(v, v.tensor_mul(ytmp[:], hh[:], crep[:, n * TC:(n + 1) * TC]), "DV")
                        p.fence(v, v.tensor_add(yacc[:], yacc[:], ytmp[:]), "DV")
                p.vector(_)

            def _(s, dc=dc):
                s.wait_ge(DV, p.val("DV"))
                p.tick(s.activation(sz[:], z_bf[:, dc * TC:(dc + 1) * TC], AF.Silu), "AC")
            p.scalar(_)
            ac_sz = p.val("AC")

            def _(v, dc=dc, ac_sz=ac_sz):
                p.fence(v, v.scalar_tensor_tensor(yacc[:], xa_bf[:, dc * TC:(dc + 1) * TC],
                                                  dcol[:, dc:dc + 1], yacc[:], ALU.mult, ALU.add), "DV")
                v.wait_ge(AC, ac_sz)
                p.tick(v.tensor_mul(ybf[:, dc * TC:(dc + 1) * TC], yacc[:], sz[:]), "DV")
            p.vector(_)

        p.barrier()
        if debug:
            def _(g, tc=tc):
                g.dma_start(out=dbg_y[tc], in_=ybf[:, 0:16 * TC]).then_inc(GD, 16)
            p.gpsimd(_)
            p.inc("GD", 16)
            p.barrier()

        for oc in range(8):
            bank = oc % 2
            def _(t, oc=oc, bank=bank):
                if oc >= 2:
                    t.wait_ge(DV, p.cnt["_oev%d" % (oc - 2)])
                ins = None
                for kc in range(16):
                    ins = t.matmul(pA[:, bank * 512:bank * 512 + 512],
                                   opT[:, kc * 1024 + oc * 128:kc * 1024 + (oc + 1) * 128],
                                   ybf[:, kc * TC:(kc + 1) * TC],
                                   start=(kc == 0), stop=(kc == 15))
                p.tick(ins, "PE")
            p.tensor(_)
            pe_o = p.val("PE")

            def _(v, oc=oc, bank=bank, pe_o=pe_o):
                v.wait_ge(PE, pe_o)
                if oc >= 1:
                    v.wait_ge(ST, p.cnt["_ost%d" % (oc - 1)])
                p.tick(v.tensor_copy(ostage[0][:], pA[:, bank * 512:bank * 512 + 512]), "DV")
            p.vector(_)
            p.cnt["_oev%d" % oc] = p.val("DV")

            def _(sy, oc=oc, bank=bank, tc=tc):
                sy.wait_ge(DV, p.cnt["_oev%d" % oc])
                sy.dma_start(out=mam_out[oc * 128:(oc + 1) * 128, tc * TC:(tc + 1) * TC],
                             in_=ostage[0][:]).then_inc(ST, 16)
            p.sync(_)
            p.cnt["_ost%d" % oc] = p.inc("ST", 16)

        p.barrier()

    def _(sy):
        sy.wait_ge(ST, p.val("ST"))
    p.sync(_)

    if standalone:
        p.end()
        return p.nc
    return None


def emit_quant(p, W, nb, src, dst, E_ap, sc, psA, psB, id_f):
    V, S, Tt = p.vector, p.scalar, p.tensor
    DV, AC, PE = p.sems["DV"], p.sems["AC"], p.sems["PE"]
    nw = (W + 511) // 512
    dv_entry = p.val("DV")

    def _(v):
        v.wait_ge(DV, dv_entry)
        p.fence(v, v.tensor_reduce(sc["bm"][:, 0:nb],
                                   src.rearrange("p (b j) -> p b j", j=16),
                                   axis=AX.X, op=ALU.max, apply_absolute_value=True), "DV")
        p.fence(v, v.tensor_scalar_max(sc["bm"][:, 0:nb], sc["bm"][:, 0:nb], 6e-12), "DV")
        p.fence(v, v.reciprocal(sc["rc"][:, 0:nb], sc["bm"][:, 0:nb]), "DV")
        p.tick(v.tensor_scalar_mul(sc["rc"][:, 0:nb], sc["rc"][:, 0:nb], 48.0), "DV")
    V(_)
    dv1 = p.val("DV")

    def _(t):
        t.wait_ge(DV, dv1)
        t.transpose(psB[:, 0:128], sc["bm"][:, 0:128], id_f)
        p.tick(t.transpose(psB[:, 128:256], sc["rc"][:, 0:128], id_f), "PE")
    Tt(_)
    pe1 = p.val("PE")

    def _(v):
        v.wait_ge(PE, pe1)
        v.tensor_copy(sc["bmT"][0:nb, :], psB[0:nb, 0:128])
        p.tick(v.tensor_copy(sc["rcT"][0:nb, :], psB[0:nb, 128:256]), "DV")
    V(_)
    dv2 = p.val("DV")

    def _(t):
        t.wait_ge(DV, dv2)
        ins = None
        for cw in range(nw):
            w0, w1 = cw * 512, min(W, (cw + 1) * 512)
            ins = t.matmul(psA[:, cw * 512:cw * 512 + (w1 - w0)], sc["bmT"][0:nb, :],
                           E_ap[0:nb, w0:w1], start=True, stop=True)
            ins = t.matmul(psA[:, nw * 512 + cw * 512:nw * 512 + cw * 512 + (w1 - w0)],
                           sc["rcT"][0:nb, :], E_ap[0:nb, w0:w1], start=True, stop=True)
        p.tick(ins, "PE")
    Tt(_)
    pe2 = p.val("PE")

    def _(s):
        s.wait_ge(DV, dv2)
        p.tick(s.activation(sc["ax"][:, 0:W], src, AF.Abs), "AC")
    S(_)
    ac1 = p.val("AC")

    def _(v):
        v.wait_ge(PE, pe2)
        v.wait_ge(AC, ac1)
        ins = None
        for cw in range(nw):
            w0, w1 = cw * 512, min(W, (cw + 1) * 512)
            ins = v.tensor_mul(sc["r"][:, w0:w1], sc["ax"][:, w0:w1],
                               psA[:, nw * 512 + cw * 512:nw * 512 + cw * 512 + (w1 - w0)])
        p.fence(v, ins, "DV")
        p.fence(v, v.tensor_single_scalar(sc["accA"][:, 0:W], sc["r"][:, 0:W], 2.0, ALU.is_gt), "DV")
        for thr in (6.0, 10.0, 14.0):
            p.fence(v, v.scalar_tensor_tensor(sc["accA"][:, 0:W], sc["r"][:, 0:W], thr,
                                              sc["accA"][:, 0:W], ALU.is_gt, ALU.add), "DV")
        p.fence(v, v.tensor_single_scalar(sc["accB"][:, 0:W], sc["r"][:, 0:W], 20.0, ALU.is_gt), "DV")
        p.fence(v, v.scalar_tensor_tensor(sc["accB"][:, 0:W], sc["r"][:, 0:W], 28.0,
                                          sc["accB"][:, 0:W], ALU.is_gt, ALU.add), "DV")
        p.fence(v, v.scalar_tensor_tensor(sc["accA"][:, 0:W], sc["accA"][:, 0:W], 0.5,
                                          sc["accB"][:, 0:W], ALU.mult, ALU.add), "DV")
        p.fence(v, v.tensor_scalar(sc["accB"][:, 0:W], sc["r"][:, 0:W], 40.0, 2.0,
                                   ALU.is_gt, ALU.mult), "DV")
        p.fence(v, v.tensor_add(sc["accA"][:, 0:W], sc["accA"][:, 0:W], sc["accB"][:, 0:W]), "DV")
        p.fence(v, v.tensor_scalar(sc["sg"][:, 0:W], src, 0.0, 1.0 / 3.0, ALU.is_gt, ALU.mult), "DV")
        p.fence(v, v.tensor_scalar_sub(sc["sg"][:, 0:W], sc["sg"][:, 0:W], 1.0 / 6.0), "DV")
        ins = None
        for cw in range(nw):
            w0, w1 = cw * 512, min(W, (cw + 1) * 512)
            ins = v.tensor_mul(sc["sg"][:, w0:w1], sc["sg"][:, w0:w1],
                               psA[:, cw * 512:cw * 512 + (w1 - w0)])
        p.fence(v, ins, "DV")
        p.tick(v.tensor_mul(dst, sc["accA"][:, 0:W], sc["sg"][:, 0:W]), "DV")
    V(_)
    return p.val("DV")


def build_C(p=None, io=None):
    standalone = p is None
    if standalone:
        p = Prog()
    else:
        p.phase("C_", io)
    xn_in = p.dram("xn_tm", [T, D], F32, kind="ExternalInput")
    xfm_in = p.dram("x_fm", [D, T], F32, kind="ExternalInput")
    mam_in = p.dram("mam_out", [D, T], F32, kind="ExternalInput")
    wg_in = p.dram("w_gate", [2816, 1024], F32, kind="ExternalInput")
    wu_in = p.dram("w_up", [2816, 1024], F32, kind="ExternalInput")
    wd_in = p.dram("w_down", [1024, 2816], F32, kind="ExternalInput")
    ident_in = p.dram("ident", [128, 128], F32, kind="ExternalInput")
    e64_in = p.dram("E64", [64, 1024], F32, kind="ExternalInput")
    e32_in = p.dram("E32", [32, 512], F32, kind="ExternalInput")
    e44_in = p.dram("E44", [44, 704], F32, kind="ExternalInput")
    x1_out = p.dram("x1_fm", [D, T], F32, kind="ExternalOutput")

    wgT_d = p.dram("wgT_d", [1024, 2816], BF16)
    wuT_d = p.dram("wuT_d", [1024, 2816], BF16)
    wdT_d = p.dram("wdT_d", [2816, 1024], BF16)

    xf = p.sbuf("xf", [128, 8 * T], F32)
    id_f = p.sbuf("id_f", [128, 128], F32)
    id_b = p.sbuf("id_b", [128, 128], BF16)
    e64 = p.sbuf("e64", [64, 1024], F32)
    e32 = p.sbuf("e32", [32, 512], F32)
    e44 = p.sbuf("e44", [44, 704], F32)
    sc = {
        "bm": p.sbuf("q_bm", [128, 128], F32),
        "rc": p.sbuf("q_rc", [128, 128], F32),
        "bmT": p.sbuf("q_bmT", [128, 128], F32),
        "rcT": p.sbuf("q_rcT", [128, 128], F32),
        "ax": p.sbuf("q_ax", [128, 1024], F32),
        "accA": p.sbuf("q_accA", [128, 1024], F32),
        "accB": p.sbuf("q_accB", [128, 1024], F32),
        "sg": p.sbuf("q_sg", [128, 1024], F32),
    }
    sc["r"] = sc["ax"]
    wstage = p.sbuf("wstage", [128, 2816], F32)
    wq_nat = p.sbuf("wq_nat", [128, 2816], BF16)
    xq_fm = p.sbuf("xq_fm", [128, 8 * 1024], BF16)
    hq_fm = p.sbuf("hq_fm", [128, 22 * 1024], BF16)
    wqg = p.sbuf("wqg", [128, 8 * 512], BF16)
    wqu = p.sbuf("wqu", [128, 8 * 512], BF16)
    wdT = p.sbuf("wdTs", [128, 22 * 128], BF16)
    xnst = p.sbuf("xnst", [128, 1024], F32)
    xqt = p.sbuf("xqt", [128, 1024], BF16)
    sgate = p.sbuf("sgate", [128, 512], F32)
    h2f = p.sbuf("h2f", [128, 512], F32)
    hqt = p.sbuf("hqt", [128, 512], BF16)

    pA = p.psum("pA", [128, 2048], F32)
    pB = p.psum("pB", [128, 1024], F32)
    pb0 = p.psum("pb0", [128, 1024], BF16)
    pb1 = p.psum("pb1", [128, 1024], BF16)

    if standalone:
        p.begin(consts=[1.0])
    LD, GD, ST, PE, DV, AC = (p.sems[k] for k in ("LD", "GD", "ST", "PE", "DV", "AC"))

    def _(sy):
        sy.dma_start(out=xf.rearrange("p (c t) -> p c t", t=T),
                     in_=xfm_in.rearrange("(c p) t -> p c t", p=128)).then_inc(LD, 16)
        sy.dma_start(out=id_f[:], in_=ident_in).then_inc(LD, 16)
        sy.dma_start(out=e64[:], in_=e64_in).then_inc(LD, 16)
        sy.dma_start(out=e32[:], in_=e32_in).then_inc(LD, 16)
        sy.dma_start(out=e44[:], in_=e44_in).then_inc(LD, 16)
    p.sync(_)
    p.inc("LD", 16 * 5)

    def _(g):
        g.dma_start(out=id_b[:], in_=ident_in).then_inc(GD, 16)
    p.gpsimd(_)
    p.inc("GD", 16)

    for c in range(8):
        def _(sy, c=c):
            sy.wait_ge(LD, p.val("LD"))
            if c >= 1:
                sy.wait_ge(DV, p.cnt["_madd%d" % (c - 1)])
            sy.dma_start(out=wstage[:, 0:2048], in_=mam_in[c * 128:(c + 1) * 128, :]).then_inc(LD, 16)
        p.sync(_)
        ld_t = p.inc("LD", 16)

        def _(v, c=c, ld_t=ld_t):
            v.wait_ge(LD, ld_t)
            p.tick(v.tensor_add(xf[:, c * T:(c + 1) * T], xf[:, c * T:(c + 1) * T],
                                wstage[:, 0:2048]), "DV")
        p.vector(_)
        p.cnt["_madd%d" % c] = p.val("DV")

    p.barrier()

    for nc8 in range(8):
        def _(sy, nc8=nc8):
            sy.dma_start(out=wstage[:, 0:2816], in_=wd_in[nc8 * 128:(nc8 + 1) * 128, :]).then_inc(LD, 16)
        p.sync(_)
        ld_t = p.inc("LD", 16)

        def _(v, ld_t=ld_t):
            v.wait_ge(LD, ld_t)
            v.engine_nop()
        p.vector(_)
        for hh in range(4):
            emit_quant(p, 704, 44, wstage[:, hh * 704:(hh + 1) * 704],
                       wq_nat[:, hh * 704:(hh + 1) * 704], e44[:], sc, pA, pB, id_f[:])
        dvq = p.val("DV")
        for grp, (k0, k1) in enumerate([(0, 11), (11, 22)]):
            def _(t, dvq=dvq, k0=k0, k1=k1, grp=grp):
                t.wait_ge(DV, dvq)
                if grp == 1:
                    t.wait_ge(DV, p.cnt["_wdev0"])
                ins = None
                for kc in range(k0, k1):
                    j = kc - k0
                    dst = pb0[:, j * 128:(j + 1) * 128] if j < 8 else pb1[:, (j - 8) * 128:(j - 7) * 128]
                    ins = t.transpose(dst, wq_nat[:, kc * 128:(kc + 1) * 128], id_b[:])
                p.tick(ins, "PE")
            p.tensor(_)
            pe_tr = p.val("PE")

            def _(v, pe_tr=pe_tr, k0=k0, k1=k1):
                v.wait_ge(PE, pe_tr)
                ins = None
                for kc in range(k0, k1):
                    j = kc - k0
                    src = pb0[:, j * 128:(j + 1) * 128] if j < 8 else pb1[:, (j - 8) * 128:(j - 7) * 128]
                    ins = v.tensor_copy(wdT[:, kc * 128:(kc + 1) * 128], src)
                p.tick(ins, "DV")
            p.vector(_)
            p.cnt["_wdev%d" % grp] = p.val("DV")
        dv_ev = p.val("DV")

        def _(sy, nc8=nc8, dv_ev=dv_ev):
            sy.wait_ge(DV, dv_ev)
            sy.dma_start(out=wdT_d.rearrange("(kc p) n -> p kc n", p=128)[:, :, nc8 * 128:(nc8 + 1) * 128],
                         in_=wdT.rearrange("p (kc n) -> p kc n", n=128)).then_inc(ST, 16)
        p.sync(_)
        p.inc("ST", 16)
        p.barrier()

    TH = 1024
    for th in range(2):
        th0 = th * TH
        for tt in range(8):
            ta = th0 + tt * 128
            def _(sy, ta=ta):
                sy.dma_start(out=xnst[:], in_=xn_in[ta:ta + 128, :]).then_inc(LD, 16)
            p.sync(_)
            ld_t = p.inc("LD", 16)

            def _(v, ld_t=ld_t):
                v.wait_ge(LD, ld_t)
                v.engine_nop()
            p.vector(_)
            emit_quant(p, 1024, 64, xnst[:], xqt[:], e64[:], sc, pA, pB, id_f[:])
            dvq = p.val("DV")

            def _(t, dvq=dvq, tt=tt):
                t.wait_ge(DV, dvq)
                ins = None
                for c in range(8):
                    ins = t.transpose(pb0[:, c * 128:(c + 1) * 128], xqt[:, c * 128:(c + 1) * 128], id_b[:])
                p.tick(ins, "PE")
            p.tensor(_)
            pe_t = p.val("PE")

            def _(v, pe_t=pe_t, tt=tt):
                v.wait_ge(PE, pe_t)
                ins = None
                for c in range(8):
                    ins = v.tensor_copy(xq_fm[:, c * 1024 + tt * 128:c * 1024 + (tt + 1) * 128],
                                        pb0[:, c * 128:(c + 1) * 128])
                p.tick(ins, "DV")
            p.vector(_)
            p.barrier()

        NS = [(0, 512), (512, 1024), (1024, 1536), (1536, 2048), (2048, 2560), (2560, 2816)]
        _emit_C2(p, th, NS, wg_in, wu_in, wgT_d, wuT_d, wqg, wqu, wstage, wq_nat,
                 xq_fm, hq_fm, sgate, h2f, hqt, e64, e32, sc, pA, pB, pb0, pb1, id_f, id_b)

        for oc in range(8):
            def _(sy, oc=oc):
                if oc >= 1:
                    sy.wait_ge(PE, p.cnt["_dnmm%d" % (oc - 1)])
                sy.dma_start(out=wdT.rearrange("p (kc n) -> p kc n", n=128),
                             in_=wdT_d.rearrange("(kc p) n -> p kc n", p=128)[:, :, oc * 128:(oc + 1) * 128]).then_inc(LD, 16)
            p.sync(_)
            ld_t = p.inc("LD", 16)

            for tch in range(2):
                bank = tch
                def _(t, oc=oc, tch=tch, bank=bank, ld_t=ld_t):
                    t.wait_ge(LD, ld_t)
                    if oc >= 1 or tch >= 1:
                        t.wait_ge(DV, p.cnt.get("_dnev%d_%d" % ((oc, tch - 1) if tch >= 1 else (oc - 1, 1)), 0))
                    ins = None
                    for kc in range(22):
                        ins = t.matmul(pA[:, bank * 512:bank * 512 + 512],
                                       wdT[:, kc * 128:(kc + 1) * 128],
                                       hq_fm[:, kc * 1024 + tch * 512:kc * 1024 + tch * 512 + 512],
                                       start=(kc == 0), stop=(kc == 21))
                    p.tick(ins, "PE")
                p.tensor(_)
                if tch == 1:
                    p.cnt["_dnmm%d" % oc] = p.val("PE")
                pe_mm = p.val("PE")

                def _(v, oc=oc, tch=tch, bank=bank, pe_mm=pe_mm, th=th):
                    v.wait_ge(PE, pe_mm)
                    dst = xf[:, oc * T + th * TH + tch * 512:oc * T + th * TH + tch * 512 + 512]
                    p.tick(v.tensor_add(dst, dst, pA[:, bank * 512:bank * 512 + 512]), "DV")
                p.vector(_)
                p.cnt["_dnev%d_%d" % (oc, tch)] = p.val("DV")
        p.barrier()

    def _(sy):
        sy.dma_start(out=x1_out.rearrange("(c p) t -> p c t", p=128),
                     in_=xf.rearrange("p (c t) -> p c t", t=T)).then_inc(ST, 16)
        sy.wait_ge(ST, p.val("ST") + 16)
    p.sync(_)
    p.inc("ST", 16)

    if standalone:
        p.end()
        return p.nc
    return None


def _emit_C2(p, th, NS, wg_in, wu_in, wgT_d, wuT_d, wqg, wqu, wstage, wq_nat,
             xq_fm, hq_fm, sgate, h2f, hqt, e64, e32, sc, pA, pB, pb0, pb1, id_f, id_b):
    LD, GD, ST, PE, DV, AC = (p.sems[k] for k in ("LD", "GD", "ST", "PE", "DV", "AC"))
    TH = 1024
    for si, (n0, n1) in enumerate(NS):
        nwid = n1 - n0
        nch = (nwid + 127) // 128
        if th == 0:
            for wi, (w_in_, wT_d, wq_sb) in enumerate([(wg_in, wgT_d, wqg), (wu_in, wuT_d, wqu)]):
                for rc in range(nch):
                    r0 = n0 + rc * 128
                    def _(sy, w_in_=w_in_, r0=r0):
                        sy.wait_ge(DV, p.val("DV"))
                        sy.dma_start(out=wstage[:, 0:1024], in_=w_in_[r0:r0 + 128, :]).then_inc(LD, 16)
                    p.sync(_)
                    ld_t = p.inc("LD", 16)

                    def _(v, ld_t=ld_t):
                        v.wait_ge(LD, ld_t)
                        v.engine_nop()
                    p.vector(_)
                    emit_quant(p, 1024, 64, wstage[:, 0:1024], wq_nat[:, 0:1024],
                               e64[:], sc, pA, pB, id_f[:])
                    dvq = p.val("DV")

                    def _(t, dvq=dvq, rc=rc):
                        t.wait_ge(DV, dvq)
                        ins = None
                        for c in range(8):
                            ins = t.transpose(pb0[:, c * 128:(c + 1) * 128],
                                              wq_nat[:, c * 128:(c + 1) * 128], id_b[:])
                        p.tick(ins, "PE")
                    p.tensor(_)
                    pe_t = p.val("PE")

                    def _(v, pe_t=pe_t, rc=rc, wq_sb=wq_sb):
                        v.wait_ge(PE, pe_t)
                        ins = None
                        for c in range(8):
                            ins = v.tensor_copy(wq_sb[:, c * 512 + rc * 128:c * 512 + (rc + 1) * 128],
                                                pb0[:, c * 128:(c + 1) * 128])
                        p.tick(ins, "DV")
                    p.vector(_)
                    p.barrier()

                dv_now = p.val("DV")
                def _(sy, wT_d=wT_d, wq_sb=wq_sb, n0=n0, nwid=nwid, dv_now=dv_now):
                    sy.wait_ge(DV, dv_now)
                    sy.dma_start(out=wT_d.rearrange("(kc p) n -> p kc n", p=128)[:, :, n0:n0 + nwid],
                                 in_=wq_sb.rearrange("p (kc n) -> p kc n", n=512)[:, :, 0:nwid]).then_inc(ST, 16)
                p.sync(_)
                p.inc("ST", 16)
        else:
            for wT_d, wq_sb in [(wgT_d, wqg), (wuT_d, wqu)]:
                def _(sy, wT_d=wT_d, wq_sb=wq_sb, n0=n0, nwid=nwid):
                    sy.wait_ge(PE, p.val("PE"))
                    sy.dma_start(out=wq_sb.rearrange("p (kc n) -> p kc n", n=512)[:, :, 0:nwid],
                                 in_=wT_d.rearrange("(kc p) n -> p kc n", p=128)[:, :, n0:n0 + nwid]).then_inc(LD, 16)
                p.sync(_)
                p.inc("LD", 16)
            p.barrier()

        for tt in range(8):
            def _(t, si=si, tt=tt, nwid=nwid):
                if tt >= 1:
                    t.wait_ge(DV, p.cnt.get("_hqev", 0))
                ins = None
                for c in range(8):
                    ins = t.matmul(pA[:, 0:nwid], xq_fm[:, c * 1024 + tt * 128:c * 1024 + (tt + 1) * 128],
                                   wqg[:, c * 512:c * 512 + nwid], start=(c == 0), stop=(c == 7))
                for c in range(8):
                    ins = t.matmul(pA[:, 512:512 + nwid], xq_fm[:, c * 1024 + tt * 128:c * 1024 + (tt + 1) * 128],
                                   wqu[:, c * 512:c * 512 + nwid], start=(c == 0), stop=(c == 7))
                p.tick(ins, "PE")
            p.tensor(_)
            pe_g = p.val("PE")

            def _(s, pe_g=pe_g, nwid=nwid):
                s.wait_ge(PE, pe_g)
                p.tick(s.activation(sgate[:, 0:nwid], pA[:, 0:nwid], AF.Silu), "AC")
            p.scalar(_)
            ac_g = p.val("AC")

            def _(v, ac_g=ac_g, nwid=nwid):
                v.wait_ge(AC, ac_g)
                p.tick(v.tensor_mul(h2f[:, 0:nwid], sgate[:, 0:nwid], pA[:, 512:512 + nwid]), "DV")
            p.vector(_)

            emit_quant(p, nwid, nwid // 16, h2f[:, 0:nwid], hqt[:, 0:nwid],
                       e32[:], sc, pA[:, 1024:2048], pB, id_f[:])
            dv_hq = p.val("DV")

            def _(t, dv_hq=dv_hq, nwid=nwid):
                t.wait_ge(DV, dv_hq)
                ins = None
                for c in range((nwid + 127) // 128):
                    ins = t.transpose(pb1[:, c * 128:(c + 1) * 128], hqt[:, c * 128:(c + 1) * 128], id_b[:])
                p.tick(ins, "PE")
            p.tensor(_)
            pe_h = p.val("PE")

            def _(v, pe_h=pe_h, si=si, tt=tt, nwid=nwid):
                v.wait_ge(PE, pe_h)
                ins = None
                for c in range((nwid + 127) // 128):
                    kc = si * 4 + c
                    ins = v.tensor_copy(hq_fm[:, kc * 1024 + tt * 128:kc * 1024 + (tt + 1) * 128],
                                        pb1[:, c * 128:(c + 1) * 128])
                p.tick(ins, "DV")
            p.vector(_)
            p.cnt["_hqev"] = p.val("DV")
        p.barrier()


def build_D(p=None, io=None):
    standalone = p is None
    if standalone:
        p = Prog()
    else:
        p.phase("D_", io)
    x1_in = p.dram("x1_fm", [D, T], F32, kind="ExternalInput")
    wat_in = p.dram("w_attn", [1024, 1152], F32, kind="ExternalInput")
    wpj_in = p.dram("w_proj", [1024, 1024], F32, kind="ExternalInput")
    ident_in = p.dram("ident", [128, 128], F32, kind="ExternalInput")
    maskd_in = p.dram("mask_diag", [128, 128], F32, kind="ExternalInput")
    xn2_out = p.dram("xn2_bf", [D, T], F32, kind="ExternalOutput")
    x2p_out = p.dram("x2p_fm", [D, T], F32, kind="ExternalOutput")

    xf = p.sbuf("xf", [128, 8 * T], F32)
    xn2 = p.sbuf("xn2", [128, 8 * T], BF16)
    oat = p.sbuf("oat", [128, 8 * T], BF16)
    wslq = p.sbuf("wslq", [128, 8 * 64], BF16)
    wslk = p.sbuf("wslk", [128, 8 * 64], BF16)
    wslv = p.sbuf("wslv", [128, 8 * 64], BF16)
    wpj = p.sbuf("wpj", [128, 8 * 1024], BF16)
    id_f = p.sbuf("id_f", [128, 128], F32)
    id_b = p.sbuf("id_b", [128, 128], BF16)
    maskd = p.sbuf("maskd", [128, 128], F32)
    xtm = p.sbuf("xtm", [128, 1024], F32)
    xntm = p.sbuf("xntm", [128, 1024], F32)
    xn2f = p.sbuf("xn2f", [128, 1024], F32)
    stats = p.sbuf("stats", [128, 8], F32)
    kfm = p.sbuf("kfm", [64, T], BF16)
    vtm = p.sbuf("vtm", [128, 16 * 64], BF16)
    qbf = p.sbuf("qbf", [64, T], BF16)
    psb = p.sbuf("psb", [128, T], BF16)
    pts = p.sbuf("pts", [128, 16 * 128], BF16)
    rs = p.sbuf("rs", [128, 8], F32)
    mx = p.sbuf("mx", [128, 4], F32)
    ostage = p.sbuf("ostage", [128, 512], F32)

    pA = p.psum("pA", [128, 2048], F32)
    pB = p.psum("pB", [128, 1024], F32)
    pb0 = p.psum("pb0", [128, 1024], BF16)
    pb1 = p.psum("pb1", [128, 1024], BF16)

    if standalone:
        p.begin(consts=[1e-5])
    LD, GD, ST, PE, DV, AC = (p.sems[k] for k in ("LD", "GD", "ST", "PE", "DV", "AC"))

    def _(sy):
        sy.dma_start(out=xf.rearrange("p (c t) -> p c t", t=T),
                     in_=x1_in.rearrange("(c p) t -> p c t", p=128)).then_inc(LD, 16)
        sy.dma_start(out=id_f[:], in_=ident_in).then_inc(LD, 16)
        sy.dma_start(out=maskd[:], in_=maskd_in).then_inc(LD, 16)
    p.sync(_)
    p.inc("LD", 48)

    def _(g):
        g.dma_start(out=id_b[:], in_=ident_in).then_inc(GD, 16)
        g.dma_start(out=wslk.rearrange("p (c n) -> p c n", n=64),
                    in_=wat_in.rearrange("(c p) n -> p c n", p=128)[:, :, 1024:1088]).then_inc(GD, 16)
        g.dma_start(out=wslv.rearrange("p (c n) -> p c n", n=64),
                    in_=wat_in.rearrange("(c p) n -> p c n", p=128)[:, :, 1088:1152]).then_inc(GD, 16)
        g.dma_start(out=wpj.rearrange("p (c n) -> p c n", n=1024),
                    in_=wpj_in.rearrange("(c p) n -> p c n", p=128)).then_inc(GD, 16)
    p.gpsimd(_)
    p.inc("GD", 64)
    p.barrier()

    for tt in range(16):
        def _(t, tt=tt):
            if tt >= 1:
                t.wait_ge(DV, p.cnt["_lnev%d" % (tt - 1)])
            ins = None
            for c in range(8):
                ins = t.transpose(pB[:, (c % 4) * 128 + (c // 4) * 512:(c % 4) * 128 + (c // 4) * 512 + 128],
                                  xf[:, c * T + tt * 128:c * T + (tt + 1) * 128], id_f[:])
            p.tick(ins, "PE")
        p.tensor(_)
        pe_t = p.val("PE")

        def _(v, pe_t=pe_t, tt=tt):
            v.wait_ge(PE, pe_t)
            v.tensor_copy(xtm[:, 0:512], pB[:, 0:512])
            p.tick(v.tensor_copy(xtm[:, 512:1024], pB[:, 512:1024]), "DV")
        p.vector(_)
        dv_tm = p.val("DV")
        p.cnt["_lnev%d" % tt] = dv_tm

        def _(s, dv_tm=dv_tm):
            s.wait_ge(DV, dv_tm)
            p.tick(s.activation(xntm[:], xtm[:], AF.Square, accum_out=stats[:, 0:1]), "AC")
        p.scalar(_)
        ac1 = p.val("AC")

        def _(v, ac1=ac1):
            v.wait_ge(AC, ac1)
            p.fence(v, v.reduce_sum(stats[:, 1:2], xtm[:], axis=AX.X), "DV")
            p.fence(v, v.tensor_scalar_mul(stats[:, 1:2], stats[:, 1:2], 1.0 / D), "DV")
            p.fence(v, v.tensor_tensor(stats[:, 2:3], stats[:, 1:2], stats[:, 1:2], ALU.mult), "DV")
            p.tick(v.scalar_tensor_tensor(stats[:, 3:4], stats[:, 0:1], 1.0 / D, stats[:, 2:3],
                                          ALU.mult, ALU.subtract), "DV")
        p.vector(_)
        dv_st = p.val("DV")

        def _(s, dv_st=dv_st):
            s.wait_ge(DV, dv_st)
            p.fence(s, s.activation(stats[:, 4:5], stats[:, 3:4], AF.Ln, bias=p.C(1e-5)), "AC")
            p.tick(s.activation(stats[:, 5:6], stats[:, 4:5], AF.Exp, scale=-0.5), "AC")
        p.scalar(_)
        ac2 = p.val("AC")

        def _(v, ac2=ac2):
            v.wait_ge(AC, ac2)
            p.tick(v.tensor_scalar(xntm[:], xtm[:], stats[:, 1:2], stats[:, 5:6],
                                   ALU.subtract, ALU.mult), "DV")
        p.vector(_)
        dv_xn = p.val("DV")

        def _(t, tt=tt, dv_xn=dv_xn):
            t.wait_ge(DV, dv_xn)
            ins = None
            for c in range(8):
                ins = t.transpose(pB[:, (c % 4) * 128 + (c // 4) * 512:(c % 4) * 128 + (c // 4) * 512 + 128],
                                  xntm[:, c * 128:(c + 1) * 128], id_f[:])
            p.tick(ins, "PE")
        p.tensor(_)
        pe_t2 = p.val("PE")

        def _(v, pe_t2=pe_t2, tt=tt):
            v.wait_ge(PE, pe_t2)
            if tt >= 1:
                v.wait_ge(ST, p.cnt["_xnst%d" % (tt - 1)])
            ins = None
            for c in range(8):
                src = pB[:, (c % 4) * 128 + (c // 4) * 512:(c % 4) * 128 + (c // 4) * 512 + 128]
                v.tensor_copy(xn2f[:, c * 128:(c + 1) * 128], src)
                ins = v.tensor_copy(xn2[:, c * T + tt * 128:c * T + (tt + 1) * 128], src)
            p.tick(ins, "DV")
        p.vector(_)
        p.cnt["_lnev%d" % tt] = p.val("DV")

        def _(sy, tt=tt):
            sy.wait_ge(DV, p.cnt["_lnev%d" % tt])
            sy.dma_start(out=xn2_out.rearrange("(c p) t -> p c t", p=128)[:, :, tt * 128:(tt + 1) * 128],
                         in_=xn2f.rearrange("p (c q) -> p c q", q=128)).then_inc(ST, 16)
        p.sync(_)
        p.cnt["_xnst%d" % tt] = p.inc("ST", 16)

    p.barrier()

    for tc in range(4):
        def _(t, tc=tc):
            if tc >= 2:
                t.wait_ge(DV, p.cnt["_kev%d" % (tc - 2)])
            ins = None
            for c in range(8):
                ins = t.matmul(pB[0:64, (tc % 2) * 512:(tc % 2) * 512 + 512],
                               wslk[:, c * 64:(c + 1) * 64],
                               xn2[:, c * T + tc * 512:c * T + (tc + 1) * 512],
                               start=(c == 0), stop=(c == 7))
            p.tick(ins, "PE")
        p.tensor(_)
        pe_k = p.val("PE")

        def _(v, tc=tc, pe_k=pe_k):
            v.wait_ge(PE, pe_k)
            p.tick(v.tensor_copy(kfm[:, tc * 512:(tc + 1) * 512],
                                 pB[0:64, (tc % 2) * 512:(tc % 2) * 512 + 512]), "DV")
        p.vector(_)
        p.cnt["_kev%d" % tc] = p.val("DV")

    for tt in range(16):
        def _(t, tt=tt):
            if tt >= 2:
                t.wait_ge(DV, p.cnt["_vev%d" % (tt - 2)])
            ins = None
            for c in range(8):
                ins = t.matmul(pA[:, (tt % 2) * 512:(tt % 2) * 512 + 64],
                               xn2[:, c * T + tt * 128:c * T + (tt + 1) * 128],
                               wslv[:, c * 64:(c + 1) * 64],
                               start=(c == 0), stop=(c == 7))
            p.tick(ins, "PE")
        p.tensor(_)
        pe_v = p.val("PE")

        def _(v, tt=tt, pe_v=pe_v):
            v.wait_ge(PE, pe_v)
            p.tick(v.tensor_copy(vtm[:, tt * 64:(tt + 1) * 64],
                                 pA[:, (tt % 2) * 512:(tt % 2) * 512 + 64]), "DV")
        p.vector(_)
        p.cnt["_vev%d" % tt] = p.val("DV")

    p.barrier()

    for h in range(16):
        def _(g, h=h):
            if h >= 1:
                g.wait_ge(PE, p.cnt["_qslmm"])
            g.dma_start(out=wslq.rearrange("p (c n) -> p c n", n=64),
                        in_=wat_in.rearrange("(c p) n -> p c n", p=128)[:, :, h * 64:(h + 1) * 64]
                        ).then_inc(GD, 16)
        p.gpsimd(_)
        gd_q = p.inc("GD", 16)

        for tc in range(4):
            def _(t, h=h, tc=tc, gd_q=gd_q):
                t.wait_ge(GD, gd_q)
                if tc >= 2:
                    t.wait_ge(DV, p.cnt["_qev%d" % (tc - 2)])
                ins = None
                for c in range(8):
                    ins = t.matmul(pB[0:64, (tc % 2) * 512:(tc % 2) * 512 + 512],
                                   wslq[:, c * 64:(c + 1) * 64],
                                   xn2[:, c * T + tc * 512:c * T + (tc + 1) * 512],
                                   start=(c == 0), stop=(c == 7))
                p.tick(ins, "PE")
            p.tensor(_)
            pe_q = p.val("PE")
            if tc == 3:
                p.cnt["_qslmm"] = pe_q

            def _(v, tc=tc, pe_q=pe_q):
                v.wait_ge(PE, pe_q)
                p.tick(v.tensor_copy(qbf[:, tc * 512:(tc + 1) * 512],
                                     pB[0:64, (tc % 2) * 512:(tc % 2) * 512 + 512]), "DV")
            p.vector(_)
            p.cnt["_qev%d" % tc] = p.val("DV")

        dv_q = p.val("DV")
        for qi in range(16):
            nk = qi * 128 + 128
            nkc = (nk + 511) // 512

            def _(t, h=h, qi=qi, nk=nk, nkc=nkc, dv_q=dv_q):
                t.wait_ge(DV, dv_q)
                if qi >= 1:
                    t.wait_ge(AC, p.cnt["_pexp%d" % (qi - 1)])
                ins = None
                for kc in range(nkc):
                    w0 = kc * 512
                    w1 = min(nk, w0 + 512)
                    ins = t.matmul(pA[:, w0:w1], qbf[:, qi * 128:(qi + 1) * 128],
                                   kfm[:, w0:w1], start=True, stop=True)
                p.tick(ins, "PE")
            p.tensor(_)
            pe_s = p.val("PE")

            def _(v, qi=qi, nk=nk, pe_s=pe_s):
                v.wait_ge(PE, pe_s)
                if qi >= 1:
                    v.wait_ge(AC, p.cnt["_pexp%d" % (qi - 1)])
                p.fence(v, v.tensor_add(pB[:, 0:128], pA[:, nk - 128:nk], maskd[:]), "DV")
                if nk > 128:
                    p.fence(v, v.reduce_max(mx[:, 0:1], pA[:, 0:nk - 128], axis=AX.X), "DV")
                    p.fence(v, v.reduce_max(mx[:, 1:2], pB[:, 0:128], axis=AX.X), "DV")
                    p.fence(v, v.tensor_max(mx[:, 0:1], mx[:, 0:1], mx[:, 1:2]), "DV")
                else:
                    p.fence(v, v.reduce_max(mx[:, 0:1], pB[:, 0:128], axis=AX.X), "DV")
                p.tick(v.tensor_scalar_mul(mx[:, 1:2], mx[:, 0:1], -0.125), "DV")
            p.vector(_)
            dv_mx = p.val("DV")

            def _(s, qi=qi, nk=nk, dv_mx=dv_mx):
                s.wait_ge(DV, dv_mx)
                if nk > 128:
                    s.activation(psb[:, 0:nk - 128], pA[:, 0:nk - 128], AF.Exp, scale=0.125,
                                 bias=mx[:, 1:2], accum_out=rs[:, 0:1])
                ins = s.activation(psb[:, nk - 128:nk], pB[:, 0:128], AF.Exp, scale=0.125,
                                   bias=mx[:, 1:2], accum_out=rs[:, 1:2])
                p.tick(ins, "AC")
            p.scalar(_)
            p.cnt["_pexp%d" % qi] = p.val("AC")
            ac_e = p.val("AC")

            def _(v, qi=qi, nk=nk, ac_e=ac_e):
                v.wait_ge(AC, ac_e)
                if nk > 128:
                    p.fence(v, v.tensor_add(rs[:, 6:7], rs[:, 0:1], rs[:, 1:2]), "DV")
                else:
                    p.fence(v, v.tensor_copy(rs[:, 6:7], rs[:, 1:2]), "DV")
                p.fence(v, v.reciprocal(rs[:, 7:8], rs[:, 6:7]), "DV")
                p.tick(v.tensor_scalar_mul(psb[:, 0:nk], psb[:, 0:nk], rs[:, 7:8]), "DV")
            p.vector(_)
            dv_pn = p.val("DV")

            nkt = nk // 128
            for g0 in range(0, nkt, 8):
                g1 = min(nkt, g0 + 8)
                def _(t, g0=g0, g1=g1, dv_pn=dv_pn, qi=qi):
                    t.wait_ge(DV, dv_pn)
                    if g0 > 0:
                        t.wait_ge(DV, p.cnt["_ptev"])
                    elif qi >= 1:
                        t.wait_ge(DV, p.cnt.get("_ptev", 0))
                    ins = None
                    for kt in range(g0, g1):
                        ins = t.transpose(pb0[:, (kt - g0) * 128:(kt - g0 + 1) * 128],
                                          psb[:, kt * 128:(kt + 1) * 128], id_b[:])
                    p.tick(ins, "PE")
                p.tensor(_)
                pe_pt = p.val("PE")

                def _(v, g0=g0, g1=g1, pe_pt=pe_pt):
                    v.wait_ge(PE, pe_pt)
                    ins = None
                    for kt in range(g0, g1):
                        ins = v.tensor_copy(pts[:, kt * 128:(kt + 1) * 128],
                                            pb0[:, (kt - g0) * 128:(kt - g0 + 1) * 128])
                    p.tick(ins, "DV")
                p.vector(_)
                p.cnt["_ptev"] = p.val("DV")
            dv_pt = p.val("DV")

            def _(t, qi=qi, nkt=nkt, dv_pt=dv_pt):
                t.wait_ge(DV, dv_pt)
                if qi >= 1:
                    t.wait_ge(DV, p.cnt["_oev"])
                ins = None
                for kt in range(nkt):
                    ins = t.matmul(pB[0:64, 512:512 + 128], vtm[:, kt * 64:(kt + 1) * 64],
                                   pts[:, kt * 128:(kt + 1) * 128],
                                   start=(kt == 0), stop=(kt == nkt - 1))
                p.tick(ins, "PE")
            p.tensor(_)
            pe_o = p.val("PE")

            def _(v, h=h, qi=qi, pe_o=pe_o):
                v.wait_ge(PE, pe_o)
                c = h // 2
                r0 = (h % 2) * 64
                p.tick(v.tensor_copy(oat[r0:r0 + 64, c * T + qi * 128:c * T + (qi + 1) * 128],
                                     pB[0:64, 512:512 + 128]), "DV")
            p.vector(_)
            p.cnt["_oev"] = p.val("DV")
        p.barrier()

    for oc in range(8):
        for tc in range(4):
            bank = (oc * 4 + tc) % 2
            def _(t, oc=oc, tc=tc, bank=bank):
                key = "_prev%d" % (1 - bank)
                if (oc * 4 + tc) >= 2:
                    t.wait_ge(DV, p.cnt["_pradd%d" % bank])
                ins = None
                for c in range(8):
                    ins = t.matmul(pA[:, bank * 512:bank * 512 + 512],
                                   wpj[:, c * 1024 + oc * 128:c * 1024 + (oc + 1) * 128],
                                   oat[:, c * T + tc * 512:c * T + (tc + 1) * 512],
                                   start=(c == 0), stop=(c == 7))
                p.tick(ins, "PE")
            p.tensor(_)
            pe_p = p.val("PE")

            def _(v, oc=oc, tc=tc, bank=bank, pe_p=pe_p):
                v.wait_ge(PE, pe_p)
                dst = xf[:, oc * T + tc * 512:oc * T + (tc + 1) * 512]
                p.tick(v.tensor_add(dst, dst, pA[:, bank * 512:bank * 512 + 512]), "DV")
            p.vector(_)
            p.cnt["_pradd%d" % bank] = p.val("DV")

    p.barrier()
    def _(sy):
        sy.dma_start(out=x2p_out.rearrange("(c p) t -> p c t", p=128),
                     in_=xf.rearrange("p (c t) -> p c t", t=T)).then_inc(ST, 16)
        sy.wait_ge(ST, p.val("ST") + 16)
    p.sync(_)
    p.inc("ST", 16)

    if standalone:
        p.end()
        return p.nc
    return None


def build_E(p=None, io=None, dbg=False):
    standalone = p is None
    if standalone:
        p = Prog()
    else:
        p.phase("E_", io)
    if dbg:
        dbg_s = p.dram("dbg_s", [8, 128, 8 * 2049], F32, kind="ExternalOutput")
        dbg_t = p.dram("dbg_t", [4, 128, T], F32, kind="ExternalOutput")
        dbg_rot = p.dram("dbg_rot", [4, 32, T], F32, kind="ExternalOutput")
        dbg_icol = p.dram("dbg_icol", [32, 4], F32, kind="ExternalOutput")
    xn2_in = p.dram("xn2_bf", [D, T], F32, kind="ExternalInput")
    x2p_in = p.dram("x2p_fm", [D, T], F32, kind="ExternalInput")
    wqkv_in = p.dram("Wqkv", [6, 1024, 3072], F32, kind="ExternalInput")
    wo_in = p.dram("Wo", [6, 1024, 1024], F32, kind="ExternalInput")
    w1_in = p.dram("W1", [6, 1024, 5460], F32, kind="ExternalInput")
    w2_in = p.dram("W2", [6, 2730, 1024], F32, kind="ExternalInput")
    qs_in = p.dram("q_scale_r", [128, 6], F32, kind="ExternalInput")
    ks_in = p.dram("k_scale_r", [128, 6], F32, kind="ExternalInput")
    lng_in = p.dram("ln_g_r", [128, 48], F32, kind="ExternalInput")
    lnb_in = p.dram("ln_b_r", [128, 48], F32, kind="ExternalInput")
    b1_in = p.dram("b1_r", [128, 6 * 44], F32, kind="ExternalInput")
    b2_in = p.dram("b2_r", [128, 48], F32, kind="ExternalInput")
    ident_in = p.dram("ident", [128, 128], F32, kind="ExternalInput")
    mb_in = p.dram("mask_band", [128, 384], F32, kind="ExternalInput")
    mb0_in = p.dram("mask_band_t0", [128, 384], F32, kind="ExternalInput")
    mb1_in = p.dram("mask_band_t1", [128, 384], F32, kind="ExternalInput")
    iota_t_in = p.dram("iota_t", [1, T], F32, kind="ExternalInput")
    iota_i_in = p.dram("iota_i", [32, 1], F32, kind="ExternalInput")
    dup32_in = p.dram("dup32", [32, 128], F32, kind="ExternalInput")
    bdiag_in = p.dram("bdiag", [128, 2], F32, kind="ExternalInput")
    bdiagT_in = p.dram("bdiagT", [2, 128], F32, kind="ExternalInput")
    ones1_in = p.dram("ones1", [1, 128], F32, kind="ExternalInput")
    out_tm = p.dram("out_tm", [T, D], F32, kind="ExternalOutput")

    TG = T + 2049 - T
    SW = 2049
    s_bf = p.sbuf("s_bf", [128, 8 * SW], BF16)
    tbl_cs = p.sbuf("tbl_cs", [128, T], BF16)
    tbl_ss = p.sbuf("tbl_ss", [128, T], BF16)
    tbl_ci = p.sbuf("tbl_ci", [128, T], BF16)
    tbl_si = p.sbuf("tbl_si", [128, T], BF16)
    xn_i = p.sbuf("xn_i", [128, 8 * T], BF16)
    maskb = p.sbuf("maskb", [128, 3 * 384], F32)
    id_f = p.sbuf("id_f", [128, 128], F32)
    id_b = p.sbuf("id_b", [128, 128], BF16)
    qs_sb = p.sbuf("qs_sb", [128, 6], F32)
    ks_sb = p.sbuf("ks_sb", [128, 6], F32)
    lng_sb = p.sbuf("lng_sb", [128, 48], F32)
    lnb_sb = p.sbuf("lnb_sb", [128, 48], F32)
    b1_sb = p.sbuf("b1_sb", [128, 6 * 44], F32)
    b2_sb = p.sbuf("b2_sb", [128, 48], F32)
    bdiag = p.sbuf("bdiag_s", [128, 2], F32)
    onec_b = p.sbuf("onec_b", [128, 1], BF16)
    onec_f = p.sbuf("onec_f", [128, 1], F32)
    bdiagT = p.sbuf("bdiagT_s", [2, 128], F32)
    ones1 = p.sbuf("ones1_s", [1, 128], F32)
    dup32 = p.sbuf("dup32_s", [32, 128], BF16)

    AR = 24576 + 4096
    arena = p.sbuf("arena_e", [128, AR], BF16)
    oat = arena[:, 0:16384]
    qh = arena[:, 16384:18432]
    kh = arena[:, 18432:20736]
    vtm = arena[:, 20736:22784]
    h2 = arena[:, 0:22528]
    af32 = p.sbuf("af32", [128, 8192], F32)
    sq_sb = af32[:, 0:512]
    nb_sb = af32[:, 512:1024]
    qn_sb = af32[:, 1024:1536]
    rr_sb = af32[:, 1536:2048]
    t1_sb = af32[:, 2048:2560]
    gel = af32[:, 2560:3072]
    wsl = [p.sbuf(f"wsl{i}", [128, 1024], BF16) for i in range(3)]
    w2sl = p.sbuf("w2sl", [128, 22 * 128], BF16)
    psb = p.sbuf("psb_e", [128, 1536], BF16)
    pts = p.sbuf("pts_e", [128, 1536], BF16)
    rs = p.sbuf("rs_e", [128, 8], F32)
    mx_e = p.sbuf("mx_e", [128, 8], F32)
    rowAB = af32[0:2, 4096:4608]
    rowA = af32[0:1, 5632:6144]
    rowB = af32[0:1, 4608:5120]
    rowC = af32[0:1, 5120:5632]
    x2st = af32[:, 0:2048]
    fst = af32[:, 2048:4096]
    rot_a = af32[0:32, 0:2048]
    rot_b = af32[0:32, 2048:4096]
    rot_c = af32[0:32, 4096:6144]
    rot_d = af32[0:32, 6144:8192]
    icol = p.sbuf("icol", [32, 4], F32)
    rint = p.sbuf("rint", [32, 512], mybir.dt.int32)

    pA = p.psum("pA", [128, 2048], F32)
    pB = p.psum("pB", [128, 1024], F32)
    pb0 = p.psum("pb0", [128, 1024], BF16)
    pb1 = p.psum("pb1", [128, 1024], BF16)

    if standalone:
        p.begin(consts=[1e-5, 1.0])
    LD, GD, ST, PE, DV, AC = (p.sems[k] for k in ("LD", "GD", "ST", "PE", "DV", "AC"))

    HI2PI = 6.28125
    LO2PI = 0.0021853071795864769
    INV2PI = 1.0 / (2.0 * np.pi)

    def _(sy):
        for dst, src in [(id_f[:], ident_in), (maskb[:, 0:384], mb_in),
                         (maskb[:, 384:768], mb0_in), (maskb[:, 768:1152], mb1_in),
                         (qs_sb[:], qs_in), (ks_sb[:], ks_in), (lng_sb[:], lng_in),
                         (lnb_sb[:], lnb_in), (b1_sb[:], b1_in), (b2_sb[:], b2_in),
                         (bdiag[:], bdiag_in), (bdiagT[:], bdiagT_in),
                         (ones1[:], ones1_in),
                         (rot_a[:, 0:1], iota_i_in), (rot_b[0:1, :], iota_t_in)]:
            sy.dma_start(out=dst, in_=src).then_inc(LD, 16)
            p.inc("LD", 16)
    p.sync(_)

    def _(g):
        g.dma_start(out=id_b[:], in_=ident_in).then_inc(GD, 16)
        g.dma_start(out=dup32[:], in_=dup32_in).then_inc(GD, 16)
        g.dma_start(out=s_bf.rearrange("p (c t) -> p c t", t=SW)[:, :, 1:SW],
                    in_=xn2_in.rearrange("(c p) t -> p c t", p=128)).then_inc(GD, 16)
    p.gpsimd(_)
    p.inc("GD", 48)

    def _(v):
        ins = None
        for c in range(8):
            ins = v.memset(s_bf[:, c * SW:c * SW + 1], 0.0)
        p.tick(ins, "DV")
        p.tick(v.memset(kh[:, 0:256], 0.0), "DV")
        p.tick(v.memset(onec_b[:], 1.0), "DV")
        p.tick(v.memset(onec_f[:], 1.0), "DV")
    p.vector(_)
    p.barrier()

    def _(t):
        ins = None
        for tc in range(2):
            ins = t.matmul(pA[:, (tc % 2) * 512:(tc % 2) * 512 + 512][0:32, :],
                           ones1[0:1, 0:32], rot_b[0:1, tc * 512:(tc + 1) * 512],
                           start=True, stop=True)
        p.tick(ins, "PE")
    p.tensor(_)
    pe_t = p.val("PE")

    def _(v, pe_t=pe_t):
        v.wait_ge(PE, pe_t)
        v.tensor_copy(rot_c[:, 0:512], pA[0:32, 0:512])
        v.tensor_copy(rot_c[:, 512:1024], pA[0:32, 512:1024])
        p.tick(v.engine_nop(), "DV")
    p.vector(_)
    def _(t):
        t.wait_ge(DV, p.val("DV"))
        ins = None
        for tc in range(2, 4):
            ins = t.matmul(pA[:, (tc % 2) * 512:(tc % 2) * 512 + 512][0:32, :],
                           ones1[0:1, 0:32], rot_b[0:1, tc * 512:(tc + 1) * 512],
                           start=True, stop=True)
        p.tick(ins, "PE")
    p.tensor(_)
    pe_t2 = p.val("PE")

    def _(v, pe_t2=pe_t2):
        v.wait_ge(PE, pe_t2)
        v.tensor_copy(rot_c[:, 1024:1536], pA[0:32, 0:512])
        p.fence(v, v.tensor_copy(rot_c[:, 1536:2048], pA[0:32, 512:1024]), "DV")
        p.fence(v, v.tensor_scalar_mul(icol[:, 0:1], rot_a[:, 0:1], 1.0 / 32.0), "DV")
        p.fence(v, v.tensor_scalar_mul(icol[:, 3:4], icol[:, 0:1], -0.00013529385118360437), "DV")
        p.fence(v, v.scalar_tensor_tensor(icol[:, 0:1], icol[:, 0:1], -9.210205078125,
                                          icol[:, 3:4], ALU.mult, ALU.add), "DV")
        p.fence(v, v.tensor_scalar(icol[:, 1:2], rot_a[:, 0:1], 5.0, 64.0,
                                   ALU.mult, ALU.add), "DV")
        p.fence(v, v.tensor_scalar_mul(icol[:, 3:4], icol[:, 1:2], -1.9956912337970589e-10), "DV")
        p.fence(v, v.scalar_tensor_tensor(icol[:, 1:2], icol[:, 1:2], 0.0044642859138548374,
                                          icol[:, 3:4], ALU.mult, ALU.add), "DV")
    p.vector(_)
    dv_r1 = p.val("DV")

    def _(s, dv_r1=dv_r1):
        s.wait_ge(DV, dv_r1)
        p.fence(s, s.activation(icol[:, 0:1], icol[:, 0:1], AF.Exp), "AC")
        p.tick(s.activation(icol[:, 2:3], icol[:, 1:2], AF.Ln), "AC")
    p.scalar(_)
    ac_r1 = p.val("AC")

    def _(v, ac_r1=ac_r1):
        v.wait_ge(AC, ac_r1)
        p.fence(v, v.tensor_scalar_mul(icol[:, 3:4], icol[:, 2:3], -1.0), "DV")
        p.fence(v, v.tensor_scalar_mul(rot_a[:], rot_c[:], icol[:, 0:1]), "DV")
        p.fence(v, v.tensor_scalar_mul(rot_b[:], rot_a[:], INV2PI), "DV")
        for qq in range(4):
            p.fence(v, v.tensor_copy(rint[:], rot_b[:, qq * 512:(qq + 1) * 512]), "DV")
            p.fence(v, v.tensor_copy(rot_b[:, qq * 512:(qq + 1) * 512], rint[:]), "DV")
        p.fence(v, v.scalar_tensor_tensor(rot_a[:], rot_b[:], -HI2PI, rot_a[:], ALU.mult, ALU.add), "DV")
        p.fence(v, v.scalar_tensor_tensor(rot_a[:], rot_b[:], -LO2PI, rot_a[:], ALU.mult, ALU.add), "DV")
        p.fence(v, v.tensor_single_scalar(rot_d[:], rot_a[:], float(np.pi), ALU.is_gt), "DV")
        p.fence(v, v.scalar_tensor_tensor(rot_a[:], rot_d[:], -2.0 * float(np.pi), rot_a[:], ALU.mult, ALU.add), "DV")
        p.tick(v.tensor_scalar(rot_c[:], rot_c[:], 1.0 / 128.0, -8.0, ALU.mult, ALU.add), "DV")
    p.vector(_)
    dv_r2 = p.val("DV")

    def _(s, dv_r2=dv_r2):
        s.wait_ge(DV, dv_r2)
        p.fence(s, s.activation(rot_b[:], rot_a[:], AF.Sin), "AC")
        p.fence(s, s.activation(rot_d[:], rot_a[:], AF.Sin, scale=0.5), "AC")
        p.fence(s, s.activation(rot_d[:], rot_d[:], AF.Square), "AC")
        p.fence(s, s.activation(rot_a[:], rot_c[:], AF.Exp, scale=icol[:, 2:3]), "AC")
        p.tick(s.activation(rot_c[:], rot_c[:], AF.Exp, scale=icol[:, 3:4]), "AC")
    p.scalar(_)
    ac_r2 = p.val("AC")

    def _(v, ac_r2=ac_r2):
        v.wait_ge(AC, ac_r2)
        p.fence(v, v.tensor_scalar(rot_d[:], rot_d[:], -2.0, 1.0, ALU.mult, ALU.add), "DV")
        p.fence(v, v.tensor_mul(tbl_cs[0:32, :], rot_d[:], rot_a[:]), "DV")
        p.fence(v, v.tensor_mul(tbl_ss[0:32, :], rot_b[:], rot_a[:]), "DV")
        p.fence(v, v.tensor_mul(tbl_ci[0:32, :], rot_d[:], rot_c[:]), "DV")
        p.tick(v.tensor_mul(tbl_si[0:32, :], rot_b[:], rot_c[:]), "DV")
    p.vector(_)
    dv_r3 = p.val("DV")

    if dbg:
        dv_now = p.val("DV")
        def _(g, dv_now=dv_now):
            g.wait_ge(p.sems["DV"], dv_now)
            g.dma_start(out=dbg_rot[0], in_=rot_a[:]).then_inc(p.sems["GD"], 16)
            g.dma_start(out=dbg_rot[1], in_=rot_b[:]).then_inc(p.sems["GD"], 16)
            g.dma_start(out=dbg_rot[2], in_=rot_c[:]).then_inc(p.sems["GD"], 16)
            g.dma_start(out=dbg_rot[3], in_=rot_d[:]).then_inc(p.sems["GD"], 16)
            g.dma_start(out=dbg_icol, in_=icol[:]).then_inc(p.sems["GD"], 16)
        p.gpsimd(_)
        p.inc("GD", 80)
        p.barrier()

    for ti, tb in enumerate([tbl_cs, tbl_ss, tbl_ci, tbl_si]):
        for tc in range(4):
            def _(t, tb=tb, tc=tc, ti=ti, dv_r3=dv_r3):
                t.wait_ge(DV, dv_r3)
                if ti > 0 or tc >= 2:
                    t.wait_ge(DV, p.cnt.get("_rotev", 0))
                p.tick(t.matmul(pA[:, (tc % 2) * 512:(tc % 2) * 512 + 512],
                                dup32[:], tb[0:32, tc * 512:(tc + 1) * 512],
                                start=True, stop=True), "PE")
            p.tensor(_)
            pe_d = p.val("PE")

            def _(v, tb=tb, tc=tc, pe_d=pe_d):
                v.wait_ge(PE, pe_d)
                p.tick(v.tensor_copy(tb[:, tc * 512:(tc + 1) * 512],
                                     pA[:, (tc % 2) * 512:(tc % 2) * 512 + 512]), "DV")
            p.vector(_)
            p.cnt["_rotev"] = p.val("DV")
    p.barrier()
    if dbg:
        def _(g):
            for ti, tb in enumerate([tbl_cs, tbl_ss, tbl_ci, tbl_si]):
                g.dma_start(out=dbg_t[ti], in_=tb[:]).then_inc(p.sems["GD"], 16)
            g.dma_start(out=dbg_s[0], in_=s_bf[:]).then_inc(p.sems["GD"], 16)
        p.gpsimd(_)
        p.inc("GD", 80)
        p.barrier()

    def shifted(kc, a, b):
        off = 0 if kc >= 4 else 1
        return s_bf[:, kc * SW + off + a:kc * SW + off + b]

    for li in range(6):
        for hp in range(8):
            for wi, col0 in enumerate([hp * 128, 1024 + hp * 128, 2048 + hp * 128]):
                def _(g, wi=wi, col0=col0, li=li, hp=hp):
                    if li > 0 or hp > 0:
                        g.wait_ge(PE, p.cnt.get("_wslmm%d" % wi, 0))
                    g.dma_start(out=wsl[wi].rearrange("p (kc n) -> p kc n", n=128),
                                in_=wqkv_in[li].rearrange("(kc p) n -> p kc n", p=128)[:, :, col0:col0 + 128]
                                ).then_inc(GD, 16)
                p.gpsimd(_)
            gd_w = p.inc("GD", 48)

            for qk, (wi, dstbuf, tbl_c, tbl_s, sc_col) in enumerate(
                    [(0, qh, tbl_cs, tbl_ss, qs_sb[:, li:li + 1]),
                     (1, kh, tbl_ci, tbl_si, ks_sb[:, li:li + 1])]):
                for tc in range(4):
                    bank = (tc % 2)
                    def _(t, wi=wi, tc=tc, bank=bank, gd_w=gd_w, qk=qk):
                        t.wait_ge(GD, gd_w)
                        t.wait_ge(DV, p.cnt.get("_qnev%d" % bank, 0))
                        ins = None
                        for kc in range(8):
                            ins = t.matmul(pA[:, bank * 512:bank * 512 + 512],
                                           wsl[wi][:, kc * 128:(kc + 1) * 128],
                                           shifted(kc, tc * 512, (tc + 1) * 512),
                                           start=(kc == 0), stop=(kc == 7))
                        p.tick(ins, "PE")
                    p.tensor(_)
                    pe_q = p.val("PE")

                    def _(s, pe_q=pe_q, bank=bank):
                        s.wait_ge(PE, pe_q)
                        p.tick(s.activation(sq_sb[:], pA[:, bank * 512:bank * 512 + 512],
                                            AF.Square), "AC")
                    p.scalar(_)
                    ac_sq = p.val("AC")

                    def _(t, ac_sq=ac_sq, bank=bank):
                        t.wait_ge(AC, ac_sq)
                        t.wait_ge(DV, p.cnt.get("_nbev", 0))
                        p.tick(t.matmul(pB[0:2, 0:512], bdiag[:], sq_sb[:],
                                        start=True, stop=True), "PE")
                    p.tensor(_)
                    pe_ss = p.val("PE")

                    def _(v, pe_ss=pe_ss):
                        v.wait_ge(PE, pe_ss)
                        p.tick(v.tensor_scalar_max(rowAB[0:2, :], pB[0:2, 0:512], 1e-24), "DV")
                    p.vector(_)
                    dv_ss = p.val("DV")

                    def _(s, dv_ss=dv_ss):
                        s.wait_ge(DV, dv_ss)
                        p.fence(s, s.activation(rowAB[0:2, :], rowAB[0:2, :], AF.Ln), "AC")
                        p.tick(s.activation(rowAB[0:2, :], rowAB[0:2, :], AF.Exp, scale=-0.5), "AC")
                    p.scalar(_)
                    ac_rf = p.val("AC")

                    def _(t, ac_rf=ac_rf):
                        t.wait_ge(AC, ac_rf)
                        p.tick(t.matmul(pB[:, 512:1024], bdiagT[:], rowAB[0:2, :],
                                        start=True, stop=True), "PE")
                    p.tensor(_)
                    pe_nb = p.val("PE")

                    def _(v, pe_nb=pe_nb, bank=bank, sc_col=sc_col, tc=tc,
                          dstbuf=dstbuf, tbl_c=tbl_c, tbl_s=tbl_s, qk=qk):
                        v.wait_ge(PE, pe_nb)
                        p.fence(v, v.tensor_copy(nb_sb[:], pB[:, 512:1024]), "DV")
                        p.cnt["_nbev"] = p.val("DV")
                        p.fence(v, v.scalar_tensor_tensor(qn_sb[:], pA[:, bank * 512:bank * 512 + 512],
                                                          sc_col, nb_sb[:], ALU.mult, ALU.mult), "DV")
                        p.cnt["_qnev%d" % bank] = p.val("DV")
                        p.fence(v, v.tensor_mul(t1_sb[:], qn_sb[:], tbl_c[:, tc * 512:(tc + 1) * 512]), "DV")
                        ins = None
                        for qrt in range(4):
                            r0 = qrt * 32
                            src_r0 = (qrt + 1) * 32 if qrt % 2 == 0 else (qrt - 1) * 32
                            sgn = -1.0 if qrt % 2 == 0 else 1.0
                            ins = v.tensor_scalar_mul(rr_sb[r0:r0 + 32, :],
                                                      qn_sb[src_r0:src_r0 + 32, :], sgn)
                        p.fence(v, ins, "DV")
                        p.fence(v, v.tensor_mul(rr_sb[:], rr_sb[:],
                                                tbl_s[:, tc * 512:(tc + 1) * 512]), "DV")
                        off = 256 if (qk == 1) else 0
                        dst = dstbuf[:, off + tc * 512:off + (tc + 1) * 512]
                        p.tick(v.tensor_add(dst, t1_sb[:], rr_sb[:]), "DV")
                    p.vector(_)

            for tt in range(16):
                bank = tt % 2
                def _(t, tt=tt, bank=bank, gd_w=gd_w):
                    t.wait_ge(GD, gd_w)
                    if tt >= 2:
                        t.wait_ge(DV, p.cnt["_vev%d" % (tt - 2)])
                    ins = None
                    for kc in range(8):
                        ins = t.matmul(pA[:, 1024 + bank * 512:1024 + bank * 512 + 128],
                                       shifted(kc, tt * 128, (tt + 1) * 128),
                                       wsl[2][:, kc * 128:(kc + 1) * 128],
                                       start=(kc == 0), stop=(kc == 7))
                    p.tick(ins, "PE")
                p.tensor(_)
                pe_v = p.val("PE")

                def _(v, tt=tt, bank=bank, pe_v=pe_v):
                    v.wait_ge(PE, pe_v)
                    p.tick(v.tensor_copy(vtm[:, tt * 128:(tt + 1) * 128],
                                         pA[:, 1024 + bank * 512:1024 + bank * 512 + 128]), "DV")
                p.vector(_)
                p.cnt["_vev%d" % tt] = p.val("DV")
            p.cnt["_wslmm0"] = p.val("PE")
            p.cnt["_wslmm1"] = p.val("PE")
            p.cnt["_wslmm2"] = p.val("PE")
            dv_vq = p.val("DV")

            for hh in range(2):
                h = hp * 2 + hh
                r0 = hh * 64
                for qg in range(4):
                    def _(t, qg=qg, r0=r0, dv_vq=dv_vq):
                        t.wait_ge(DV, dv_vq)
                        t.wait_ge(AC, p.cnt.get("_eexp", 0))
                        ins = None
                        for j in range(4):
                            qi = qg * 4 + j
                            ins = t.matmul(pA[:, j * 512:j * 512 + 384],
                                           qh[r0:r0 + 64, qi * 128:(qi + 1) * 128],
                                           kh[r0:r0 + 64, qi * 128:qi * 128 + 384],
                                           start=True, stop=True)
                        p.tick(ins, "PE")
                    p.tensor(_)
                    pe_s = p.val("PE")

                    def _(v, qg=qg, pe_s=pe_s):
                        v.wait_ge(PE, pe_s)
                        v.wait_ge(AC, p.cnt.get("_eexp", 0))
                        ins = None
                        for j in range(4):
                            qi = qg * 4 + j
                            mo = 384 if qi == 0 else (768 if qi == 1 else 0)
                            ins = v.tensor_add(pA[:, j * 512:j * 512 + 384],
                                               pA[:, j * 512:j * 512 + 384],
                                               maskb[:, mo:mo + 384])
                        p.fence(v, ins, "DV")
                        ins = None
                        for j in range(4):
                            ins = v.reduce_max(mx_e[:, j:j + 1], pA[:, j * 512:j * 512 + 384],
                                               axis=AX.X)
                        p.fence(v, ins, "DV")
                        p.tick(v.tensor_scalar_mul(mx_e[:, 4:8], mx_e[:, 0:4], -8.0), "DV")
                    p.vector(_)
                    dv_mx = p.val("DV")

                    def _(sx, dv_mx=dv_mx):
                        sx.wait_ge(DV, dv_mx)
                        ins = None
                        for j in range(4):
                            ins = sx.activation(psb[:, j * 384:(j + 1) * 384],
                                                pA[:, j * 512:j * 512 + 384], AF.Exp,
                                                scale=8.0, bias=mx_e[:, 4 + j:5 + j],
                                                accum_out=rs[:, j:j + 1])
                        p.tick(ins, "AC")
                    p.scalar(_)
                    p.cnt["_eexp"] = p.val("AC")
                    ac_e = p.val("AC")

                    def _(v, ac_e=ac_e):
                        v.wait_ge(AC, ac_e)
                        p.fence(v, v.reciprocal(rs[:, 4:8], rs[:, 0:4]), "DV")
                        ins = None
                        for j in range(4):
                            ins = v.tensor_scalar_mul(psb[:, j * 384:(j + 1) * 384],
                                                      psb[:, j * 384:(j + 1) * 384],
                                                      rs[:, 4 + j:5 + j])
                        p.tick(ins, "DV")
                    p.vector(_)
                    dv_pn = p.val("DV")

                    def _(t, dv_pn=dv_pn):
                        t.wait_ge(DV, dv_pn)
                        t.wait_ge(DV, p.cnt.get("_ptev_e", 0))
                        ins = None
                        for j in range(4):
                            for kt in range(3):
                                idx = j * 3 + kt
                                dst = pb0[:, idx * 128:(idx + 1) * 128] if idx < 8 else                                     pb1[:, (idx - 8) * 128:(idx - 7) * 128]
                                ins = t.transpose(dst, psb[:, j * 384 + kt * 128:j * 384 + (kt + 1) * 128],
                                                  id_b[:])
                        p.tick(ins, "PE")
                    p.tensor(_)
                    pe_pt = p.val("PE")

                    def _(v, pe_pt=pe_pt):
                        v.wait_ge(PE, pe_pt)
                        v.tensor_copy(pts[:, 0:1024], pb0[:])
                        p.tick(v.tensor_copy(pts[:, 1024:1536], pb1[:, 0:512]), "DV")
                    p.vector(_)
                    p.cnt["_ptev_e"] = p.val("DV")
                    dv_pt = p.val("DV")

                    def _(t, qg=qg, hh=hh, dv_pt=dv_pt):
                        t.wait_ge(DV, dv_pt)
                        t.wait_ge(DV, p.cnt.get("_oev_e", 0))
                        ins = None
                        for j in range(4):
                            qi = qg * 4 + j
                            first = True
                            for kt in range(3):
                                vi = qi - 2 + kt
                                if vi < 0:
                                    continue
                                ins = t.matmul(pB[0:64, j * 128:(j + 1) * 128],
                                               vtm[:, vi * 128 + hh * 64:vi * 128 + hh * 64 + 64],
                                               pts[:, (j * 3 + kt) * 128:(j * 3 + kt + 1) * 128],
                                               start=first, stop=(kt == 2))
                                first = False
                        p.tick(ins, "PE")
                    p.tensor(_)
                    pe_o = p.val("PE")

                    def _(v, h=h, qg=qg, pe_o=pe_o):
                        v.wait_ge(PE, pe_o)
                        c = h // 2
                        rr = (h % 2) * 64
                        p.tick(v.tensor_copy(oat[rr:rr + 64, c * T + qg * 512:c * T + (qg + 1) * 512],
                                             pB[0:64, 0:512]), "DV")
                    p.vector(_)
                    p.cnt["_oev_e"] = p.val("DV")
            p.barrier()

        for oc in range(8):
            def _(g, oc=oc, li=li):
                if oc >= 1:
                    g.wait_ge(PE, p.cnt["_womm%d" % (oc - 1)])
                g.dma_start(out=wsl[0].rearrange("p (kc n) -> p kc n", n=128),
                            in_=wo_in[li].rearrange("(kc p) n -> p kc n", p=128)[:, :, oc * 128:(oc + 1) * 128]
                            ).then_inc(GD, 16)
            p.gpsimd(_)
            gd_wo = p.inc("GD", 16)

            for tc in range(4):
                bank = tc % 2
                def _(t, oc=oc, tc=tc, bank=bank, gd_wo=gd_wo):
                    t.wait_ge(GD, gd_wo)
                    t.wait_ge(DV, p.cnt.get("_woadd%d" % bank, 0))
                    ins = None
                    for kc in range(8):
                        ins = t.matmul(pA[:, bank * 512:bank * 512 + 512],
                                       wsl[0][:, kc * 128:(kc + 1) * 128],
                                       oat[:, kc * T + tc * 512:kc * T + (tc + 1) * 512],
                                       start=(kc == 0), stop=(kc == 7))
                    p.tick(ins, "PE")
                p.tensor(_)
                if tc == 3:
                    p.cnt["_womm%d" % oc] = p.val("PE")
                pe_wo = p.val("PE")

                def _(v, oc=oc, tc=tc, bank=bank, pe_wo=pe_wo):
                    v.wait_ge(PE, pe_wo)
                    dst = s_bf[:, oc * SW + 1 + tc * 512:oc * SW + 1 + (tc + 1) * 512]
                    p.tick(v.tensor_add(dst, dst, pA[:, bank * 512:bank * 512 + 512]), "DV")
                p.vector(_)
                p.cnt["_woadd%d" % bank] = p.val("DV")
        p.barrier()

        onesb_col = id_b
        for tc in range(4):
            def _(t, tc=tc, li=li):
                t.wait_ge(DV, p.cnt.get("_xnev%d" % ((tc - 1) % 4), 0) if tc >= 1 else 0)
                ins = None
                for kc in range(8):
                    ins = t.matmul(pB[0:1, 0:512], onec_b[:, 0:1],
                                   shifted(kc, tc * 512, (tc + 1) * 512),
                                   start=(kc == 0), stop=(kc == 7))
                p.tick(ins, "PE")
            p.tensor(_)
            pe_sm = p.val("PE")

            for kc in range(8):
                def _(s, kc=kc, tc=tc, pe_sm=pe_sm):
                    if kc == 0:
                        s.wait_ge(PE, pe_sm)
                    else:
                        s.wait_ge(PE, p.val("PE"))
                    p.tick(s.activation(sq_sb[:], shifted(kc, tc * 512, (tc + 1) * 512),
                                        AF.Square), "AC")
                p.scalar(_)
                ac_q = p.val("AC")

                def _(t, kc=kc, ac_q=ac_q):
                    t.wait_ge(AC, ac_q)
                    p.tick(t.matmul(pB[0:1, 512:1024], onec_f[:, 0:1], sq_sb[:],
                                    start=(kc == 0), stop=(kc == 7)), "PE")
                p.tensor(_)
            pe_sq = p.val("PE")

            def _(v, pe_sq=pe_sq):
                v.wait_ge(PE, pe_sq)
                p.fence(v, v.tensor_scalar_mul(rowA[0:1, :], pB[0:1, 0:512], 1.0 / D), "DV")
                p.fence(v, v.tensor_tensor(rowC[0:1, :], rowA[0:1, :], rowA[0:1, :], ALU.mult), "DV")
                p.tick(v.scalar_tensor_tensor(rowB[0:1, :], pB[0:1, 512:1024], 1.0 / D,
                                              rowC[0:1, :], ALU.mult, ALU.subtract), "DV")
            p.vector(_)
            dv_st = p.val("DV")

            def _(s, dv_st=dv_st):
                s.wait_ge(DV, dv_st)
                p.fence(s, s.activation(rowB[0:1, :], rowB[0:1, :], AF.Ln, bias=p.C(1e-5)[0:1, 0:1]), "AC")
                p.tick(s.activation(rowB[0:1, :], rowB[0:1, :], AF.Exp, scale=-0.5), "AC")
            p.scalar(_)
            ac_rs = p.val("AC")

            def _(v, ac_rs=ac_rs):
                v.wait_ge(AC, ac_rs)
                p.fence(v, v.scalar_tensor_tensor(rowC[0:1, :], rowA[0:1, :], -1.0,
                                                  rowB[0:1, :], ALU.mult, ALU.mult), "DV")
                p.tick(v.engine_nop(), "DV")
            p.vector(_)
            dv_nm = p.val("DV")

            def _(t, dv_nm=dv_nm):
                t.wait_ge(DV, dv_nm)
                t.matmul(pB[:, 0:512], ones1[0:1, :], rowB[0:1, :], start=True, stop=True)
                p.tick(t.matmul(pB[:, 512:1024], ones1[0:1, :], rowC[0:1, :],
                                start=True, stop=True), "PE")
            p.tensor(_)
            pe_bc = p.val("PE")

            for kc in range(8):
                def _(v, kc=kc, tc=tc, li=li, pe_bc=pe_bc):
                    v.wait_ge(PE, pe_bc)
                    p.fence(v, v.tensor_mul(t1_sb[:], shifted(kc, tc * 512, (tc + 1) * 512),
                                            pB[:, 0:512]), "DV")
                    p.fence(v, v.tensor_add(t1_sb[:], t1_sb[:], pB[:, 512:1024]), "DV")
                    p.fence(v, v.tensor_scalar(xn_i[:, kc * T + tc * 512:kc * T + (tc + 1) * 512],
                                               t1_sb[:], lng_sb[:, li * 8 + kc:li * 8 + kc + 1],
                                               lnb_sb[:, li * 8 + kc:li * 8 + kc + 1],
                                               ALU.mult, ALU.add), "DV")
                p.vector(_)
            p.cnt["_xnev%d" % tc] = p.val("DV")
        p.barrier()

        for th in range(2):
            for j in range(22):
                wj = 128 if j < 21 else 42
                def _(g, j=j, li=li, wj=wj):
                    g.wait_ge(PE, p.cnt.get("_w1mm", 0))
                    g.dma_start(out=wsl[0].rearrange("p (kc n) -> p kc n", n=128)[:, :, 0:wj],
                                in_=w1_in[li].rearrange("(kc p) n -> p kc n", p=128)[:, :, j * 128:j * 128 + wj]
                                ).then_inc(GD, 16)
                    g.dma_start(out=wsl[1].rearrange("p (kc n) -> p kc n", n=128)[:, :, 0:wj],
                                in_=w1_in[li].rearrange("(kc p) n -> p kc n", p=128)[:, :, 2730 + j * 128:2730 + j * 128 + wj]
                                ).then_inc(GD, 16)
                p.gpsimd(_)
                gd_w1 = p.inc("GD", 32)

                for tc in range(2):
                    ta = th * 1024 + tc * 512
                    def _(t, j=j, tc=tc, ta=ta, gd_w1=gd_w1, wj=wj):
                        t.wait_ge(GD, gd_w1)
                        t.wait_ge(DV, p.cnt.get("_h2ev", 0))
                        ins = None
                        for kc in range(8):
                            ins = t.matmul(pA[0:wj, 0:512], wsl[0][:, kc * 128:kc * 128 + wj],
                                           xn_i[:, kc * T + ta:kc * T + ta + 512],
                                           start=(kc == 0), stop=(kc == 7))
                        for kc in range(8):
                            ins = t.matmul(pA[0:wj, 512:1024], wsl[1][:, kc * 128:kc * 128 + wj],
                                           xn_i[:, kc * T + ta:kc * T + ta + 512],
                                           start=(kc == 0), stop=(kc == 7))
                        p.tick(ins, "PE")
                    p.tensor(_)
                    p.cnt["_w1mm"] = p.val("PE")
                    pe_w1 = p.val("PE")

                    def _(s, j=j, li=li, wj=wj, pe_w1=pe_w1):
                        s.wait_ge(PE, pe_w1)
                        p.tick(s.activation(gel[0:wj, :], pA[0:wj, 512:1024], AF.Gelu,
                                            bias=b1_sb[:, li * 44 + 22 + j:li * 44 + 22 + j + 1][0:wj, :]), "AC")
                    p.scalar(_)
                    ac_g = p.val("AC")

                    def _(v, j=j, li=li, tc=tc, wj=wj, ac_g=ac_g):
                        v.wait_ge(AC, ac_g)
                        p.tick(v.scalar_tensor_tensor(
                            h2[0:wj, j * 1024 + tc * 512:j * 1024 + tc * 512 + 512],
                            pA[0:wj, 0:512], b1_sb[:, li * 44 + j:li * 44 + j + 1][0:wj, :],
                            gel[0:wj, :], ALU.add, ALU.mult), "DV")
                    p.vector(_)
                    p.cnt["_h2ev"] = p.val("DV")
            dv_h2 = p.val("DV")

            for oc in range(8):
                def _(g, oc=oc, li=li):
                    g.wait_ge(PE, p.cnt.get("_w2mm", 0))
                    g.dma_start(out=w2sl.rearrange("p (kc n) -> p kc n", n=128)[:, 0:21, :],
                                in_=w2_in[li][0:2688].rearrange("(kc p) n -> p kc n", p=128)[:, :, oc * 128:(oc + 1) * 128]
                                ).then_inc(GD, 16)
                    g.dma_start(out=w2sl[0:42, 21 * 128:22 * 128],
                                in_=w2_in[li][2688:2730, oc * 128:(oc + 1) * 128]).then_inc(GD, 16)
                p.gpsimd(_)
                gd_w2 = p.inc("GD", 32)

                for tc in range(2):
                    bank = tc
                    ta = th * 1024 + tc * 512
                    def _(t, oc=oc, tc=tc, bank=bank, ta=ta, gd_w2=gd_w2, dv_h2=dv_h2):
                        t.wait_ge(GD, gd_w2)
                        t.wait_ge(DV, dv_h2)
                        t.wait_ge(DV, p.cnt.get("_w2add%d" % bank, 0))
                        ins = None
                        for j in range(22):
                            wj = 128 if j < 21 else 42
                            ins = t.matmul(pA[:, 1024 + bank * 512:1024 + bank * 512 + 512],
                                           w2sl[0:wj, j * 128:(j + 1) * 128],
                                           h2[0:wj, j * 1024 + tc * 512:j * 1024 + tc * 512 + 512],
                                           start=(j == 0), stop=(j == 21))
                        p.tick(ins, "PE")
                    p.tensor(_)
                    p.cnt["_w2mm"] = p.val("PE")
                    pe_w2 = p.val("PE")

                    def _(v, oc=oc, li=li, ta=ta, bank=bank, pe_w2=pe_w2):
                        v.wait_ge(PE, pe_w2)
                        dst = s_bf[:, oc * SW + 1 + ta:oc * SW + 1 + ta + 512]
                        p.tick(v.scalar_tensor_tensor(dst, pA[:, 1024 + bank * 512:1024 + bank * 512 + 512],
                                                      b2_sb[:, li * 8 + oc:li * 8 + oc + 1],
                                                      dst, ALU.add, ALU.add), "DV")
                    p.vector(_)
                    p.cnt["_w2add%d" % bank] = p.val("DV")
            p.barrier()
        if dbg:
            def _(g, li=li):
                g.dma_start(out=dbg_s[li + 1], in_=s_bf[:]).then_inc(p.sems["GD"], 16)
            p.gpsimd(_)
            p.inc("GD", 16)
            p.barrier()

    for c in range(8):
        def _(sy, c=c):
            if c >= 1:
                sy.wait_ge(PE, p.cnt.get("_eptr%d" % (c - 1), 0))
            sy.dma_start(out=x2st[:], in_=x2p_in[c * 128:(c + 1) * 128, :]).then_inc(LD, 16)
        p.sync(_)
        ld_x = p.inc("LD", 16)

        def _(v, c=c, ld_x=ld_x):
            v.wait_ge(LD, ld_x)
            if c >= 1:
                v.wait_ge(PE, p.cnt.get("_eptr%d" % (c - 1), 0))
            p.tick(v.tensor_add(fst[:], x2st[:], s_bf[:, c * SW + 1:c * SW + 1 + T]), "DV")
        p.vector(_)
        dv_f = p.val("DV")

        for tg in range(2):
            def _(t, c=c, tg=tg, dv_f=dv_f):
                t.wait_ge(DV, dv_f)
                if c > 0 or tg > 0:
                    t.wait_ge(DV, p.cnt.get("_epev", 0))
                ins = None
                for tt in range(8):
                    ti = tg * 8 + tt
                    ins = t.transpose(pA[:, tt * 128:(tt + 1) * 128] if tt < 4 else pA[:, 512 + (tt - 4) * 128:512 + (tt - 3) * 128],
                                      fst[:, ti * 128:(ti + 1) * 128], id_f[:])
                p.tick(ins, "PE")
            p.tensor(_)
            p.cnt["_eptr%d" % c] = p.val("PE")
            pe_ep = p.val("PE")

            def _(v, pe_ep=pe_ep):
                v.wait_ge(PE, pe_ep)
                v.wait_ge(ST, p.cnt.get("_epst", 0))
                v.tensor_copy(x2st[:, 0:512], pA[:, 0:512])
                p.tick(v.tensor_copy(x2st[:, 512:1024], pA[:, 512:1024]), "DV")
            p.vector(_)
            p.cnt["_epev"] = p.val("DV")
            dv_ep = p.val("DV")

            def _(sy, c=c, tg=tg, dv_ep=dv_ep):
                sy.wait_ge(DV, dv_ep)
                sy.dma_start(out=out_tm.rearrange("(g q) d -> q g d", q=128)[:, tg * 8:(tg + 1) * 8, c * 128:(c + 1) * 128] if False else
                             out_tm[tg * 1024:(tg + 1) * 1024, c * 128:(c + 1) * 128].rearrange("(w q) d -> q w d", q=128),
                             in_=x2st[:, 0:1024].rearrange("p (w d) -> p w d", d=128)).then_inc(ST, 16)
            p.sync(_)
            p.cnt["_epst"] = p.inc("ST", 16)

    def _(sy):
        sy.wait_ge(ST, p.val("ST"))
    p.sync(_)

    if standalone:
        p.end()
        return p.nc
    return None


def build_fused(dbg=False):
    p = Prog()
    I32 = mybir.dt.int32
    ikind = "ExternalOutput" if dbg else "Internal"

    ext = {}
    for name, shape in [
        ("x", [T, D]), ("ident", [128, 128]), ("sel", [32, 32 * 128]),
        ("in_proj_w", [4096, 1024]), ("conv_w", [2048, 1, 4]), ("conv_b", [128, 16]),
        ("x_proj_w", [96, 2048]), ("dt_proj_w", [2048, 64]), ("dt_proj_b", [128, 16]),
        ("A_log", [2048, 16]), ("D_par", [128, 16]), ("out_proj_w", [1024, 2048]),
        ("w_gate", [2816, 1024]), ("w_up", [2816, 1024]), ("w_down", [1024, 2816]),
        ("E64", [64, 1024]), ("E32", [32, 512]), ("E44", [44, 704]),
        ("w_attn", [1024, 1152]), ("w_proj", [1024, 1024]), ("mask_diag", [128, 128]),
        ("Wqkv", [6, 1024, 3072]), ("Wo", [6, 1024, 1024]),
        ("W1", [6, 1024, 5460]), ("W2", [6, 2730, 1024]),
        ("q_scale_r", [128, 6]), ("k_scale_r", [128, 6]),
        ("ln_g_r", [128, 48]), ("ln_b_r", [128, 48]), ("b1_r", [128, 6 * 44]),
        ("b2_r", [128, 48]), ("mask_band", [128, 384]), ("mask_band_t0", [128, 384]),
        ("mask_band_t1", [128, 384]), ("iota_t", [1, T]), ("iota_i", [32, 1]),
        ("dup32", [32, 128]), ("bdiag", [128, 2]), ("bdiagT", [2, 128]),
        ("ones1", [1, 128]),
    ]:
        ext[name] = p.dram(name, shape, F32, kind="ExternalInput")
    out_tm = p.dram("out_tm", [T, D], F32, kind="ExternalOutput")

    xfm_d = p.dram("xfm_d", [D, T], F32, kind=ikind)
    xnbf_d = p.dram("xnbf_d", [D, T], F32, kind=ikind)
    xntm_d = p.dram("xntm_d", [T, D], F32, kind=ikind)
    mam_d = p.dram("mam_d", [D, T], F32, kind=ikind)
    x1_d = p.dram("x1_d", [D, T], F32, kind=ikind)
    xn2_d = p.dram("xn2_d", [D, T], F32, kind=ikind)
    x2p_d = p.dram("x2p_d", [D, T], F32, kind=ikind)

    pA = p.psum("pA", [128, 2048], F32)
    pB = p.psum("pB", [128, 1024], F32)
    pb0 = p.psum("pb0", [128, 1024], BF16)
    pb1 = p.psum("pb1", [128, 1024], BF16)
    ps = {"pA": pA, "pB": pB, "pb0": pb0, "pb1": pb1}

    G1f = p.sbuf("G1", [128, 16388], F32)
    G1b = G1f[:].bitcast(BF16)
    G2b = p.sbuf("G2", [128, 16384], BF16)
    G2f = G2b[:].bitcast(F32)
    G2i = G2b[:].bitcast(I32)
    G3b = p.sbuf("G3", [128, 30720], BF16)
    G4f = p.sbuf("G4", [128, 8448], F32)
    G4b = G4f[:].bitcast(BF16)
    G5b = p.sbuf("G5", [128, 8192], BF16)
    G5f = G5b[:].bitcast(F32)
    id_f = p.sbuf("gid_f", [128, 128], F32)
    id_b = p.sbuf("gid_b", [128, 128], BF16)

    p.begin(consts=[1e-5, 1.0])

    def _(sy):
        sy.dma_start(out=id_f[:], in_=ext["ident"]).then_inc(p.sems["LD"], 16)
    p.sync(_)
    p.inc("LD", 16)

    def _(g):
        g.dma_start(out=id_b[:], in_=ext["ident"]).then_inc(p.sems["GD"], 16)
    p.gpsimd(_)
    p.inc("GD", 16)

    ioA = {"pA": pA, "idf": id_f,
           "x": ext["x"], "ident": ext["ident"],
           "x_fm": xfm_d, "xn_bf": xnbf_d, "xn_tm": xntm_d}
    for i in range(3):
        ioA[f"stg{i}"] = G1f[:, i * 1024:(i + 1) * 1024]
        ioA[f"xntm{i}"] = G1f[:, 3072 + i * 1024:3072 + (i + 1) * 1024]
        ioA[f"evac{i}"] = G1f[:, 6144 + i * 1024:6144 + (i + 1) * 1024]
        ioA[f"evb{i}"] = G1f[:, 9216 + i * 1024:9216 + (i + 1) * 1024]
        ioA[f"stats{i}"] = G1f[:, 12288 + i * 8:12288 + (i + 1) * 8]

    build_A(p=p, io=ioA)

    ioB = dict(ps)
    ioB.update({"id_b": id_b, "xn_bf": xnbf_d, "mam_out": mam_d,
                "in_proj_w": ext["in_proj_w"], "conv_w": ext["conv_w"],
                "conv_b": ext["conv_b"], "x_proj_w": ext["x_proj_w"],
                "dt_proj_w": ext["dt_proj_w"], "dt_proj_b": ext["dt_proj_b"],
                "A_log": ext["A_log"], "D_par": ext["D_par"],
                "out_proj_w": ext["out_proj_w"], "ident": ext["ident"],
                "sel": ext["sel"],
                "xnb": G2b[:, 0:16384],
                "opT": G1b[:, 0:16384],
                "z_bf": G1b[:, 16384:24576],
                "xa_bf": G1b[:, 24576:32768],
                "arena": G3b[:, 0:24576],
                "selb": G3b[0:32, 24576:28672],
                "ipT0": G3b[:, 28672:29696], "ipT1": G3b[:, 29696:30720],
                "dl_bf": G5b[0:64, 0:512], "bc_bf": G5b[0:32, 512:1024],
                "tails": G5b[:, 1024:1072], "u_dc": G5b[:, 1072:1584],
                "dtT": G5b[0:64, 1584:3632], "xpjT": G5b[:, 3632:5168],
                "v0": G4f[:, 0:512], "sp_ab": G4f[:, 512:1024],
                "sp_l": G4f[:, 1024:1536], "delta_dc": G4f[:, 1536:2048],
                "xc0": G4f[:, 2048:2560], "xc1": G4f[:, 2560:3072],
                "ytmp": G4f[:, 3072:3584], "yacc": G4f[:, 3584:4096],
                "ostage0": G4f[:, 4096:4608],
                "dA0": G4f[:, 4608:5120], "dA1": G4f[:, 5120:5632],
                "dBu0": G4f[:, 5632:6144], "dBu1": G4f[:, 6144:6656],
                "hh": G4f[:, 6656:7168],
                "convw": G4f[:, 7168:7232], "convb": G4f[:, 7232:7248],
                "dtbc": G4f[:, 7248:7264], "dcol": G4f[:, 7264:7280],
                "a_sb": G4f[:, 7280:7536], "hstate": G4f[:, 7536:7792]})
    build_B(p=p, io=ioB)

    ioC = dict(ps)
    ioC.update({"id_f": id_f, "id_b": id_b,
                "xn_tm": xntm_d, "x_fm": xfm_d, "mam_out": mam_d, "x1_fm": x1_d,
                "w_gate": ext["w_gate"], "w_up": ext["w_up"], "w_down": ext["w_down"],
                "ident": ext["ident"], "E64": ext["E64"], "E32": ext["E32"],
                "E44": ext["E44"],
                "xf": G1f[:, 0:16384],
                "e64": G5f[0:64, 0:1024], "e32": G5f[0:32, 1024:1536],
                "e44": G5f[0:44, 1536:2240],
                "q_bm": G4f[:, 0:128], "q_rc": G4f[:, 128:256],
                "q_bmT": G4f[:, 256:384], "q_rcT": G4f[:, 384:512],
                "q_ax": G4f[:, 512:1536], "q_accA": G4f[:, 1536:2560],
                "q_accB": G4f[:, 2560:3584], "q_sg": G4f[:, 3584:4608],
                "wstage": G4f[:, 4608:7424], "xnst": G4f[:, 4608:5632],
                "sgate": G4f[:, 7424:7936], "h2f": G4f[:, 7936:8448],
                "wq_nat": G2b[:, 0:2816], "xqt": G2b[:, 2816:3840],
                "hqt": G2b[:, 3840:4352], "wqg": G2b[:, 4352:8448],
                "wqu": G2b[:, 8448:12544], "wdTs": G2b[:, 12544:15360],
                "xq_fm": G3b[:, 0:8192], "hq_fm": G3b[:, 8192:30720]})
    build_C(p=p, io=ioC)

    ioD = dict(ps)
    ioD.update({"id_f": id_f, "id_b": id_b,
                "x1_fm": x1_d, "xn2_bf": xn2_d, "x2p_fm": x2p_d,
                "w_attn": ext["w_attn"], "w_proj": ext["w_proj"],
                "ident": ext["ident"], "mask_diag": ext["mask_diag"],
                "xf": G1f[:, 0:16384],
                "xn2": G2b[:, 0:16384],
                "oat": G3b[:, 0:16384], "psb": G3b[:, 16384:18432],
                "pts": G3b[:, 18432:20480], "kfm": G3b[0:64, 20480:22528],
                "vtm": G3b[:, 22528:23552], "qbf": G3b[0:64, 23552:25600],
                "wslq": G3b[:, 25600:26112], "wslk": G3b[:, 26112:26624],
                "wslv": G3b[:, 26624:27136],
                "xtm": G4f[:, 0:1024], "xntm": G4f[:, 1024:2048],
                "xn2f": G4f[:, 2048:3072], "stats": G4f[:, 3072:3080],
                "rs": G4f[:, 3080:3088], "mx": G4f[:, 3088:3092],
                "ostage": G4f[:, 3092:3604], "maskd": G4f[:, 3604:3732],
                "wpj": G4b[:, 7464:15656]})
    build_D(p=p, io=ioD)

    ioE = dict(ps)
    ioE.update({"id_f": id_f, "id_b": id_b,
                "xn2_bf": xn2_d, "x2p_fm": x2p_d, "out_tm": out_tm,
                "Wqkv": ext["Wqkv"], "Wo": ext["Wo"], "W1": ext["W1"], "W2": ext["W2"],
                "q_scale_r": ext["q_scale_r"], "k_scale_r": ext["k_scale_r"],
                "ln_g_r": ext["ln_g_r"], "ln_b_r": ext["ln_b_r"],
                "b1_r": ext["b1_r"], "b2_r": ext["b2_r"], "ident": ext["ident"],
                "mask_band": ext["mask_band"], "mask_band_t0": ext["mask_band_t0"],
                "mask_band_t1": ext["mask_band_t1"], "iota_t": ext["iota_t"],
                "iota_i": ext["iota_i"], "dup32": ext["dup32"],
                "bdiag": ext["bdiag"], "bdiagT": ext["bdiagT"], "ones1": ext["ones1"],
                "s_bf": G1b[:, 0:16392],
                "xn_i": G1b[:, 16392:32776],
                "tbl_cs": G5b[:, 0:2048], "tbl_ss": G5b[:, 2048:4096],
                "tbl_ci": G5b[:, 4096:6144], "tbl_si": G5b[:, 6144:8192],
                "af32": G4f[:, 0:8192], "maskb": G2f[:, 4400:5552],
                "arena_e": G3b[:, 0:28672],
                "psb_e": G3b[:, 28672:30208], "pts_e": G3b[:, 22784:24320],
                "w2sl": G2b[:, 0:2816], "wsl0": G2b[:, 2816:3840],
                "wsl1": G2b[:, 3840:4864], "wsl2": G2b[:, 4864:5888],
                "b1_sb": G2f[:, 2944:3208],
                "qs_sb": G2f[:, 3208:3214], "ks_sb": G2f[:, 3214:3220],
                "lng_sb": G2f[:, 3220:3268], "lnb_sb": G2f[:, 3268:3316],
                "b2_sb": G2f[:, 3316:3364], "rs_e": G2f[:, 3364:3372],
                "mx_e": G2f[:, 3372:3380], "bdiag_s": G2f[:, 3380:3382],
                "onec_f": G2f[:, 3382:3383],
                "bdiagT_s": G2f[0:2, 3383:3511], "ones1_s": G2f[0:1, 3511:3639],
                "dup32_s": G2b[0:32, 8576:8704], "icol": G2f[0:32, 3763:3767],
                "rint": G2i[0:32, 3767:4279],
                "onec_b": G2b[:, 8704:8705]})
    build_E(p=p, io=ioE, dbg=dbg)

    p.end()
    return p.nc


def prep_shared(inputs, c):
    m = dict(prep_E(inputs, c))
    m.update({
        "sel": c["sel"],
        "in_proj_w": inputs["L0_attn_in_proj_w"],
        "conv_w": inputs["L0_attn_conv_w"],
        "conv_b": np.ascontiguousarray(np.asarray(inputs["L0_attn_conv_b"]).reshape(16, 128).T),
        "x_proj_w": inputs["L0_attn_x_proj_w"],
        "dt_proj_w": inputs["L0_attn_dt_proj_w"],
        "dt_proj_b": np.ascontiguousarray(np.asarray(inputs["L0_attn_dt_proj_b"]).reshape(16, 128).T),
        "A_log": inputs["L0_attn_A_log"],
        "D_par": np.ascontiguousarray(np.asarray(inputs["L0_attn_D"]).reshape(16, 128).T),
        "out_proj_w": inputs["L0_attn_out_proj_w"],
        "w_gate": inputs["L0_ffn_w_gate"], "w_up": inputs["L0_ffn_w_up"],
        "w_down": inputs["L0_ffn_w_down"],
        "E64": c["E64"], "E32": c["E32"], "E44": c["E44"],
        "w_attn": inputs["L1_attn_w_attn"], "w_proj": inputs["L1_attn_w_proj"],
        "mask_diag": c["mask_diag"],
    })
    return m


def kernel(**inputs):
    NCORES = 8
    c = host_constants()
    x_full = np.ascontiguousarray(np.asarray(inputs["L0_attn_x"]))
    cores = list(range(NCORES))
    shared = prep_shared(inputs, c)
    nc = build_fused()
    maps = []
    for i in range(NCORES):
        m = dict(shared)
        m["x"] = np.ascontiguousarray(x_full[i])
        maps.append(m)
    res = run_bass_kernel_spmd(nc, maps, core_ids=cores).results
    out = np.stack([res[i]["out_tm"] for i in range(NCORES)], axis=0)
    return out.astype(np.float32)


def kernel_multilaunch(**inputs):
    NCORES = 8
    c = host_constants()
    x_full = np.ascontiguousarray(np.asarray(inputs["L0_attn_x"]))
    cores = list(range(NCORES))

    ncA = build_A()
    mapsA = [{"x": np.ascontiguousarray(x_full[i]), "ident": c["ident"]} for i in range(NCORES)]
    resA = run_bass_kernel_spmd(ncA, mapsA, core_ids=cores).results

    ncB = build_B()
    mapsB = []
    for i in range(NCORES):
        mapsB.append({
            "xn_bf": resA[i]["xn_bf"],
            "in_proj_w": inputs["L0_attn_in_proj_w"],
            "conv_w": inputs["L0_attn_conv_w"],
            "conv_b": np.ascontiguousarray(np.asarray(inputs["L0_attn_conv_b"]).reshape(16, 128).T),
            "x_proj_w": inputs["L0_attn_x_proj_w"],
            "dt_proj_w": inputs["L0_attn_dt_proj_w"],
            "dt_proj_b": np.ascontiguousarray(np.asarray(inputs["L0_attn_dt_proj_b"]).reshape(16, 128).T),
            "A_log": inputs["L0_attn_A_log"],
            "D_par": np.ascontiguousarray(np.asarray(inputs["L0_attn_D"]).reshape(16, 128).T),
            "out_proj_w": inputs["L0_attn_out_proj_w"],
            "ident": c["ident"], "sel": c["sel"],
        })
    resB = run_bass_kernel_spmd(ncB, mapsB, core_ids=cores).results

    ncC = build_C()
    mapsC = []
    for i in range(NCORES):
        mapsC.append({
            "xn_tm": resA[i]["xn_tm"],
            "x_fm": resA[i]["x_fm"],
            "mam_out": resB[i]["mam_out"],
            "w_gate": inputs["L0_ffn_w_gate"], "w_up": inputs["L0_ffn_w_up"],
            "w_down": inputs["L0_ffn_w_down"],
            "ident": c["ident"], "E64": c["E64"], "E32": c["E32"], "E44": c["E44"],
        })
    resC = run_bass_kernel_spmd(ncC, mapsC, core_ids=cores).results

    ncD = build_D()
    mapsD = []
    for i in range(NCORES):
        mapsD.append({
            "x1_fm": resC[i]["x1_fm"],
            "w_attn": inputs["L1_attn_w_attn"], "w_proj": inputs["L1_attn_w_proj"],
            "ident": c["ident"], "mask_diag": c["mask_diag"],
        })
    resD = run_bass_kernel_spmd(ncD, mapsD, core_ids=cores).results

    ncE = build_E()
    base = prep_E(inputs, c)
    mapsE = []
    for i in range(NCORES):
        m = dict(base)
        m["xn2_bf"] = resD[i]["xn2_bf"]
        m["x2p_fm"] = resD[i]["x2p_fm"]
        mapsE.append(m)
    resE = run_bass_kernel_spmd(ncE, mapsE, core_ids=cores).results

    out = np.stack([resE[i]["out_tm"] for i in range(NCORES)], axis=0)
    return out.astype(np.float32)
